# version 07
# baseline (speedup 1.0000x reference)
import numpy as np
import ml_dtypes
import concourse.bass as bass
import concourse.mybir as mybir
from concourse.bass_utils import run_bass_kernel_spmd

F32 = mybir.dt.float32
BF16 = mybir.dt.bfloat16
AF = mybir.ActivationFunctionType
ALU = mybir.AluOpType
AX = mybir.AxisListType

S = 4096
H = 1024
NH, HD = 16, 64
NE = 32
TG = 512
NTG = S // TG
NFC = H // 128
HALF = S // 2
NEPS = 1e-5
SIGC = float(1.0 / (1.0 + np.exp(-1.702 * 7.0)))

ENG = ("pe", "ve", "ae", "sy", "gp")
_NE_RUN = [NE]
_FLAGS = {"pool": True, "cbtw2": True}


class Prog:
    def __init__(self, nc):
        self.nc = nc
        self.st = {e: [] for e in ENG}
        self.sems = {}
        self.cnt = {}

    def sem(self, name):
        if name not in self.sems:
            self.sems[name] = self.nc.alloc_semaphore(name)
            self.cnt[name] = 0
        return self.sems[name]

    def em(self, eng, f):
        self.st[eng].append(f)

    def wait(self, eng, ticket):
        if ticket is None:
            return
        name, val = ticket
        if val <= 0:
            return
        s = self.sem(name)
        self.em(eng, lambda e, s=s, v=val: e.wait_ge(s, v))

    def waits(self, eng, tickets):
        for t in tickets:
            self.wait(eng, t)

    def dma(self, q, out_ap, in_ap, semname):
        s = self.sem(semname)
        self.cnt[semname] += 16
        v = self.cnt[semname]
        self.em(q, lambda e, o=out_ap, i=in_ap, s=s: e.dma_start(out=o, in_=i).then_inc(s, 16))
        return (semname, v)

    def op(self, eng, fn, inc=False):
        if not inc:
            self.em(eng, lambda e, f=fn: f(e))
            return None
        s = self.sem(eng)
        self.cnt[eng] += 1
        v = self.cnt[eng]
        self.em(eng, lambda e, f=fn, s=s: f(e).then_inc(s, 1))
        return (eng, v)

    def now(self, *sems):
        return [(s, self.cnt.get(s, 0)) for s in (sems or self.cnt.keys())]


class Buf:
    def __init__(self, n):
        self.n = n
        self.free = [[] for _ in range(n)]

    def acquire(self, p, eng, i):
        for t in self.free[i % self.n]:
            p.wait(eng, t)
        self.free[i % self.n] = []

    def release(self, i, *tickets):
        self.free[i % self.n] += [t for t in tickets if t is not None]


def _final(p):
    for name, v in list(p.cnt.items()):
        if name in ENG:
            continue
        p.wait("sy", (name, v))


def build_program(debug_stage=0):
    nc = bass.Bass("TRN2", target_bir_lowering=False)
    p = Prog(nc)

    api = lambda name, shape, dt: nc.dram_tensor(name, shape, dt, kind="ExternalInput").ap()
    xT = api("xT", [H, S], F32)
    wqT = api("wqT", [H, H], F32)
    wkT = api("wkT", [H, H], F32)
    wvT = api("wvT", [H, H], F32)
    woT = api("woT", [H, H], F32)
    gwT = api("gwT", [H, NE], F32)
    gbb_h = api("gbb", [128, NE], F32)
    w1n = api("w1n", [NE, 8, 128, 2048], F32)
    b1pp_h = api("b1pp", [128, NE * 16], F32)
    w2n = api("w2n", [NE, 8, 128, 1024], F32)
    b2_h = api("b2", [NE, H], F32)
    nsc_h = api("nsc", [128, NFC], F32)
    cos_h = api("cosb", [128, S], F32)
    sin_h = api("sinb", [128, S], F32)
    mask_h = api("maskb", [128, 4 * TG], F32)
    rT_h = api("rT", [128, 128], F32)
    ident_h = api("ident", [128, 128], F32)
    onescol_h = api("onescol", [128, 3], F32)
    onescolb_h = api("onescolb", [128, 3], F32)
    onesrow_h = api("onesrow", [1, 128], F32)
    onesrowb_h = api("onesrowb", [1, 128], F32)
    cv_h = api("cv", [128, 4], F32)

    outT = nc.dram_tensor("outT", [H, S], F32, kind="ExternalOutput").ap()

    def scratch(name, shape, dt, stage):
        kind = "ExternalOutput" if (0 < debug_stage == stage) else "Internal"
        return nc.dram_tensor(name, shape, dt, kind=kind).ap()
    q_s = scratch("q_s", [NH // 2, 128, S], BF16, 1)
    k_s = scratch("k_s", [NH // 2, 128, S], BF16, 1)
    v_s = scratch("v_s", [S, H], BF16, 1)
    o_s = scratch("o_s", [NH // 2, 128, S], BF16, 2)
    x2_s = scratch("x2_s", [H, S], F32, 3)

    A1 = nc.alloc_sbuf_tensor("A1", [128, 16384], F32)
    A2 = nc.alloc_sbuf_tensor("A2", [128, 12320], F32)
    A3 = nc.alloc_sbuf_tensor("A3", [128, 8192], F32)
    A4 = nc.alloc_sbuf_tensor("A4", [128, 8192], F32)
    A1b, A2b, A3b, A4b = (a.bitcast(BF16) for a in (A1, A2, A3, A4))
    masks = A3[:, 3072:5120]
    nstage = A3[0:1, 5120:7168]
    cvt = nc.alloc_sbuf_tensor("cvt_sb", [128, 4], F32)
    nsc = nc.alloc_sbuf_tensor("nsc_sb", [128, NFC], F32)
    gbb = nc.alloc_sbuf_tensor("gbb_sb", [128, NE], F32)
    b1pp = nc.alloc_sbuf_tensor("b1pp_sb", [128, NE * 16], F32)
    b1sig = nc.alloc_sbuf_tensor("b1sig_sb", [128, NE * 8], F32)
    b1lin1 = nc.alloc_sbuf_tensor("b1lin1_sb", [128, NE * 8], F32)
    b2bf = nc.alloc_sbuf_tensor("b2bf_sb", [NE, H], BF16)
    gw_sb = nc.alloc_sbuf_tensor("gw_sb", [128, NFC * NE], BF16)
    rT = nc.alloc_sbuf_tensor("rT_sb", [128, 128], BF16)
    ident = nc.alloc_sbuf_tensor("ident_sb", [128, 128], F32)
    onescol = nc.alloc_sbuf_tensor("onescol_sb", [128, 3], F32)
    onescolb = nc.alloc_sbuf_tensor("onescolb_sb", [128, 3], BF16)
    onesrow = nc.alloc_sbuf_tensor("onesrow_sb", [1, 128], F32)
    onesrowb = nc.alloc_sbuf_tensor("onesrowb_sb", [1, 128], BF16)
    rows2 = nc.alloc_sbuf_tensor("rows2", [1, 3072], F32)
    nb_sb = nc.alloc_sbuf_tensor("nb_sb", [128, 16], F32)
    cfm = nc.alloc_sbuf_tensor("cfm", [NE, HALF], BF16)
    crow = nc.alloc_sbuf_tensor("crow", [1, HALF], BF16)
    g_sb = nc.alloc_sbuf_tensor("g_sb", [128, NE], F32)
    m8 = nc.alloc_sbuf_tensor("m8", [128, 8], F32)
    top8 = nc.alloc_sbuf_tensor("top8", [128, 8], F32)
    gmod = nc.alloc_sbuf_tensor("gmod", [128, NE], F32)
    gmask = nc.alloc_sbuf_tensor("gmask", [128, NE], F32)
    negm1 = nc.alloc_sbuf_tensor("negm1", [128, 1], F32)
    srec = nc.alloc_sbuf_tensor("srec", [128, 1], F32)
    comb = nc.alloc_sbuf_tensor("comb", [128, NE], F32)
    nq_d = nc.dram_tensor("nq_d", [1, NH * S], F32, kind="Internal").ap()
    nk_d = nc.dram_tensor("nk_d", [1, NH * S], F32, kind="Internal").ap()
    rows_d = nc.dram_tensor("rows_d", [8, S], F32, kind="Internal").ap()

    nc.const_aps.register(F32, 0.0, cvt[:, 0:1])
    nc.const_aps.register(F32, NEPS, cvt[:, 1:2])

    PS = nc.alloc_psum_tensor("PS", [128, 8 * TG], F32)

    def bank(i, n=1, p0=0, p1=128):
        return PS[p0:p1, i * TG:(i + n) * TG]

    (R_A2, R_B2, R_A3, R_B3) = range(4)

    for dst, src in ((cvt, cv_h), (nsc, nsc_h), (gbb, gbb_h), (b1pp, b1pp_h),
                     (ident, ident_h), (onescol, onescol_h),
                     (onesrow, onesrow_h), (masks, mask_h)):
        p.dma("sy", dst[:], src, "s_init")
    for dst, src in ((rT, rT_h), (onescolb, onescolb_h), (onesrowb, onesrowb_h)):
        p.dma("gp", dst[:], src, "s_initg")
    p.dma("gp", gw_sb.ap().rearrange("p (c n) -> p c n", c=NFC),
          gwT.rearrange("(c p) n -> p c n", p=128), "s_initg")
    p.dma("gp", b2bf[:], b2_h, "s_initg")
    T_INIT = [("s_init", p.cnt["s_init"]), ("s_initg", p.cnt["s_initg"])]

    b1r = b1pp.ap().rearrange("p (e c) -> p e c", e=NE)
    p.waits("ve", T_INIT)
    p.op("ve", lambda e: e.tensor_scalar(
        b1sig.ap().rearrange("p (e c) -> p e c", e=NE), b1r[:, :, 0:8], 1.702, None, ALU.mult))
    p.op("ve", lambda e: e.tensor_scalar(
        b1lin1.ap().rearrange("p (e c) -> p e c", e=NE), b1r[:, :, 8:16], 1.0, None, ALU.add), inc=True)

    ln1c = lambda fc, lo=0, n=TG: A1b[:, fc * 512 + lo: fc * 512 + lo + n]
    xcell = [A1[:, 2048 + i * 512: 2048 + (i + 1) * 512] for i in range(3)]
    sqcell = [A1[:, 3584 + i * 512: 3584 + (i + 1) * 512] for i in range(2)]
    vw = [A2b[:, i * 4096:(i + 1) * 4096] for i in range(2)]
    wqk = [A2b[:, 8192 + i * 1024: 8192 + (i + 1) * 1024] for i in range(2)]
    cosc = [A2b[:, 10240 + i * 512: 10240 + (i + 1) * 512] for i in range(2)]
    sinc = [A2b[:, 11264 + i * 512: 11264 + (i + 1) * 512] for i in range(2)]
    vcell = [A2b[:, 12288 + i * 512: 12288 + (i + 1) * 512] for i in range(4)]
    qraw = [A2b[:, 14336 + i * 512: 14336 + (i + 1) * 512] for i in range(2)]
    qrope = [A2b[:, 15360 + i * 512: 15360 + (i + 1) * 512] for i in range(2)]
    sqro = [A2b[:, 16384 + i * 512: 16384 + (i + 1) * 512] for i in range(2)]
    qcc = [A2[:, 8704 + i * 512: 8704 + (i + 1) * 512] for i in range(2)]
    rsc = [A2[:, 9728 + i * 512: 9728 + (i + 1) * 512] for i in range(2)]
    cbA = A2[:, 10752:11264]
    cbB = A2[:, 11264:11776]

    xT_r = xT.rearrange("(c p) t -> c p t", p=128)
    wq_r = wqT.rearrange("(c p) n -> p c n", p=128)
    wk_r = wkT.rearrange("(c p) n -> p c n", p=128)
    wv_r = wvT.rearrange("(c p) n -> p c n", p=128)

    bx, bsq = Buf(3), Buf(2)
    bwqk, bvw, bvc = Buf(2), Buf(2), Buf(4)
    bqraw, bqc, bqro, bsqro, bcs = Buf(2), Buf(2), Buf(2), Buf(2), Buf(2)
    b_proj, b_rot, b_stat, b_vps = Buf(2), Buf(2), Buf(1), Buf(2)
    b_ln1, b_cb = Buf(1), Buf(1)
    bnst = Buf(2)

    p.waits("pe", T_INIT)
    p.waits("ae", T_INIT)
    xi = 0
    for tg in range(NTG):
        lo = tg * TG
        cs = tg % 2
        bcs.acquire(p, "gp", cs)
        t_c1 = p.dma("gp", cosc[cs], cos_h[:, lo:lo + TG], f"s_cs{cs}")
        t_c2 = p.dma("gp", sinc[cs], sin_h[:, lo:lo + TG], f"s_cs{cs}")
        b_stat.acquire(p, "pe", 0)
        for fc in range(NFC):
            sl = xi % 3
            xi += 1
            bx.acquire(p, "sy", sl)
            tx = p.dma("sy", xcell[sl], xT_r[fc, :, lo:lo + TG], f"s_x{sl}")
            ss = fc % 2
            bsq.acquire(p, "ae", ss)
            p.wait("ae", tx)
            tq = p.op("ae", lambda e, o=sqcell[ss], i=xcell[sl]: e.activation(o, i, AF.Square), inc=True)
            p.wait("pe", tx)
            p.op("pe", lambda e, i=xcell[sl], st=(fc == 0), sp=(fc == NFC - 1):
                 e.matmul(bank(2, p1=1), onescol[:, 0:1], i, start=st, stop=sp))
            p.wait("pe", tq)
            tmm = p.op("pe", lambda e, i=sqcell[ss], st=(fc == 0), sp=(fc == NFC - 1):
                       e.matmul(bank(3, p1=1), onescol[:, 0:1], i, start=st, stop=sp), inc=True)
            bx.release(sl, tmm)
            bsq.release(ss, tmm)
        p.wait("ve", tmm)
        p.op("ve", lambda e: e.tensor_scalar(rows2[0:1, 0:512], bank(2, p1=1), 1.0 / H, None, ALU.mult))
        p.op("ve", lambda e: e.tensor_scalar(rows2[0:1, 512:1024], bank(3, p1=1), 1.0 / H, None, ALU.mult))
        p.op("ve", lambda e: e.tensor_mul(rows2[0:1, 1024:1536], rows2[0:1, 0:512], rows2[0:1, 0:512]))
        tvar = p.op("ve", lambda e: e.tensor_sub(rows2[0:1, 1536:2048], rows2[0:1, 512:1024], rows2[0:1, 1024:1536]), inc=True)
        b_stat.release(0, tvar)
        p.wait("ae", tvar)
        p.op("ae", lambda e: e.activation(rows2[0:1, 1024:1536], rows2[0:1, 1536:2048], AF.Ln, bias=NEPS))
        trs = p.op("ae", lambda e: e.activation(rows2[0:1, 2048:2560], rows2[0:1, 1024:1536], AF.Exp, scale=-0.5), inc=True)
        p.wait("ve", trs)
        tbr = p.op("ve", lambda e:
                   e.scalar_tensor_tensor(rows2[0:1, 2560:3072], rows2[0:1, 0:512], -1.0, rows2[0:1, 2048:2560], ALU.mult, ALU.mult), inc=True)
        b_stat.acquire(p, "pe", 0)
        p.wait("pe", tbr)
        p.op("pe", lambda e: e.matmul(bank(2), onesrow[:], rows2[0:1, 2048:2560], start=True, stop=True))
        tbc = p.op("pe", lambda e: e.matmul(bank(3), onesrow[:], rows2[0:1, 2560:3072], start=True, stop=True), inc=True)
        b_cb.acquire(p, "ve", 0)
        p.wait("ve", tbc)
        p.op("ve", lambda e: e.tensor_copy(cbA, bank(2)))
        tcb = p.op("ve", lambda e: e.tensor_copy(cbB, bank(3)), inc=True)
        b_stat.release(0, tcb)
        b_ln1.acquire(p, "ve", 0)
        for fc in range(NFC):
            sl = xi % 3
            xi += 1
            bx.acquire(p, "sy", sl)
            tx = p.dma("sy", xcell[sl], xT_r[fc, :, lo:lo + TG], f"s_x{sl}")
            p.wait("ve", tx)
            p.op("ve", lambda e, i=xcell[sl], o=rsc[fc % 2]: e.tensor_mul(o, i, cbA))
            tl = p.op("ve", lambda e, o=ln1c(fc), i=rsc[fc % 2]: e.tensor_add(o, i, cbB), inc=True)
            bx.release(sl, tl)
        t_ln1 = tl
        b_cb.release(0, t_ln1)

        for og in range(2):
            ws = og % 2
            bvw.acquire(p, "gp", ws)
            tw = p.dma("gp", vw[ws].rearrange("p (c n) -> p c n", c=NFC),
                       wv_r[:, :, og * 512:(og + 1) * 512], f"s_vw{ws}")
            p.wait("pe", tw)
            p.wait("pe", t_ln1)
            for tt in range(4):
                vb = tt % 2
                b_vps.acquire(p, "pe", vb)
                for fc in range(NFC):
                    tpm = p.op("pe", lambda e, l=ln1c(fc, tt * 128, 128),
                               r=vw[ws].rearrange("p (c n) -> p c n", c=NFC)[:, fc, :],
                               o=bank(4 + vb), st=(fc == 0), sp=(fc == NFC - 1):
                               e.matmul(o, l, r, start=st, stop=sp), inc=(fc == NFC - 1))
                vc = (og * 4 + tt) % 4
                bvc.acquire(p, "ve", vc)
                p.wait("ve", tpm)
                tve = p.op("ve", lambda e, o=vcell[vc], b=bank(4 + vb): e.tensor_copy(o, b), inc=True)
                b_vps.release(vb, tve)
                p.wait("sy", tve)
                tvd = p.dma("sy", v_s[lo + tt * 128: lo + (tt + 1) * 128, og * 512:(og + 1) * 512],
                            vcell[vc], f"s_vs{vc}")
                bvc.release(vc, tvd)
            bvw.release(ws, tpm)

        for wi, (wh_r, dst, ndst) in enumerate(((wq_r, q_s, nq_d), (wk_r, k_s, nk_d))):
            for oc in range(NFC):
                ws = oc % 2
                bwqk.acquire(p, "gp", ws)
                tw = p.dma("gp", wqk[ws].rearrange("p (c n) -> p c n", c=NFC),
                           wh_r[:, :, oc * 128:(oc + 1) * 128], f"s_w{ws}")
                pb = oc % 2
                b_proj.acquire(p, "pe", pb)
                p.wait("pe", tw)
                if oc == 0 and wi == 0:
                    p.wait("pe", t_ln1)
                for fc in range(NFC):
                    tpm = p.op("pe", lambda e, l=wqk[ws].rearrange("p (c n) -> p c n", c=NFC)[:, fc, :],
                               r=ln1c(fc), o=bank(pb), st=(fc == 0), sp=(fc == NFC - 1):
                               e.matmul(o, l, r, start=st, stop=sp), inc=(fc == NFC - 1))
                bwqk.release(ws, tpm)
                qs2 = oc % 2
                bqraw.acquire(p, "ae", qs2)
                p.wait("ae", tpm)
                tqr = p.op("ae", lambda e, o=qraw[qs2], b=bank(pb): e.copy(o, b), inc=True)
                bqc.acquire(p, "ve", qs2)
                p.wait("ve", tqr)
                if oc < 2 and wi == 0:
                    p.waits("ve", [t_c1, t_c2])
                tqc = p.op("ve", lambda e, o=qcc[qs2], b=bank(pb), c=cosc[cs]: e.tensor_mul(o, b, c), inc=True)
                b_proj.release(pb, tqc)
                rb_ = oc % 2
                b_rot.acquire(p, "pe", rb_)
                p.wait("pe", tqc)
                trm = p.op("pe", lambda e, o=bank(6 + rb_), i=qraw[qs2]:
                           e.matmul(o, rT[:], i, start=True, stop=True), inc=True)
                bqraw.release(qs2, trm)
                bqro.acquire(p, "ve", qs2)
                p.wait("ve", trm)
                p.op("ve", lambda e, o=rsc[qs2], b=bank(6 + rb_), s2=sinc[cs]: e.tensor_mul(o, b, s2))
                tqo = p.op("ve", lambda e, o=qrope[qs2], a=qcc[qs2], b=rsc[qs2]: e.tensor_add(o, a, b), inc=True)
                b_rot.release(rb_, tqo)
                bqc.release(qs2, tqo)
                bsqro.acquire(p, "ae", qs2)
                p.wait("ae", tqo)
                tq2 = p.op("ae", lambda e, o=sqro[qs2], i=qrope[qs2]: e.activation(o, i, AF.Square), inc=True)
                b_stat.acquire(p, "pe", 0)
                p.wait("pe", tq2)
                p.op("pe", lambda e, i=sqro[qs2]: e.matmul(bank(2, p1=1), onescolb[:, 1:2], i, start=True, stop=True))
                tnm = p.op("pe", lambda e, i=sqro[qs2]: e.matmul(bank(3, p1=1), onescolb[:, 2:3], i, start=True, stop=True), inc=True)
                bsqro.release(qs2, tnm)
                p.wait("ve", tnm)
                c0 = (2 * oc) * S + lo
                c1 = (2 * oc + 1) * S + lo
                bnst.acquire(p, "ve", qs2)
                p.op("ve", lambda e, o=nstage[0:1, qs2 * 1024: qs2 * 1024 + 512]: e.tensor_copy(o, bank(2, p1=1)))
                tne = p.op("ve", lambda e, o=nstage[0:1, qs2 * 1024 + 512: qs2 * 1024 + 1024]: e.tensor_copy(o, bank(3, p1=1)), inc=True)
                b_stat.release(0, tne)
                p.wait("sy", tne)
                p.dma("sy", ndst[0:1, c0:c0 + TG], nstage[0:1, qs2 * 1024: qs2 * 1024 + 512], f"s_nq{qs2}")
                tnd = p.dma("sy", ndst[0:1, c1:c1 + TG], nstage[0:1, qs2 * 1024 + 512: qs2 * 1024 + 1024], f"s_nq{qs2}")
                bnst.release(qs2, tnd)
                p.wait("sy", tqo)
                tqd = p.dma("sy", dst[oc, :, lo:lo + TG], qrope[qs2][:, :], f"s_qk{qs2}")
                bqro.release(qs2, tqd)
            if wi == 1:
                b_ln1.release(0, tpm)
        bcs.release(cs, tqo)

    if debug_stage == 1:
        _final(p)
        return _wire(nc, p)

    PH1 = p.now()

    qh2 = [A1b[:, i * 4096:(i + 1) * 4096] for i in range(2)]
    kh2 = [A1b[:, 8192 + i * 4096: 8192 + (i + 1) * 4096] for i in range(2)]
    vh2 = [[A1b[:, 16384 + (2 * i + j) * 2080: 16384 + (2 * i + j + 1) * 2080]
            for j in range(2)] for i in range(2)]
    pcell = [A1b[:, 24704 + i * 512: 24704 + (i + 1) * 512] for i in range(4)]
    pcp2 = [A1b[:, 24704 + i * 1024: 24704 + (i + 1) * 1024] for i in range(2)]
    osbc = [A1b[:, 26752 + i * 512: 26752 + (i + 1) * 512] for i in range(4)]
    nstage = [A2[0:1, i * 2048:(i + 1) * 2048] for i in range(2)]
    rqrow = A2[0:1, 4096:4128]
    rkrow = A2[0:1, 4128:4160]
    nbsc = A2[0:1, 4224:4288]
    zraw = A2[0:1, 4608:5632]
    zrecb = A2b[0:1, 11264:12288]
    zln = A2[0:1, 8704:9728]
    rbc = [A2[:, 6656 + i * 512: 6656 + (i + 1) * 512] for i in range(2)]
    msb = [A2[:, 7680 + i * 512: 7680 + (i + 1) * 512] for i in range(2)]
    msb2 = A2[:, 7680:8704]

    bqk, bvh = Buf(2), Buf(2)
    bpcp, bmsbU, bosb = Buf(2), Buf(1), Buf(2)
    bavz, bnst = Buf(2), Buf(2)
    sguard = {0: [], 1: [], 2: [], 3: []}
    rot = [0]

    def claim_slot(extra=()):
        r = rot[0] % 2
        rot[0] += 1
        bA, bB = 2 * r, 2 * r + 1
        p.waits("pe", sguard[bA] + sguard[bB] + [t for t in extra if t is not None])
        sguard[bA], sguard[bB] = [], []
        return bA, bB

    for eng in ENG:
        p.waits(eng, PH1)

    nsl = 0
    pend = [None]
    tzeh, trbh = [None], [None]
    trd = None
    for hh in range(NH):
        for src_d, rrow in ((nq_d, rqrow), (nk_d, rkrow)):
            for part in range(2):
                bnst.acquire(p, "ae", nsl)
                tld = p.dma("ae", nstage[nsl % 2],
                            src_d[0:1, hh * S + part * 2048: hh * S + (part + 1) * 2048], "s_nrm")
                p.wait("ve", tld)
                trd = p.op("ve", lambda e, o=rrow[0:1, hh * 2 + part: hh * 2 + part + 1],
                           i=nstage[nsl % 2]: e.tensor_reduce(o, i, AX.X, ALU.max), inc=True)
                bnst.release(nsl, trd)
                nsl += 1
    p.wait("ve", trd)
    tmx = None
    for hh in range(NH):
        p.op("ve", lambda e, o=nbsc[0:1, hh:hh + 1], a=rqrow[0:1, 2 * hh:2 * hh + 1],
             b=rqrow[0:1, 2 * hh + 1:2 * hh + 2]: e.tensor_max(o, a, b))
        tmx = p.op("ve", lambda e, o=nbsc[0:1, 16 + hh:17 + hh], a=rkrow[0:1, 2 * hh:2 * hh + 1],
                   b=rkrow[0:1, 2 * hh + 1:2 * hh + 2]: e.tensor_max(o, a, b), inc=(hh == NH - 1))
    p.wait("ve", tmx)
    tml = p.op("ve", lambda e: e.tensor_mul(nbsc[0:1, 32:48], nbsc[0:1, 0:16], nbsc[0:1, 16:32]), inc=True)
    p.wait("ve", tml)
    tpm = None
    for pj in range(NH // 2):
        tpm = p.op("ve", lambda e, o=nbsc[0:1, 48 + pj:49 + pj], a=nbsc[0:1, 32 + 2 * pj:33 + 2 * pj],
                   b=nbsc[0:1, 33 + 2 * pj:34 + 2 * pj]: e.tensor_max(o, a, b), inc=(pj == NH // 2 - 1))
    p.wait("ae", tpm)
    p.op("ae", lambda e: e.activation(nbsc[0:1, 56:64], nbsc[0:1, 48:56], AF.Ln))
    tsq = p.op("ae", lambda e: e.activation(nbsc[0:1, 32:40], nbsc[0:1, 56:64], AF.Exp, scale=0.5), inc=True)
    p.wait("ve", tsq)
    tnb = p.op("ve", lambda e, o=A2b[0:1, 8576:8584]:
               e.tensor_scalar(o, nbsc[0:1, 32:40], -0.125, None, ALU.mult), inc=True)
    rA, rB = claim_slot([tnb])
    tnm = p.op("pe", lambda e, o=bank(rA)[:, 0:8]: e.matmul(o, onesrowb[:], A2b[0:1, 8576:8584],
                                                            start=True, stop=True), inc=True)
    p.wait("ve", tnm)
    tnbc = p.op("ve", lambda e, o=nb_sb[:, 0:8]: e.tensor_copy(o, bank(rA)[:, 0:8]), inc=True)
    sguard[rA].append(tnbc)

    for pp in range(NH // 2):
        s = pp % 2
        bqk.acquire(p, "sy", s)
        tq = p.dma("sy", qh2[s], q_s[pp], f"s_qh{s}")
        tk = p.dma("sy", kh2[s], k_s[pp], f"s_qh{s}")
        bvh.acquire(p, "sy", s)
        vr = [vh2[s][j].rearrange("p (c d) -> p c d", d=65) for j in range(2)]
        tvh = None
        for j, hh in enumerate((2 * pp, 2 * pp + 1)):
            tvh = p.dma("sy", vr[j][:, :, 0:64],
                        v_s.rearrange("(c p) o -> p c o", p=128)[:, :, 64 * hh: 64 * hh + 64], f"s_vh{s}")
        p.wait("ve", tvh)
        tvm = None
        for j in range(2):
            tvm = p.op("ve", lambda e, o=vr[j][:, :, 64:65]: e.memset(o, 1.0), inc=(j == 1))

        nbc = nb_sb[:, pp:pp + 1]
        qk_last = av_last = None
        for qg in range(NTG):
            nkc = 4 * (qg + 1)
            qpar = qg % 2
            avA, avB = 4 + 2 * qpar, 5 + 2 * qpar
            kinfo = []

            def emit_av(kc, te, qg=qg, qpar=qpar, avA=avA, avB=avB, nkc=nkc, pp=pp, s=s):
                if kc == 0:
                    bavz.acquire(p, "pe", qpar)
                p.wait("pe", te)
                if qg == 0 and kc == 0:
                    p.waits("pe", [tvh, tvm])
                st_, sp_ = (kc == 0), (kc == nkc - 1)
                sp2 = kc % 2
                vrl = [vh2[s][j].rearrange("p (c d) -> p c d", d=65) for j in range(2)]
                p.op("pe", lambda e, o=PS[0:65, avA * TG:(avA + 1) * TG], l=vrl[0][:, kc, :],
                     r=pcell[2 * sp2], st=st_, sp=sp_: e.matmul(o, l, r, start=st, stop=sp))
                t_ = p.op("pe", lambda e, o=PS[0:65, avB * TG:(avB + 1) * TG], l=vrl[1][:, kc, :],
                          r=pcell[2 * sp2 + 1], st=st_, sp=sp_: e.matmul(o, l, r, start=st, stop=sp),
                          inc=True)
                bpcp.release(sp2, t_)
                return t_

            for kc in range(nkc):
                first = (qg == 0 and kc == 0)
                bA, bB = claim_slot([tq, tk] if first else ())
                p.op("pe", lambda e, o=bank(bA), l=kh2[s][0:64, kc * 128:(kc + 1) * 128],
                     r=qh2[s][0:64, qg * TG:(qg + 1) * TG]:
                     e.matmul(o, l, r, start=True, stop=True))
                qk_last = p.op("pe", lambda e, o=bank(bB), l=kh2[s][64:128, kc * 128:(kc + 1) * 128],
                               r=qh2[s][64:128, qg * TG:(qg + 1) * TG]:
                               e.matmul(o, l, r, start=True, stop=True), inc=True)
                sp2 = kc % 2
                bpcp.acquire(p, "ae", sp2)
                if kc >= 4 * qg:
                    r_ = kc - 4 * qg
                    bmsbU.acquire(p, "ve", 0)
                    p.wait("ve", qk_last)
                    tmA = p.op("ve", lambda e, o=msb[0], b=bank(bA), m=masks[:, r_ * TG:(r_ + 1) * TG]:
                               e.tensor_add(o, b, m), inc=True)
                    tmB = p.op("ve", lambda e, o=msb[1], b=bank(bB), m=masks[:, r_ * TG:(r_ + 1) * TG]:
                               e.tensor_add(o, b, m), inc=True)
                    sguard[bA].append(tmA)
                    sguard[bB].append(tmB)
                    p.wait("ae", tmB)
                    if first:
                        p.wait("ae", tnbc)
                    te = p.op("ae", lambda e, o=pcp2[sp2], i=msb2, bi=nbc:
                              e.activation(o, i, AF.Exp, bias=bi, scale=0.125), inc=True)
                    bmsbU.release(0, te)
                else:
                    p.wait("ae", qk_last)
                    te = p.op("ae", lambda e, o=pcp2[sp2], b=PS[:, bA * TG:(bB + 1) * TG], bi=nbc:
                              e.activation(o, b, AF.Exp, bias=bi, scale=0.125), inc=True)
                    sguard[bA].append(te)
                    sguard[bB].append(te)
                kinfo.append(te)
                if kc >= 1:
                    av_last = emit_av(kc - 1, kinfo[kc - 1])
                if kc == 3 and pend[0] is not None:
                    pend[0]()
                    pend[0] = None
            av_last = emit_av(nkc - 1, kinfo[nkc - 1])

            def make_tail(pp=pp, qg=qg, qpar=qpar, avA=avA, avB=avB, av_last=av_last):
                p.wait("ve", av_last)
                p.wait("ve", tzeh[0])
                p.op("ve", lambda e, o=zraw[0:1, 0:512], b=PS[64:65, avA * TG:(avA + 1) * TG]:
                     e.tensor_scalar(o, b, 1e-30, None, ALU.add))
                tz2 = p.op("ve", lambda e, o=zraw[0:1, 512:1024], b=PS[64:65, avB * TG:(avB + 1) * TG]:
                           e.tensor_scalar(o, b, 1e-30, None, ALU.add), inc=True)
                p.waits("ae", [tz2, trbh[0]])
                p.op("ae", lambda e: e.activation(zln[0:1, 0:1024], zraw[0:1, 0:1024], AF.Ln))
                tze = p.op("ae", lambda e: e.activation(zrecb[0:1, 0:1024], zln[0:1, 0:1024],
                                                        AF.Exp, scale=-1.0), inc=True)
                tzeh[0] = tze

                def emit_pe():
                    rA2, rB2 = claim_slot([tze])
                    p.op("pe", lambda e, o=PS[0:64, rA2 * TG:(rA2 + 1) * TG]:
                         e.matmul(o, onesrowb[0:1, 0:64], zrecb[0:1, 0:512], start=True, stop=True))
                    trb = p.op("pe", lambda e, o=PS[0:64, rB2 * TG:(rB2 + 1) * TG]:
                               e.matmul(o, onesrowb[0:1, 0:64], zrecb[0:1, 512:1024],
                                        start=True, stop=True), inc=True)
                    trbh[0] = trb
                    p.wait("ve", trb)
                    p.op("ve", lambda e, o=rbc[0][0:64, :]: e.tensor_copy(o, PS[0:64, rA2 * TG:(rA2 + 1) * TG]))
                    trbc = p.op("ve", lambda e, o=rbc[1][0:64, :]: e.tensor_copy(o, PS[0:64, rB2 * TG:(rB2 + 1) * TG]), inc=True)
                    sguard[rA2].append(trbc)
                    sguard[rB2].append(trbc)
                    bosb.acquire(p, "ve", qpar)
                    p.op("ve", lambda e, o=osbc[2 * qpar][0:64, :], b=PS[0:64, avA * TG:(avA + 1) * TG],
                         r=rbc[0][0:64, :]: e.tensor_mul(o, b, r))
                    tosb = p.op("ve", lambda e, o=osbc[2 * qpar + 1][0:64, :], b=PS[0:64, avB * TG:(avB + 1) * TG],
                                r=rbc[1][0:64, :]: e.tensor_mul(o, b, r), inc=True)
                    bavz.release(qpar, tosb, tz2)
                    p.wait("sy", tosb)
                    p.dma("sy", o_s[pp, 0:64, qg * TG:(qg + 1) * TG], osbc[2 * qpar][0:64, :], f"s_os{qpar}")
                    tod = p.dma("sy", o_s[pp, 64:128, qg * TG:(qg + 1) * TG], osbc[2 * qpar + 1][0:64, :], f"s_os{qpar}")
                    bosb.release(qpar, tod)
                return emit_pe

            pend[0] = make_tail()
        bqk.release(s, qk_last)
        bvh.release(s, av_last)
    if pend[0] is not None:
        pend[0]()
        pend[0] = None

    if debug_stage == 2:
        p.waits("sy", p.now("ve", "ae"))
        p.dma("sy", nc.dram_tensor("dbg_nb", [128, 16], F32, kind="ExternalOutput").ap(),
              nb_sb[:], "s_dbg")
        p.dma("sy", nc.dram_tensor("dbg_rows", [1, 256], F32, kind="ExternalOutput").ap(),
              A2[0:1, 4096:4352], "s_dbg")
        _final(p)
        return _wire(nc, p)
    PH2 = p.now()

    wo_sb = A2b[:, 0:8192]
    o_sb = [A2b[:, 8192 + i * 4096: 8192 + (i + 1) * 4096] for i in range(2)]
    x3cell = [A2[:, 8192 + i * 512: 8192 + (i + 1) * 512] for i in range(3)]
    x2cell = [A2[:, 9728 + i * 512: 9728 + (i + 1) * 512] for i in range(2)]
    sq2 = [A2[:, 10752 + i * 512: 10752 + (i + 1) * 512] for i in range(2)]

    for eng in ENG:
        p.waits(eng, PH2)
    t_wo = p.dma("gp", wo_sb.rearrange("p (c n) -> p c n", c=NFC),
                 woT.rearrange("(c p) n -> p c n", p=128), "s_wo")
    bos, bx3, bx2c, bsq2 = Buf(2), Buf(3), Buf(2), Buf(2)
    b_wops, b_st2 = Buf(2), Buf(1)
    wo_r = wo_sb.rearrange("p (c n) -> p c n", c=NFC)
    x2_r = x2_s.rearrange("(c p) t -> c p t", p=128)

    p.wait("pe", t_wo)
    xi = 0
    for tg in range(NTG):
        lo = tg * TG
        ts_ = tg % 2
        bos.acquire(p, "sy", ts_)
        tol = None
        for pp in range(NH // 2):
            dst = o_sb[ts_].rearrange("p (c n) -> p c n", c=NFC)[:, pp, :]
            tol = p.dma("sy", dst, o_s[pp, :, lo:lo + TG], f"s_ol{ts_}")
        b_st2.acquire(p, "pe", 0)
        tpm = None
        for oc in range(NFC):
            pb = oc % 2
            b_wops.acquire(p, "pe", pb)
            p.wait("pe", tol)
            for fc in range(NFC):
                tpm = p.op("pe", lambda e, o=bank(pb), l=wo_r[:, fc, oc * 128:(oc + 1) * 128],
                           r=o_sb[ts_].rearrange("p (c n) -> p c n", c=NFC)[:, fc, :],
                           st=(fc == 0), sp=(fc == NFC - 1):
                           e.matmul(o, l, r, start=st, stop=sp), inc=(fc == NFC - 1))
            sl = xi % 3
            xi += 1
            bx3.acquire(p, "sy", sl)
            tx = p.dma("sy", x3cell[sl], xT_r[oc, :, lo:lo + TG], f"s_x3{sl}")
            xs = oc % 2
            bx2c.acquire(p, "ve", xs)
            p.waits("ve", [tpm, tx])
            tx2 = p.op("ve", lambda e, o=x2cell[xs], b=bank(pb), x=x3cell[sl]: e.tensor_add(o, b, x), inc=True)
            b_wops.release(pb, tx2)
            bx3.release(sl, tx2)
            p.wait("sy", tx2)
            tx2d = p.dma("sy", x2_r[oc, :, lo:lo + TG], x2cell[xs], f"s_x2{xs}")
            ss = oc % 2
            bsq2.acquire(p, "ae", ss)
            p.wait("ae", tx2)
            tq2 = p.op("ae", lambda e, o=sq2[ss], i=x2cell[xs]: e.activation(o, i, AF.Square), inc=True)
            p.wait("pe", tx2)
            p.op("pe", lambda e, i=x2cell[xs], st=(oc == 0), sp=(oc == NFC - 1):
                 e.matmul(bank(4, p1=1), onescol[:, 0:1], i, start=st, stop=sp))
            p.wait("pe", tq2)
            tsm = p.op("pe", lambda e, i=sq2[ss], st=(oc == 0), sp=(oc == NFC - 1):
                       e.matmul(bank(5, p1=1), onescol[:, 0:1], i, start=st, stop=sp), inc=True)
            bx2c.release(xs, tsm, tx2d)
            bsq2.release(ss, tsm)
        bos.release(ts_, tpm)
        p.wait("ve", tsm)
        p.op("ve", lambda e: e.tensor_scalar(rows2[0:1, 0:512], bank(4, p1=1), 1.0 / H, None, ALU.mult))
        p.op("ve", lambda e: e.tensor_scalar(rows2[0:1, 512:1024], bank(5, p1=1), 1.0 / H, None, ALU.mult))
        p.op("ve", lambda e: e.tensor_mul(rows2[0:1, 1024:1536], rows2[0:1, 0:512], rows2[0:1, 0:512]))
        tvar = p.op("ve", lambda e: e.tensor_sub(rows2[0:1, 1536:2048], rows2[0:1, 512:1024], rows2[0:1, 1024:1536]), inc=True)
        b_st2.release(0, tvar)
        p.wait("ae", tvar)
        p.op("ae", lambda e: e.activation(rows2[0:1, 1024:1536], rows2[0:1, 1536:2048], AF.Ln, bias=NEPS))
        tr2 = p.op("ae", lambda e: e.activation(rows2[0:1, 2048:2560], rows2[0:1, 1024:1536], AF.Exp, scale=-0.5), inc=True)
        tu = p.op("ve", lambda e: e.tensor_scalar(rows2[0:1, 2560:3072], rows2[0:1, 1536:2048], NEPS, None, ALU.add), inc=True)
        p.wait("ve", tu)
        p.wait("ve", tr2)
        tur = p.op("ve", lambda e: e.reciprocal(rows2[0:1, 1024:1536], rows2[0:1, 2560:3072]), inc=True)
        p.wait("ve", tur)
        tq_ = p.op("ve", lambda e: e.tensor_mul(rows2[0:1, 2560:3072], rows2[0:1, 1536:2048], rows2[0:1, 1024:1536]), inc=True)
        p.wait("ae", tq_)
        p.op("ae", lambda e: e.activation(rows2[0:1, 1024:1536], rows2[0:1, 2560:3072], AF.Ln, bias=NEPS))
        trs2 = p.op("ae", lambda e: e.activation(rows2[0:1, 2560:3072], rows2[0:1, 1024:1536], AF.Exp, scale=-0.5), inc=True)
        p.waits("ve", [tr2, trs2])
        p.op("ve", lambda e: e.tensor_mul(rows2[0:1, 1024:1536], rows2[0:1, 2048:2560], rows2[0:1, 2560:3072]))
        p.op("ve", lambda e:
             e.scalar_tensor_tensor(rows2[0:1, 1536:2048], rows2[0:1, 0:512], -1.0, rows2[0:1, 1024:1536], ALU.mult, ALU.mult))
        p.op("ve", lambda e: e.tensor_scalar(rows2[0:1, 512:1024], rows2[0:1, 2048:2560], 1.0, None, ALU.add))
        trows = p.op("ve", lambda e:
                     e.scalar_tensor_tensor(rows2[0:1, 2560:3072], rows2[0:1, 0:512], -1.0, rows2[0:1, 2048:2560], ALU.mult, ALU.mult), inc=True)
        p.wait("sy", trows)
        for ri, sl_ in ((R_A2, (1024, 1536)), (R_B2, (1536, 2048)), (R_A3, (512, 1024)), (R_B3, (2560, 3072))):
            trd = p.dma("sy", rows_d[ri:ri + 1, lo:lo + TG], rows2[0:1, sl_[0]:sl_[1]], "s_rows")

    if debug_stage == 3:
        _final(p)
        return _wire(nc, p)
    PH3 = p.now()

    w1buf = [A2b[:, k * 2048:(k + 1) * 2048] for k in range(3)]
    w2buf = [A2b[:, 6144 + k * 1024: 6144 + (k + 1) * 1024] for k in range(2)]
    w1r = [w.rearrange("p (c n) -> p c n", c=NFC) for w in w1buf]
    w2r = [w.rearrange("p (c n) -> p c n", c=NFC) for w in w2buf]
    sSc = [A2[:, 4096 + i * 1024: 4096 + (i + 1) * 1024] for i in range(2)]
    sGc = [A2[:, 6144 + i * 1024: 6144 + (i + 1) * 1024] for i in range(2)]
    q1c = [A2[:, 8192 + i * 1024: 8192 + (i + 1) * 1024] for i in range(2)]
    scr = [A2[:, 4096 + i * 512: 4096 + (i + 1) * 512] for i in range(5)]
    bcA2 = A2[:, 6656:7168]
    bcB2 = A2[:, 7168:7680]
    bcA3 = A2[:, 7680:8192]
    bcB3 = A2[:, 8192:8704]
    cbtf = [A2[:, 10240 + i * 512: 10240 + (i + 1) * 512] for i in range(4)]
    cbtf2 = [A2[:, 10240 + i * 1024: 10240 + (i + 1) * 1024] for i in range(2)]
    gbank = lambda base: PS[:, base * TG:(base + 2) * TG]
    lbank = lambda base: PS[:, (base + 2) * TG:(base + 4) * TG]
    gw_r = gw_sb.ap().rearrange("p (c n) -> p c n", c=NFC)
    out_r = outT.rearrange("(c p) t -> p c t", p=128)

    for eng in ENG:
        p.waits(eng, PH3)

    t_out_prev = None
    t_w1cons = [None, None, None]
    t_w2cons = [None, None, None]
    last_p_ve = {0: None, 1: None}
    t_w2rd = {0: None, 1: None}
    t_cbtmm_last = None
    for half in range(2):
        hlo = half * HALF
        acc = lambda oc, t0=0, n=HALF: A1[:, oc * 2048 + t0: oc * 2048 + t0 + n]
        tsb = lambda fc, t0=0, n=TG: A3b[:, fc * 2048 + t0: fc * 2048 + t0 + n]
        hsl = lambda jj, ti: A4b[:, (jj * 4 + ti) * 512: (jj * 4 + ti + 1) * 512]
        hsl2 = lambda jj, pr: A4b[:, (jj * 4 + pr * 2) * 512: (jj * 4 + pr * 2 + 2) * 512]

        p.wait("sy", t_out_prev)
        p.waits("ve", p.now("gp", "ae"))
        t_x2l = None
        for oc in range(NFC):
            t_x2l = p.dma("sy", acc(oc), x2_r[oc, :, hlo:hlo + HALF], "s_x2l")
        b_bc = Buf(1)
        tb_prev = None
        for tgi in range(4):
            lo = hlo + tgi * TG
            b_bc.acquire(p, "pe", 0)
            p.waits("pe", p.now("ve", "ae"))
            p.waits("sy", [tb_prev])
            trl = None
            for bi, ri in enumerate((R_A2, R_B2, R_A3, R_B3)):
                trl = p.dma("sy", rows2[0:1, bi * 512:(bi + 1) * 512], rows_d[ri:ri + 1, lo:lo + TG], "s_rld")
            p.wait("pe", trl)
            tb = None
            for bi in range(4):
                tb = p.op("pe", lambda e, o=bank(bi), r=rows2[0:1, bi * 512:(bi + 1) * 512]:
                          e.matmul(o, onesrow[:], r, start=True, stop=True), inc=(bi == 3))
            tb_prev = tb
            p.wait("ve", tb)
            tbe = None
            for bi, dstc in enumerate((bcA2, bcB2, bcA3, bcB3)):
                tbe = p.op("ve", lambda e, o=dstc, b=bank(bi): e.tensor_copy(o, b), inc=(bi == 3))
            b_bc.release(0, tbe)
            p.wait("ve", t_x2l)
            tacc = None
            for oc in range(NFC):
                x2c = acc(oc, tgi * TG, TG)
                p.op("ve", lambda e, o=scr[0], i=x2c: e.tensor_mul(o, i, bcA2))
                p.op("ve", lambda e, o=scr[1], i=scr[0]: e.tensor_add(o, i, bcB2))
                p.op("ve", lambda e, o=tsb(oc, tgi * TG, TG), i=scr[1], s2=nsc[:, oc:oc + 1]:
                     e.tensor_scalar(o, i, s2, None, ALU.mult))
                p.op("ve", lambda e, o=scr[2], i=x2c: e.tensor_mul(o, i, bcA3))
                tacc = p.op("ve", lambda e, o=x2c, i=scr[2]: e.tensor_add(o, i, bcB3), inc=True)
        t_prep = tacc
        tbe_last = tbe
        last_p_ve = {0: t_prep, 1: t_prep}

        b_g, b_tr = Buf(1), Buf(1)
        t_cfm = None
        tgb = None
        for ti in range(16):
            tlo = ti * 128
            b_g.acquire(p, "pe", 0)
            p.wait("pe", t_prep)
            tgm = None
            for fc in range(NFC):
                tgm = p.op("pe", lambda e, o=bank(6)[:, 0:NE], l=tsb(fc, tlo, 128), r=gw_r[:, fc, :],
                           st=(fc == 0), sp=(fc == NFC - 1):
                           e.matmul(o, l, r, start=st, stop=sp), inc=(fc == NFC - 1))
            p.wait("ve", tgm)
            tgb = p.op("ve", lambda e: e.tensor_add(g_sb[:], bank(6)[:, 0:NE], gbb[:]), inc=True)
            b_g.release(0, tgb)
            p.wait("ve", tgb)
            tt8 = p.op("ve", lambda e: e.max(top8[:], g_sb[:]), inc=True)
            p.wait("ve", tt8)
            p.op("ve", lambda e: e.memset(m8[:], 1e30))
            p.op("ve", lambda e: e.tensor_copy(gmod[:], g_sb[:]))
            for r_ in range(4):
                tmv = p.op("ve", lambda e, i=r_: e.tensor_copy(m8[:, 0:1], top8[:, i:i + 1]), inc=True)
                p.wait("ve", tmv)
                tmr = p.op("ve", lambda e: e.match_replace(gmod[:], m8[:], gmod[:], -1e30), inc=True)
            p.wait("ve", tmr)
            tmk = p.op("ve", lambda e: e.tensor_scalar(gmask[:], gmod[:], -1e29, None, ALU.is_le), inc=True)
            tng = p.op("ve", lambda e: e.tensor_scalar(negm1[:], top8[:, 0:1], -1.0, None, ALU.mult), inc=True)
            p.waits("ae", [tng, tgb])
            tex = p.op("ae", lambda e: e.activation(comb[:], g_sb[:], AF.Exp, bias=negm1[:]), inc=True)
            p.waits("ve", [tex, tmk])
            tcu = p.op("ve", lambda e: e.tensor_mul(comb[:], comb[:], gmask[:]), inc=True)
            p.wait("ve", tcu)
            tsu = p.op("ve", lambda e: e.reduce_sum(srec[:], comb[:], AX.X), inc=True)
            p.wait("ve", tsu)
            trc = p.op("ve", lambda e: e.reciprocal(srec[:], srec[:]), inc=True)
            p.wait("ve", trc)
            tcf = p.op("ve", lambda e: e.tensor_scalar(comb[:], comb[:], srec[:], None, ALU.mult), inc=True)
            b_tr.acquire(p, "pe", 0)
            p.wait("pe", tcf)
            ttr = p.op("pe", lambda e: e.transpose(PS[0:NE, 7 * TG: 7 * TG + 128], comb[:], ident[:]), inc=True)
            p.wait("ve", ttr)
            t_cfm = p.op("ve", lambda e, o=cfm[0:NE, tlo:tlo + 128]: e.tensor_copy(o, PS[0:NE, 7 * TG: 7 * TG + 128]), inc=True)
            b_tr.release(0, t_cfm)

        p.waits("gp", [t_cfm, t_cbtmm_last])
        t_crow = p.dma("gp", crow[0:1, 0:HALF], cfm[0:1, :], "s_cr")
        p.waits("pe", [tgb, t_cfm, t_crow])
        tcbt0 = []
        for ti4 in range(4):
            t_cbtmm_last = p.op("pe", lambda e, o=bank(4 + ti4), r=crow[0:1, ti4 * 512:(ti4 + 1) * 512]:
                                e.matmul(o, onesrowb[:], r, start=True, stop=True), inc=True)
            p.wait("ve", t_cbtmm_last)
            tcc = p.op("ve", lambda e, o=cbtf[ti4], b=bank(4 + ti4): e.tensor_copy(o, b), inc=True)
            tcbt0.append(tcc)

        bb2 = Buf(2)
        bb2.release(0, tbe_last)
        bb2.release(1, tbe_last)
        ta2_pb = {}
        p.wait("pe", t_cfm)
        for oc in range(NFC):
            for tgi in range(4):
                pb = (oc * 4 + tgi) % 2
                bb2.acquire(p, "pe", pb)
                tb2 = p.op("pe", lambda e, o=bank(pb), l=b2bf[:, oc * 128:(oc + 1) * 128],
                           r=cfm[0:NE, tgi * TG:(tgi + 1) * TG]:
                           e.matmul(o, l, r, start=True, stop=True), inc=True)
                p.wait("ve", tb2)
                ta2 = p.op("ve", lambda e, o=acc(oc, tgi * TG, TG), b=bank(pb):
                           e.tensor_add(o, o, b), inc=True)
                bb2.release(pb, ta2)
                ta2_pb[pb] = ta2

        if debug_stage == 4 and half == 0:
            p.waits("sy", p.now("ve"))
            for nm, tn in (("dbg_g", g_sb), ("dbg_top8", top8), ("dbg_gmod", gmod),
                           ("dbg_gmask", gmask), ("dbg_comb", comb)):
                p.dma("sy", nc.dram_tensor(nm, list(tn.shape), F32, kind="ExternalOutput").ap(),
                      tn[:], "s_dbg")
            p.dma("sy", nc.dram_tensor("dbg_acc", [128, 16384], F32, kind="ExternalOutput").ap(),
                  A1.ap(), "s_dbg")
            p.dma("sy", nc.dram_tensor("dbg_t", [128, 16384], BF16, kind="ExternalOutput").ap(),
                  A3b[:, 0:16384], "s_dbg")
            p.dma("sy", nc.dram_tensor("dbg_cfm", [NE, HALF], BF16, kind="ExternalOutput").ap(),
                  cfm[:], "s_dbg")
            _final(p)
            return _wire(nc, p)

        b_set = Buf(2)
        b_set.release(0, ta2_pb.get(0), ta2_pb.get(1), tbe_last)
        b_set.release(1, *(tcbt0 + [tgb, t_cfm]))
        t_w1l, t_w2l = {}, {}
        for i in range(3):
            p.wait("gp", t_w1cons[i])
            t_w1l[i] = p.dma("gp", w1buf[i], w1n[0, i], "s_w1")
        for i in range(2):
            p.wait("gp", t_w2cons[i])
            t_w2l[i] = p.dma("gp", w2buf[i], w2n[0, i], "s_w2")
        gctr = 0
        t_hw_done = {0: None, 1: None}
        for e_ in range(_NE_RUN[0]):
            for jj in range(NFC):
                idx = e_ * 8 + jj
                for pr in range(2):
                    g = gctr
                    gctr += 1
                    gpar = g % 2
                    base = gpar * 4
                    b_set.acquire(p, "pe", g)
                    if pr == 0:
                        p.wait("pe", t_w1l[idx])
                        if idx == 0:
                            p.wait("pe", t_prep)
                    tgl = None
                    for t2 in range(2):
                        for fc in range(NFC):
                            tgl = p.op("pe", lambda e, o=bank(base + t2), l=w1r[idx % 3][:, fc, 0:128],
                                       r=tsb(fc, (pr * 2 + t2) * TG, TG), st=(fc == 0), sp=(fc == NFC - 1):
                                       e.matmul(o, l, r, start=st, stop=sp), inc=(fc == NFC - 1 and t2 == 1))
                    tll = None
                    for t2 in range(2):
                        for fc in range(NFC):
                            tll = p.op("pe", lambda e, o=bank(base + 2 + t2), l=w1r[idx % 3][:, fc, 128:256],
                                       r=tsb(fc, (pr * 2 + t2) * TG, TG), st=(fc == 0), sp=(fc == NFC - 1):
                                       e.matmul(o, l, r, start=st, stop=sp), inc=(fc == NFC - 1 and t2 == 1))
                    sS, sG, q1 = sSc[gpar], sGc[gpar], q1c[gpar]
                    bsap = b1sig[:, e_ * 8 + jj: e_ * 8 + jj + 1]
                    b1ap = b1pp[:, e_ * 16 + jj: e_ * 16 + jj + 1]
                    b1lap = b1lin1[:, e_ * 8 + jj: e_ * 8 + jj + 1]
                    p.wait("ae", tgl)
                    p.wait("ae", last_p_ve[gpar])
                    tsS = p.op("ae", lambda e, o=sS, b=gbank(base), bi=bsap:
                               e.activation(o, b, AF.Sigmoid, bias=bi, scale=1.702), inc=True)
                    p.wait("ve", tsS)
                    p.op("ve", lambda e, o=sG, b=gbank(base), bi=b1ap:
                         e.tensor_scalar(o, b, bi, 7.0, ALU.add, ALU.min))
                    tp = p.op("ve", lambda e, o=sG, i=sS, g2=sG:
                              e.scalar_tensor_tensor(o, i, SIGC, g2, ALU.min, ALU.mult), inc=True)
                    last_p_ve[gpar] = tp
                    p.wait("ve", tll)
                    tq1 = p.op("ve", lambda e, o=q1, b=lbank(base), bi=b1lap:
                               e.tensor_scalar(o, b, bi, 8.0, ALU.add, ALU.min), inc=True)
                    p.op("ve", lambda e, o=q1, i=q1, g2=sG:
                         e.scalar_tensor_tensor(o, i, -6.0, g2, ALU.max, ALU.mult))
                    if jj == 0:
                        p.wait("ve", t_w2rd[pr])
                    thw = p.op("ve", lambda e, o=hsl2(jj, pr), i=q1, c=cbtf2[pr]:
                               e.tensor_mul(o, i, c), inc=True)
                    t_hw_done[pr] = thw
                    b_set.release(g, tsS, tq1)
                    if pr == 1:
                        nxt = idx + 3
                        if nxt < NE * NFC:
                            p.wait("gp", tll)
                            t_w1l[nxt] = p.dma("gp", w1buf[nxt % 3], w1n[nxt // 8, nxt % 8], "s_w1")
                        t_w1cons[idx % 3] = tll
            t_crow_next = None
            if e_ < NE - 1:
                p.wait("gp", t_cbtmm_last)
                t_crow_next = p.dma("gp", crow[0:1, 0:HALF], cfm[e_ + 1:e_ + 2, :], "s_cr")
            for oc in range(NFC):
                idx2 = e_ * 8 + oc
                for pr in range(2):
                    g = gctr
                    gctr += 1
                    base = (g % 2) * 4
                    b_set.acquire(p, "pe", g)
                    if pr == 0:
                        p.wait("pe", t_w2l[idx2])
                    if oc == 0:
                        p.wait("pe", t_hw_done[pr])
                    tww = None
                    for fc2 in range(NFC):
                        for t2 in range(2):
                            tww = p.op("pe", lambda e, o=bank(base + t2), l=w2r[idx2 % 2][:, fc2, :],
                                       r=hsl(fc2, pr * 2 + t2), st=(fc2 == 0), sp=(fc2 == NFC - 1):
                                       e.matmul(o, l, r, start=st, stop=sp), inc=(fc2 == NFC - 1 and t2 == 1))
                    rel2 = []
                    if e_ < NE - 1 and pr == 1 and oc < 4:
                        p.wait("pe", t_crow_next)
                        t_cbtmm_last = p.op("pe", lambda e, o=bank(base + 2), r=crow[0:1, oc * 512:(oc + 1) * 512]:
                                            e.matmul(o, onesrowb[:], r, start=True, stop=True), inc=True)
                        p.wait("ve", t_cbtmm_last)
                        tcc = p.op("ve", lambda e, o=cbtf[oc], b=bank(base + 2): e.tensor_copy(o, b), inc=True)
                        rel2.append(tcc)
                    p.wait("ve", tww)
                    ta = p.op("ve", lambda e, o=acc(oc, pr * 2 * TG, 2 * TG), b=gbank(base):
                              e.tensor_add(o, o, b), inc=True)
                    rel2.append(ta)
                    if oc == NFC - 1:
                        t_w2rd[pr] = tww
                    b_set.release(g, *rel2)
                    if pr == 1:
                        nxt2 = idx2 + 2
                        if nxt2 < NE * NFC:
                            p.wait("gp", tww)
                            t_w2l[nxt2] = p.dma("gp", w2buf[nxt2 % 2], w2n[nxt2 // 8, nxt2 % 8], "s_w2")
                        t_w2cons[idx2 % 2] = tww
        p.waits("sy", p.now("ve"))
        t_out_prev = p.dma("sy", out_r[:, :, hlo:hlo + HALF],
                           A1.ap().rearrange("p (c t) -> p c t", c=NFC), "s_out")
    _final(p)
    return _wire(nc, p)


def _wire(nc, p):
    with nc.Block() as block:
        @block.tensor
        def _(t):
            for f in p.st["pe"]:
                f(t)

        @block.vector
        def _(v):
            for f in p.st["ve"]:
                f(v)

        @block.scalar
        def _(s2):
            for f in p.st["ae"]:
                f(s2)

        @block.sync
        def _(s2):
            for f in p.st["sy"]:
                f(s2)

        @block.gpsimd
        def _(g):
            for f in p.st["gp"]:
                f(g)
    return nc, p


_CACHE = {}


def _w1_layout(w1, perm):
    A = w1[:, perm, :].reshape(NE, 16, 128, NFC, 128)
    out = np.empty((NE, 8, 128, NFC, 256), np.float32)
    out[:, :, :, :, 0:128] = A[:, 0:8].transpose(0, 1, 4, 3, 2)
    out[:, :, :, :, 128:256] = A[:, 8:16].transpose(0, 1, 4, 3, 2)
    return np.ascontiguousarray(out.reshape(NE, 8, 128, 2048))


def _w2_layout(w2):
    B = w2.reshape(NE, 8, 128, NFC, 128)
    return np.ascontiguousarray(B.transpose(0, 1, 4, 3, 2).reshape(NE, 8, 128, 1024))


def _host_prep(inputs):
    x = np.asarray(inputs["L0_attn_x"], np.float32)
    Wq = np.asarray(inputs["L0_attn_Wq"], np.float32)
    Wk = np.asarray(inputs["L0_attn_Wk"], np.float32)
    Wv = np.asarray(inputs["L0_attn_Wv"], np.float32)
    Wo = np.asarray(inputs["L0_attn_Wo"], np.float32)
    nscale = np.asarray(inputs["L0_ffn_norm_scale"], np.float32)
    gate_w = np.asarray(inputs["L0_ffn_gate_w"], np.float32)
    gate_b = np.asarray(inputs["L0_ffn_gate_b"], np.float32)
    w1 = np.asarray(inputs["L0_ffn_w1"], np.float32)
    b1 = np.asarray(inputs["L0_ffn_b1"], np.float32)
    w2 = np.asarray(inputs["L0_ffn_w2"], np.float32)
    b2 = np.asarray(inputs["L0_ffn_b2"], np.float32)

    perm = np.concatenate([np.arange(0, 2048, 2), np.arange(1, 2048, 2)])
    com = {
        "wqT": np.ascontiguousarray(Wq.T),
        "wkT": np.ascontiguousarray(Wk.T),
        "wvT": np.ascontiguousarray(Wv.T),
        "woT": np.ascontiguousarray(Wo.T),
        "gwT": np.ascontiguousarray(gate_w.T),
        "gbb": np.ascontiguousarray(np.broadcast_to(gate_b, (128, NE))),
        "w1n": _w1_layout(w1, perm),
        "b1pp": np.ascontiguousarray(
            b1[:, perm].reshape(NE, 16, 128).transpose(2, 0, 1).reshape(128, NE * 16)),
        "w2n": _w2_layout(w2),
        "b2": np.ascontiguousarray(b2),
        "nsc": np.ascontiguousarray(nscale.reshape(NFC, 128).T),
    }
    inv = 1.0 / (10000.0 ** (np.arange(0, HD, 2, dtype=np.float64) / HD))
    pos = np.arange(S, dtype=np.float64)
    fr = np.outer(pos, inv)
    emb = np.concatenate([fr, fr], -1)
    cos1 = np.cos(emb).astype(np.float32).T
    sin1 = np.sin(emb).astype(np.float32).T
    com["cosb"] = np.ascontiguousarray(np.vstack([cos1, cos1]))
    com["sinb"] = np.ascontiguousarray(np.vstack([sin1, sin1]))
    mk = np.zeros((128, 4 * TG), np.float32)
    kk = np.arange(128)[:, None]
    qq = np.arange(TG)[None, :]
    for r in range(4):
        mk[:, r * TG:(r + 1) * TG] = np.where(qq >= kk + 128 * r, 0.0, -1e30)
    com["maskb"] = mk
    rt = np.zeros((128, 128), np.float32)
    for base in (0, 64):
        for d in range(32):
            rt[base + d + 32, base + d] = -1.0
            rt[base + d, base + d + 32] = 1.0
    com["rT"] = rt
    com["ident"] = np.eye(128, dtype=np.float32)
    oc3 = np.zeros((128, 3), np.float32)
    oc3[:, 0] = 1.0
    oc3[0:64, 1] = 1.0
    oc3[64:128, 2] = 1.0
    com["onescol"] = oc3
    com["onescolb"] = oc3
    com["onesrow"] = np.ones((1, 128), np.float32)
    com["onesrowb"] = np.ones((1, 128), np.float32)
    cv = np.zeros((128, 4), np.float32)
    cv[:, 1] = NEPS
    com["cv"] = cv

    in_maps = []
    for b in range(x.shape[0]):
        m = dict(com)
        m["xT"] = np.ascontiguousarray(x[b].T)
        in_maps.append(m)
    return in_maps


def kernel(**inputs):
    in_maps = _host_prep(inputs)
    if "nc" not in _CACHE:
        _CACHE["nc"], _ = build_program()
    res = run_bass_kernel_spmd(_CACHE["nc"], in_maps, core_ids=list(range(8)))
    outs = [np.ascontiguousarray(np.asarray(r["outT"]).T) for r in res.results]
    return np.stack(outs, axis=0)


# version 08
# speedup vs baseline: 1.0438x; 1.0438x over previous
import numpy as np
import ml_dtypes
import concourse.bass as bass
import concourse.mybir as mybir
from concourse.bass_utils import run_bass_kernel_spmd

F32 = mybir.dt.float32
BF16 = mybir.dt.bfloat16
AF = mybir.ActivationFunctionType
ALU = mybir.AluOpType
AX = mybir.AxisListType

S = 4096
H = 1024
NH, HD = 16, 64
NE = 32
TG = 512
NTG = S // TG
NFC = H // 128
HALF = S // 2
NEPS = 1e-5
SIGC = float(1.0 / (1.0 + np.exp(-1.702 * 7.0)))

ENG = ("pe", "ve", "ae", "sy", "gp")
_NE_RUN = [NE]
_FLAGS = {"pool": True, "cbtw2": True}


class Prog:
    def __init__(self, nc):
        self.nc = nc
        self.st = {e: [] for e in ENG}
        self.sems = {}
        self.cnt = {}

    def sem(self, name):
        if name not in self.sems:
            self.sems[name] = self.nc.alloc_semaphore(name)
            self.cnt[name] = 0
        return self.sems[name]

    def em(self, eng, f):
        self.st[eng].append(f)

    def wait(self, eng, ticket):
        if ticket is None:
            return
        name, val = ticket
        if val <= 0:
            return
        s = self.sem(name)
        self.em(eng, lambda e, s=s, v=val: e.wait_ge(s, v))

    def waits(self, eng, tickets):
        for t in tickets:
            self.wait(eng, t)

    def dma(self, q, out_ap, in_ap, semname):
        s = self.sem(semname)
        self.cnt[semname] += 16
        v = self.cnt[semname]
        self.em(q, lambda e, o=out_ap, i=in_ap, s=s: e.dma_start(out=o, in_=i).then_inc(s, 16))
        return (semname, v)

    def op(self, eng, fn, inc=False):
        if not inc:
            self.em(eng, lambda e, f=fn: f(e))
            return None
        s = self.sem(eng)
        self.cnt[eng] += 1
        v = self.cnt[eng]
        self.em(eng, lambda e, f=fn, s=s: f(e).then_inc(s, 1))
        return (eng, v)

    def now(self, *sems):
        return [(s, self.cnt.get(s, 0)) for s in (sems or self.cnt.keys())]


class Buf:
    def __init__(self, n):
        self.n = n
        self.free = [[] for _ in range(n)]

    def acquire(self, p, eng, i):
        for t in self.free[i % self.n]:
            p.wait(eng, t)
        self.free[i % self.n] = []

    def release(self, i, *tickets):
        self.free[i % self.n] += [t for t in tickets if t is not None]


def _final(p):
    for name, v in list(p.cnt.items()):
        if name in ENG:
            continue
        p.wait("sy", (name, v))


def build_program(debug_stage=0):
    nc = bass.Bass("TRN2", target_bir_lowering=False)
    p = Prog(nc)

    api = lambda name, shape, dt: nc.dram_tensor(name, shape, dt, kind="ExternalInput").ap()
    xT = api("xT", [H, S], F32)
    wqT = api("wqT", [H, H], F32)
    wkT = api("wkT", [H, H], F32)
    wvT = api("wvT", [H, H], F32)
    woT = api("woT", [H, H], F32)
    gwT = api("gwT", [H, NE], F32)
    gbb_h = api("gbb", [128, NE], F32)
    w1n = api("w1n", [NE, 8, 128, 2048], F32)
    b1pp_h = api("b1pp", [128, NE * 16], F32)
    w2n = api("w2n", [NE, 8, 128, 1024], F32)
    b2_h = api("b2", [NE, H], F32)
    nsc_h = api("nsc", [128, NFC], F32)
    cos_h = api("cosb", [128, S], F32)
    sin_h = api("sinb", [128, S], F32)
    mask_h = api("maskb", [128, 4 * 1024], F32)
    rT_h = api("rT", [128, 128], F32)
    ident_h = api("ident", [128, 128], F32)
    onescol_h = api("onescol", [128, 3], F32)
    onescolb_h = api("onescolb", [128, 3], F32)
    onesrow_h = api("onesrow", [1, 128], F32)
    onesrowb_h = api("onesrowb", [1, 128], F32)
    cv_h = api("cv", [128, 4], F32)

    outT = nc.dram_tensor("outT", [H, S], F32, kind="ExternalOutput").ap()

    def scratch(name, shape, dt, stage):
        kind = "ExternalOutput" if (0 < debug_stage == stage) else "Internal"
        return nc.dram_tensor(name, shape, dt, kind=kind).ap()
    q_s = scratch("q_s", [NH // 2, 128, S], BF16, 1)
    k_s = scratch("k_s", [NH // 2, 128, S], BF16, 1)
    v_s = scratch("v_s", [S, H], BF16, 1)
    o_s = scratch("o_s", [NH // 2, 128, S], BF16, 2)
    x2_s = scratch("x2_s", [H, S], F32, 3)

    A1 = nc.alloc_sbuf_tensor("A1", [128, 16384], F32)
    A2 = nc.alloc_sbuf_tensor("A2", [128, 12320], F32)
    A3 = nc.alloc_sbuf_tensor("A3", [128, 8192], F32)
    A4 = nc.alloc_sbuf_tensor("A4", [128, 8192], F32)
    A1b, A2b, A3b, A4b = (a.bitcast(BF16) for a in (A1, A2, A3, A4))
    masks2 = A3[:, 3072:7168]
    nstage = A1[0:1, 4608:6656]
    cvt = nc.alloc_sbuf_tensor("cvt_sb", [128, 4], F32)
    nsc = nc.alloc_sbuf_tensor("nsc_sb", [128, NFC], F32)
    gbb = nc.alloc_sbuf_tensor("gbb_sb", [128, NE], F32)
    b1pp = nc.alloc_sbuf_tensor("b1pp_sb", [128, NE * 16], F32)
    b1sig = nc.alloc_sbuf_tensor("b1sig_sb", [128, NE * 8], F32)
    b1lin1 = nc.alloc_sbuf_tensor("b1lin1_sb", [128, NE * 8], F32)
    b2bf = nc.alloc_sbuf_tensor("b2bf_sb", [NE, H], BF16)
    gw_sb = nc.alloc_sbuf_tensor("gw_sb", [128, NFC * NE], BF16)
    rT = nc.alloc_sbuf_tensor("rT_sb", [128, 128], BF16)
    ident = nc.alloc_sbuf_tensor("ident_sb", [128, 128], F32)
    onescol = nc.alloc_sbuf_tensor("onescol_sb", [128, 3], F32)
    onescolb = nc.alloc_sbuf_tensor("onescolb_sb", [128, 3], BF16)
    onesrow = nc.alloc_sbuf_tensor("onesrow_sb", [1, 128], F32)
    onesrowb = nc.alloc_sbuf_tensor("onesrowb_sb", [1, 128], BF16)
    rows2 = nc.alloc_sbuf_tensor("rows2", [1, 3072], F32)
    nb_sb = nc.alloc_sbuf_tensor("nb_sb", [128, 16], F32)
    cfm = nc.alloc_sbuf_tensor("cfm", [NE, HALF], BF16)
    crow = nc.alloc_sbuf_tensor("crow", [1, HALF], BF16)
    g_sb = nc.alloc_sbuf_tensor("g_sb", [128, NE], F32)
    m8 = nc.alloc_sbuf_tensor("m8", [128, 8], F32)
    top8 = nc.alloc_sbuf_tensor("top8", [128, 8], F32)
    gmod = nc.alloc_sbuf_tensor("gmod", [128, NE], F32)
    gmask = nc.alloc_sbuf_tensor("gmask", [128, NE], F32)
    negm1 = nc.alloc_sbuf_tensor("negm1", [128, 1], F32)
    srec = nc.alloc_sbuf_tensor("srec", [128, 1], F32)
    comb = nc.alloc_sbuf_tensor("comb", [128, NE], F32)
    nq_d = nc.dram_tensor("nq_d", [1, NH * S], F32, kind="Internal").ap()
    nk_d = nc.dram_tensor("nk_d", [1, NH * S], F32, kind="Internal").ap()
    rows_d = nc.dram_tensor("rows_d", [8, S], F32, kind="Internal").ap()

    nc.const_aps.register(F32, 0.0, cvt[:, 0:1])
    nc.const_aps.register(F32, NEPS, cvt[:, 1:2])

    PS = nc.alloc_psum_tensor("PS", [128, 8 * TG], F32)

    def bank(i, n=1, p0=0, p1=128):
        return PS[p0:p1, i * TG:(i + n) * TG]

    (R_A2, R_B2, R_A3, R_B3) = range(4)

    for dst, src in ((cvt, cv_h), (nsc, nsc_h), (gbb, gbb_h), (b1pp, b1pp_h),
                     (ident, ident_h), (onescol, onescol_h),
                     (onesrow, onesrow_h), (masks2, mask_h)):
        p.dma("sy", dst[:], src, "s_init")
    for dst, src in ((rT, rT_h), (onescolb, onescolb_h), (onesrowb, onesrowb_h)):
        p.dma("gp", dst[:], src, "s_initg")
    p.dma("gp", gw_sb.ap().rearrange("p (c n) -> p c n", c=NFC),
          gwT.rearrange("(c p) n -> p c n", p=128), "s_initg")
    p.dma("gp", b2bf[:], b2_h, "s_initg")
    T_INIT = [("s_init", p.cnt["s_init"]), ("s_initg", p.cnt["s_initg"])]

    b1r = b1pp.ap().rearrange("p (e c) -> p e c", e=NE)
    p.waits("ve", T_INIT)
    p.op("ve", lambda e: e.tensor_scalar(
        b1sig.ap().rearrange("p (e c) -> p e c", e=NE), b1r[:, :, 0:8], 1.702, None, ALU.mult))
    p.op("ve", lambda e: e.tensor_scalar(
        b1lin1.ap().rearrange("p (e c) -> p e c", e=NE), b1r[:, :, 8:16], 1.0, None, ALU.add), inc=True)

    ln1c = lambda fc, lo=0, n=TG: A1b[:, fc * 512 + lo: fc * 512 + lo + n]
    xcell = [A1[:, 2048 + i * 512: 2048 + (i + 1) * 512] for i in range(3)]
    sqcell = [A1[:, 3584 + i * 512: 3584 + (i + 1) * 512] for i in range(2)]
    vw = [A2b[:, i * 4096:(i + 1) * 4096] for i in range(2)]
    wqk = [A2b[:, 8192 + i * 1024: 8192 + (i + 1) * 1024] for i in range(2)]
    cosc = [A2b[:, 10240 + i * 512: 10240 + (i + 1) * 512] for i in range(2)]
    sinc = [A2b[:, 11264 + i * 512: 11264 + (i + 1) * 512] for i in range(2)]
    vcell = [A2b[:, 12288 + i * 512: 12288 + (i + 1) * 512] for i in range(4)]
    qraw = [A2b[:, 14336 + i * 512: 14336 + (i + 1) * 512] for i in range(2)]
    qrope = [A2b[:, 15360 + i * 512: 15360 + (i + 1) * 512] for i in range(2)]
    sqro = [A2b[:, 16384 + i * 512: 16384 + (i + 1) * 512] for i in range(2)]
    qcc = [A2[:, 8704 + i * 512: 8704 + (i + 1) * 512] for i in range(2)]
    rsc = [A2[:, 9728 + i * 512: 9728 + (i + 1) * 512] for i in range(2)]
    cbA = A2[:, 10752:11264]
    cbB = A2[:, 11264:11776]

    xT_r = xT.rearrange("(c p) t -> c p t", p=128)
    wq_r = wqT.rearrange("(c p) n -> p c n", p=128)
    wk_r = wkT.rearrange("(c p) n -> p c n", p=128)
    wv_r = wvT.rearrange("(c p) n -> p c n", p=128)

    bx, bsq = Buf(3), Buf(2)
    bwqk, bvw, bvc = Buf(2), Buf(2), Buf(4)
    bqraw, bqc, bqro, bsqro, bcs = Buf(2), Buf(2), Buf(2), Buf(2), Buf(2)
    b_proj, b_rot, b_stat, b_vps = Buf(2), Buf(2), Buf(1), Buf(2)
    b_ln1, b_cb = Buf(1), Buf(1)
    bnst = Buf(2)

    p.waits("pe", T_INIT)
    p.waits("ae", T_INIT)
    xi = 0
    for tg in range(NTG):
        lo = tg * TG
        cs = tg % 2
        bcs.acquire(p, "gp", cs)
        t_c1 = p.dma("gp", cosc[cs], cos_h[:, lo:lo + TG], f"s_cs{cs}")
        t_c2 = p.dma("gp", sinc[cs], sin_h[:, lo:lo + TG], f"s_cs{cs}")
        b_stat.acquire(p, "pe", 0)
        for fc in range(NFC):
            sl = xi % 3
            xi += 1
            bx.acquire(p, "sy", sl)
            tx = p.dma("sy", xcell[sl], xT_r[fc, :, lo:lo + TG], f"s_x{sl}")
            ss = fc % 2
            bsq.acquire(p, "ae", ss)
            p.wait("ae", tx)
            tq = p.op("ae", lambda e, o=sqcell[ss], i=xcell[sl]: e.activation(o, i, AF.Square), inc=True)
            p.wait("pe", tx)
            p.op("pe", lambda e, i=xcell[sl], st=(fc == 0), sp=(fc == NFC - 1):
                 e.matmul(bank(2, p1=1), onescol[:, 0:1], i, start=st, stop=sp))
            p.wait("pe", tq)
            tmm = p.op("pe", lambda e, i=sqcell[ss], st=(fc == 0), sp=(fc == NFC - 1):
                       e.matmul(bank(3, p1=1), onescol[:, 0:1], i, start=st, stop=sp), inc=True)
            bx.release(sl, tmm)
            bsq.release(ss, tmm)
        p.wait("ve", tmm)
        p.op("ve", lambda e: e.tensor_scalar(rows2[0:1, 0:512], bank(2, p1=1), 1.0 / H, None, ALU.mult))
        p.op("ve", lambda e: e.tensor_scalar(rows2[0:1, 512:1024], bank(3, p1=1), 1.0 / H, None, ALU.mult))
        p.op("ve", lambda e: e.tensor_mul(rows2[0:1, 1024:1536], rows2[0:1, 0:512], rows2[0:1, 0:512]))
        tvar = p.op("ve", lambda e: e.tensor_sub(rows2[0:1, 1536:2048], rows2[0:1, 512:1024], rows2[0:1, 1024:1536]), inc=True)
        b_stat.release(0, tvar)
        p.wait("ae", tvar)
        p.op("ae", lambda e: e.activation(rows2[0:1, 1024:1536], rows2[0:1, 1536:2048], AF.Ln, bias=NEPS))
        trs = p.op("ae", lambda e: e.activation(rows2[0:1, 2048:2560], rows2[0:1, 1024:1536], AF.Exp, scale=-0.5), inc=True)
        p.wait("ve", trs)
        tbr = p.op("ve", lambda e:
                   e.scalar_tensor_tensor(rows2[0:1, 2560:3072], rows2[0:1, 0:512], -1.0, rows2[0:1, 2048:2560], ALU.mult, ALU.mult), inc=True)
        b_stat.acquire(p, "pe", 0)
        p.wait("pe", tbr)
        p.op("pe", lambda e: e.matmul(bank(2), onesrow[:], rows2[0:1, 2048:2560], start=True, stop=True))
        tbc = p.op("pe", lambda e: e.matmul(bank(3), onesrow[:], rows2[0:1, 2560:3072], start=True, stop=True), inc=True)
        b_cb.acquire(p, "ve", 0)
        p.wait("ve", tbc)
        p.op("ve", lambda e: e.tensor_copy(cbA, bank(2)))
        tcb = p.op("ve", lambda e: e.tensor_copy(cbB, bank(3)), inc=True)
        b_stat.release(0, tcb)
        b_ln1.acquire(p, "ve", 0)
        for fc in range(NFC):
            sl = xi % 3
            xi += 1
            bx.acquire(p, "sy", sl)
            tx = p.dma("sy", xcell[sl], xT_r[fc, :, lo:lo + TG], f"s_x{sl}")
            p.wait("ve", tx)
            p.op("ve", lambda e, i=xcell[sl], o=rsc[fc % 2]: e.tensor_mul(o, i, cbA))
            tl = p.op("ve", lambda e, o=ln1c(fc), i=rsc[fc % 2]: e.tensor_add(o, i, cbB), inc=True)
            bx.release(sl, tl)
        t_ln1 = tl
        b_cb.release(0, t_ln1)

        for og in range(2):
            ws = og % 2
            bvw.acquire(p, "gp", ws)
            tw = p.dma("gp", vw[ws].rearrange("p (c n) -> p c n", c=NFC),
                       wv_r[:, :, og * 512:(og + 1) * 512], f"s_vw{ws}")
            p.wait("pe", tw)
            p.wait("pe", t_ln1)
            for tt in range(4):
                vb = tt % 2
                b_vps.acquire(p, "pe", vb)
                for fc in range(NFC):
                    tpm = p.op("pe", lambda e, l=ln1c(fc, tt * 128, 128),
                               r=vw[ws].rearrange("p (c n) -> p c n", c=NFC)[:, fc, :],
                               o=bank(4 + vb), st=(fc == 0), sp=(fc == NFC - 1):
                               e.matmul(o, l, r, start=st, stop=sp), inc=(fc == NFC - 1))
                vc = (og * 4 + tt) % 4
                bvc.acquire(p, "ve", vc)
                p.wait("ve", tpm)
                tve = p.op("ve", lambda e, o=vcell[vc], b=bank(4 + vb): e.tensor_copy(o, b), inc=True)
                b_vps.release(vb, tve)
                p.wait("sy", tve)
                tvd = p.dma("sy", v_s[lo + tt * 128: lo + (tt + 1) * 128, og * 512:(og + 1) * 512],
                            vcell[vc], f"s_vs{vc}")
                bvc.release(vc, tvd)
            bvw.release(ws, tpm)

        for wi, (wh_r, dst, ndst) in enumerate(((wq_r, q_s, nq_d), (wk_r, k_s, nk_d))):
            for oc in range(NFC):
                ws = oc % 2
                bwqk.acquire(p, "gp", ws)
                tw = p.dma("gp", wqk[ws].rearrange("p (c n) -> p c n", c=NFC),
                           wh_r[:, :, oc * 128:(oc + 1) * 128], f"s_w{ws}")
                pb = oc % 2
                b_proj.acquire(p, "pe", pb)
                p.wait("pe", tw)
                if oc == 0 and wi == 0:
                    p.wait("pe", t_ln1)
                for fc in range(NFC):
                    tpm = p.op("pe", lambda e, l=wqk[ws].rearrange("p (c n) -> p c n", c=NFC)[:, fc, :],
                               r=ln1c(fc), o=bank(pb), st=(fc == 0), sp=(fc == NFC - 1):
                               e.matmul(o, l, r, start=st, stop=sp), inc=(fc == NFC - 1))
                bwqk.release(ws, tpm)
                qs2 = oc % 2
                bqraw.acquire(p, "ae", qs2)
                p.wait("ae", tpm)
                tqr = p.op("ae", lambda e, o=qraw[qs2], b=bank(pb): e.copy(o, b), inc=True)
                bqc.acquire(p, "ve", qs2)
                p.wait("ve", tqr)
                if oc < 2 and wi == 0:
                    p.waits("ve", [t_c1, t_c2])
                tqc = p.op("ve", lambda e, o=qcc[qs2], b=bank(pb), c=cosc[cs]: e.tensor_mul(o, b, c), inc=True)
                b_proj.release(pb, tqc)
                rb_ = oc % 2
                b_rot.acquire(p, "pe", rb_)
                p.wait("pe", tqc)
                trm = p.op("pe", lambda e, o=bank(6 + rb_), i=qraw[qs2]:
                           e.matmul(o, rT[:], i, start=True, stop=True), inc=True)
                bqraw.release(qs2, trm)
                bqro.acquire(p, "ve", qs2)
                p.wait("ve", trm)
                p.op("ve", lambda e, o=rsc[qs2], b=bank(6 + rb_), s2=sinc[cs]: e.tensor_mul(o, b, s2))
                tqo = p.op("ve", lambda e, o=qrope[qs2], a=qcc[qs2], b=rsc[qs2]: e.tensor_add(o, a, b), inc=True)
                b_rot.release(rb_, tqo)
                bqc.release(qs2, tqo)
                bsqro.acquire(p, "ae", qs2)
                p.wait("ae", tqo)
                tq2 = p.op("ae", lambda e, o=sqro[qs2], i=qrope[qs2]: e.activation(o, i, AF.Square), inc=True)
                b_stat.acquire(p, "pe", 0)
                p.wait("pe", tq2)
                p.op("pe", lambda e, i=sqro[qs2]: e.matmul(bank(2, p1=1), onescolb[:, 1:2], i, start=True, stop=True))
                tnm = p.op("pe", lambda e, i=sqro[qs2]: e.matmul(bank(3, p1=1), onescolb[:, 2:3], i, start=True, stop=True), inc=True)
                bsqro.release(qs2, tnm)
                p.wait("ve", tnm)
                c0 = (2 * oc) * S + lo
                c1 = (2 * oc + 1) * S + lo
                bnst.acquire(p, "ve", qs2)
                p.op("ve", lambda e, o=nstage[0:1, qs2 * 1024: qs2 * 1024 + 512]: e.tensor_copy(o, bank(2, p1=1)))
                tne = p.op("ve", lambda e, o=nstage[0:1, qs2 * 1024 + 512: qs2 * 1024 + 1024]: e.tensor_copy(o, bank(3, p1=1)), inc=True)
                b_stat.release(0, tne)
                p.wait("sy", tne)
                p.dma("sy", ndst[0:1, c0:c0 + TG], nstage[0:1, qs2 * 1024: qs2 * 1024 + 512], f"s_nq{qs2}")
                tnd = p.dma("sy", ndst[0:1, c1:c1 + TG], nstage[0:1, qs2 * 1024 + 512: qs2 * 1024 + 1024], f"s_nq{qs2}")
                bnst.release(qs2, tnd)
                p.wait("sy", tqo)
                tqd = p.dma("sy", dst[oc, :, lo:lo + TG], qrope[qs2][:, :], f"s_qk{qs2}")
                bqro.release(qs2, tqd)
            if wi == 1:
                b_ln1.release(0, tpm)
        bcs.release(cs, tqo)

    if debug_stage == 1:
        _final(p)
        return _wire(nc, p)

    PH1 = p.now()

    qh2 = [A1b[:, i * 4096:(i + 1) * 4096] for i in range(2)]
    kh2 = [A1b[:, 8192 + i * 4096: 8192 + (i + 1) * 4096] for i in range(2)]
    vh2 = [[A1b[:, 16384 + (2 * i + j) * 2080: 16384 + (2 * i + j + 1) * 2080]
            for j in range(2)] for i in range(2)]
    pcell = [A1b[:, 24704 + i * 512: 24704 + (i + 1) * 512] for i in range(4)]
    pcp2 = [A1b[:, 24704 + i * 1024: 24704 + (i + 1) * 1024] for i in range(2)]
    osbc = [A1b[:, 26752 + i * 512: 26752 + (i + 1) * 512] for i in range(4)]
    nstage = [A2[0:1, i * 2048:(i + 1) * 2048] for i in range(2)]
    rqrow = A2[0:1, 4096:4128]
    rkrow = A2[0:1, 4128:4160]
    nbsc = A2[0:1, 4224:4288]
    zraw = A2[0:1, 4608:5632]
    zrecb = A2b[0:1, 11264:12288]
    zln = A2[0:1, 8704:9728]
    rbc = [A2[:, 6656 + i * 512: 6656 + (i + 1) * 512] for i in range(2)]
    msb = [A2[:, 7680 + i * 512: 7680 + (i + 1) * 512] for i in range(2)]
    msb2 = A2[:, 7680:8704]

    bqk, bvh = Buf(2), Buf(2)
    bpcp, bmsbU, bosb = Buf(2), Buf(1), Buf(2)
    bavz, bnst = Buf(2), Buf(2)
    sguard = {0: [], 1: [], 2: [], 3: []}
    rot = [0]

    def claim_slot(extra=()):
        r = rot[0] % 2
        rot[0] += 1
        bA, bB = 2 * r, 2 * r + 1
        p.waits("pe", sguard[bA] + sguard[bB] + [t for t in extra if t is not None])
        sguard[bA], sguard[bB] = [], []
        return bA, bB

    for eng in ENG:
        p.waits(eng, PH1)

    nsl = 0
    pend = [None]
    tzeh, trbh = [None], [None]
    trd = None
    for hh in range(NH):
        for src_d, rrow in ((nq_d, rqrow), (nk_d, rkrow)):
            for part in range(2):
                bnst.acquire(p, "ae", nsl)
                tld = p.dma("ae", nstage[nsl % 2],
                            src_d[0:1, hh * S + part * 2048: hh * S + (part + 1) * 2048], "s_nrm")
                p.wait("ve", tld)
                trd = p.op("ve", lambda e, o=rrow[0:1, hh * 2 + part: hh * 2 + part + 1],
                           i=nstage[nsl % 2]: e.tensor_reduce(o, i, AX.X, ALU.max), inc=True)
                bnst.release(nsl, trd)
                nsl += 1
    p.wait("ve", trd)
    tmx = None
    for hh in range(NH):
        p.op("ve", lambda e, o=nbsc[0:1, hh:hh + 1], a=rqrow[0:1, 2 * hh:2 * hh + 1],
             b=rqrow[0:1, 2 * hh + 1:2 * hh + 2]: e.tensor_max(o, a, b))
        tmx = p.op("ve", lambda e, o=nbsc[0:1, 16 + hh:17 + hh], a=rkrow[0:1, 2 * hh:2 * hh + 1],
                   b=rkrow[0:1, 2 * hh + 1:2 * hh + 2]: e.tensor_max(o, a, b), inc=(hh == NH - 1))
    p.wait("ve", tmx)
    tml = p.op("ve", lambda e: e.tensor_mul(nbsc[0:1, 32:48], nbsc[0:1, 0:16], nbsc[0:1, 16:32]), inc=True)
    p.wait("ve", tml)
    tpm = None
    for pj in range(NH // 2):
        tpm = p.op("ve", lambda e, o=nbsc[0:1, 48 + pj:49 + pj], a=nbsc[0:1, 32 + 2 * pj:33 + 2 * pj],
                   b=nbsc[0:1, 33 + 2 * pj:34 + 2 * pj]: e.tensor_max(o, a, b), inc=(pj == NH // 2 - 1))
    p.wait("ae", tpm)
    p.op("ae", lambda e: e.activation(nbsc[0:1, 56:64], nbsc[0:1, 48:56], AF.Ln))
    tsq = p.op("ae", lambda e: e.activation(nbsc[0:1, 32:40], nbsc[0:1, 56:64], AF.Exp, scale=0.5), inc=True)
    p.wait("ve", tsq)
    tnb = p.op("ve", lambda e, o=A2b[0:1, 8576:8584]:
               e.tensor_scalar(o, nbsc[0:1, 32:40], -0.125, None, ALU.mult), inc=True)
    rA, rB = claim_slot([tnb])
    tnm = p.op("pe", lambda e, o=bank(rA)[:, 0:8]: e.matmul(o, onesrowb[:], A2b[0:1, 8576:8584],
                                                            start=True, stop=True), inc=True)
    p.wait("ve", tnm)
    tnbc = p.op("ve", lambda e, o=nb_sb[:, 0:8]: e.tensor_copy(o, bank(rA)[:, 0:8]), inc=True)
    sguard[rA].append(tnbc)

    for pp in range(NH // 2):
        s = pp % 2
        bqk.acquire(p, "sy", s)
        tq = p.dma("sy", qh2[s], q_s[pp], f"s_qh{s}")
        tk = p.dma("sy", kh2[s], k_s[pp], f"s_qh{s}")
        bvh.acquire(p, "sy", s)
        vr = [vh2[s][j].rearrange("p (c d) -> p c d", d=65) for j in range(2)]
        tvh = None
        for j, hh in enumerate((2 * pp, 2 * pp + 1)):
            tvh = p.dma("sy", vr[j][:, :, 0:64],
                        v_s.rearrange("(c p) o -> p c o", p=128)[:, :, 64 * hh: 64 * hh + 64], f"s_vh{s}")
        p.wait("ve", tvh)
        tvm = None
        for j in range(2):
            tvm = p.op("ve", lambda e, o=vr[j][:, :, 64:65]: e.memset(o, 1.0), inc=(j == 1))

        nbc = nb_sb[:, pp:pp + 1]
        qk_last = av_last = None
        for qg in range(NTG):
            nkc = 4 * (qg + 1)
            qpar = qg % 2
            avA, avB = 4 + 2 * qpar, 5 + 2 * qpar
            kinfo = []

            def emit_av(kc, te, qg=qg, qpar=qpar, avA=avA, avB=avB, nkc=nkc, pp=pp, s=s):
                if kc == 0:
                    bavz.acquire(p, "pe", qpar)
                p.wait("pe", te)
                if qg == 0 and kc == 0:
                    p.waits("pe", [tvh, tvm])
                st_, sp_ = (kc == 0), (kc == nkc - 1)
                sp2 = kc % 2
                vrl = [vh2[s][j].rearrange("p (c d) -> p c d", d=65) for j in range(2)]
                p.op("pe", lambda e, o=PS[0:65, avA * TG:(avA + 1) * TG], l=vrl[0][:, kc, :],
                     r=pcell[2 * sp2], st=st_, sp=sp_: e.matmul(o, l, r, start=st, stop=sp))
                t_ = p.op("pe", lambda e, o=PS[0:65, avB * TG:(avB + 1) * TG], l=vrl[1][:, kc, :],
                          r=pcell[2 * sp2 + 1], st=st_, sp=sp_: e.matmul(o, l, r, start=st, stop=sp),
                          inc=True)
                bpcp.release(sp2, t_)
                return t_

            for kc in range(nkc):
                first = (qg == 0 and kc == 0)
                bA, bB = claim_slot([tq, tk] if first else ())
                p.op("pe", lambda e, o=bank(bA), l=kh2[s][0:64, kc * 128:(kc + 1) * 128],
                     r=qh2[s][0:64, qg * TG:(qg + 1) * TG]:
                     e.matmul(o, l, r, start=True, stop=True))
                qk_last = p.op("pe", lambda e, o=bank(bB), l=kh2[s][64:128, kc * 128:(kc + 1) * 128],
                               r=qh2[s][64:128, qg * TG:(qg + 1) * TG]:
                               e.matmul(o, l, r, start=True, stop=True), inc=True)
                sp2 = kc % 2
                bpcp.acquire(p, "ae", sp2)
                if kc >= 4 * qg:
                    r_ = kc - 4 * qg
                    p.wait("ve", qk_last)
                    tmB = p.op("ve", lambda e, o=PS[:, bA * TG:(bB + 1) * TG],
                               m=masks2[:, r_ * 1024:(r_ + 1) * 1024]:
                               e.tensor_add(o, o, m), inc=True)
                    p.wait("ae", tmB)
                    if first:
                        p.wait("ae", tnbc)
                else:
                    p.wait("ae", qk_last)
                te = p.op("ae", lambda e, o=pcp2[sp2], b=PS[:, bA * TG:(bB + 1) * TG], bi=nbc:
                          e.activation(o, b, AF.Exp, bias=bi, scale=0.125), inc=True)
                sguard[bA].append(te)
                sguard[bB].append(te)
                kinfo.append(te)
                if kc >= 1:
                    av_last = emit_av(kc - 1, kinfo[kc - 1])
                if kc == 3 and pend[0] is not None:
                    pend[0]()
                    pend[0] = None
            av_last = emit_av(nkc - 1, kinfo[nkc - 1])

            def make_tail(pp=pp, qg=qg, qpar=qpar, avA=avA, avB=avB, av_last=av_last):
                p.wait("ve", av_last)
                p.wait("ve", tzeh[0])
                p.op("ve", lambda e, o=zraw[0:1, 0:512], b=PS[64:65, avA * TG:(avA + 1) * TG]:
                     e.tensor_scalar(o, b, 1e-30, None, ALU.add))
                tz2 = p.op("ve", lambda e, o=zraw[0:1, 512:1024], b=PS[64:65, avB * TG:(avB + 1) * TG]:
                           e.tensor_scalar(o, b, 1e-30, None, ALU.add), inc=True)
                p.waits("ae", [tz2, trbh[0]])
                p.op("ae", lambda e: e.activation(zln[0:1, 0:1024], zraw[0:1, 0:1024], AF.Ln))
                tze = p.op("ae", lambda e: e.activation(zrecb[0:1, 0:1024], zln[0:1, 0:1024],
                                                        AF.Exp, scale=-1.0), inc=True)
                tzeh[0] = tze

                def emit_pe():
                    rA2, rB2 = claim_slot([tze])
                    p.op("pe", lambda e, o=PS[0:64, rA2 * TG:(rA2 + 1) * TG]:
                         e.matmul(o, onesrowb[0:1, 0:64], zrecb[0:1, 0:512], start=True, stop=True))
                    trb = p.op("pe", lambda e, o=PS[0:64, rB2 * TG:(rB2 + 1) * TG]:
                               e.matmul(o, onesrowb[0:1, 0:64], zrecb[0:1, 512:1024],
                                        start=True, stop=True), inc=True)
                    trbh[0] = trb
                    p.wait("ve", trb)
                    p.op("ve", lambda e, o=rbc[0][0:64, :]: e.tensor_copy(o, PS[0:64, rA2 * TG:(rA2 + 1) * TG]))
                    trbc = p.op("ve", lambda e, o=rbc[1][0:64, :]: e.tensor_copy(o, PS[0:64, rB2 * TG:(rB2 + 1) * TG]), inc=True)
                    sguard[rA2].append(trbc)
                    sguard[rB2].append(trbc)
                    bosb.acquire(p, "ve", qpar)
                    p.op("ve", lambda e, o=osbc[2 * qpar][0:64, :], b=PS[0:64, avA * TG:(avA + 1) * TG],
                         r=rbc[0][0:64, :]: e.tensor_mul(o, b, r))
                    tosb = p.op("ve", lambda e, o=osbc[2 * qpar + 1][0:64, :], b=PS[0:64, avB * TG:(avB + 1) * TG],
                                r=rbc[1][0:64, :]: e.tensor_mul(o, b, r), inc=True)
                    bavz.release(qpar, tosb, tz2)
                    p.wait("sy", tosb)
                    p.dma("sy", o_s[pp, 0:64, qg * TG:(qg + 1) * TG], osbc[2 * qpar][0:64, :], f"s_os{qpar}")
                    tod = p.dma("sy", o_s[pp, 64:128, qg * TG:(qg + 1) * TG], osbc[2 * qpar + 1][0:64, :], f"s_os{qpar}")
                    bosb.release(qpar, tod)
                return emit_pe

            pend[0] = make_tail()
        bqk.release(s, qk_last)
        bvh.release(s, av_last)
    if pend[0] is not None:
        pend[0]()
        pend[0] = None

    if debug_stage == 2:
        p.waits("sy", p.now("ve", "ae"))
        p.dma("sy", nc.dram_tensor("dbg_nb", [128, 16], F32, kind="ExternalOutput").ap(),
              nb_sb[:], "s_dbg")
        p.dma("sy", nc.dram_tensor("dbg_rows", [1, 256], F32, kind="ExternalOutput").ap(),
              A2[0:1, 4096:4352], "s_dbg")
        _final(p)
        return _wire(nc, p)
    PH2 = p.now()

    wo_sb = A2b[:, 0:8192]
    o_sb = [A2b[:, 8192 + i * 4096: 8192 + (i + 1) * 4096] for i in range(2)]
    x3cell = [A2[:, 8192 + i * 512: 8192 + (i + 1) * 512] for i in range(3)]
    x2cell = [A2[:, 9728 + i * 512: 9728 + (i + 1) * 512] for i in range(2)]
    sq2 = [A2[:, 10752 + i * 512: 10752 + (i + 1) * 512] for i in range(2)]

    for eng in ENG:
        p.waits(eng, PH2)
    t_wo = p.dma("gp", wo_sb.rearrange("p (c n) -> p c n", c=NFC),
                 woT.rearrange("(c p) n -> p c n", p=128), "s_wo")
    bos, bx3, bx2c, bsq2 = Buf(2), Buf(3), Buf(2), Buf(2)
    b_wops, b_st2 = Buf(2), Buf(1)
    wo_r = wo_sb.rearrange("p (c n) -> p c n", c=NFC)
    x2_r = x2_s.rearrange("(c p) t -> c p t", p=128)

    p.wait("pe", t_wo)
    xi = 0
    for tg in range(NTG):
        lo = tg * TG
        ts_ = tg % 2
        bos.acquire(p, "sy", ts_)
        tol = None
        for pp in range(NH // 2):
            dst = o_sb[ts_].rearrange("p (c n) -> p c n", c=NFC)[:, pp, :]
            tol = p.dma("sy", dst, o_s[pp, :, lo:lo + TG], f"s_ol{ts_}")
        b_st2.acquire(p, "pe", 0)
        tpm = None
        for oc in range(NFC):
            pb = oc % 2
            b_wops.acquire(p, "pe", pb)
            p.wait("pe", tol)
            for fc in range(NFC):
                tpm = p.op("pe", lambda e, o=bank(pb), l=wo_r[:, fc, oc * 128:(oc + 1) * 128],
                           r=o_sb[ts_].rearrange("p (c n) -> p c n", c=NFC)[:, fc, :],
                           st=(fc == 0), sp=(fc == NFC - 1):
                           e.matmul(o, l, r, start=st, stop=sp), inc=(fc == NFC - 1))
            sl = xi % 3
            xi += 1
            bx3.acquire(p, "sy", sl)
            tx = p.dma("sy", x3cell[sl], xT_r[oc, :, lo:lo + TG], f"s_x3{sl}")
            xs = oc % 2
            bx2c.acquire(p, "ve", xs)
            p.waits("ve", [tpm, tx])
            tx2 = p.op("ve", lambda e, o=x2cell[xs], b=bank(pb), x=x3cell[sl]: e.tensor_add(o, b, x), inc=True)
            b_wops.release(pb, tx2)
            bx3.release(sl, tx2)
            p.wait("sy", tx2)
            tx2d = p.dma("sy", x2_r[oc, :, lo:lo + TG], x2cell[xs], f"s_x2{xs}")
            ss = oc % 2
            bsq2.acquire(p, "ae", ss)
            p.wait("ae", tx2)
            tq2 = p.op("ae", lambda e, o=sq2[ss], i=x2cell[xs]: e.activation(o, i, AF.Square), inc=True)
            p.wait("pe", tx2)
            p.op("pe", lambda e, i=x2cell[xs], st=(oc == 0), sp=(oc == NFC - 1):
                 e.matmul(bank(4, p1=1), onescol[:, 0:1], i, start=st, stop=sp))
            p.wait("pe", tq2)
            tsm = p.op("pe", lambda e, i=sq2[ss], st=(oc == 0), sp=(oc == NFC - 1):
                       e.matmul(bank(5, p1=1), onescol[:, 0:1], i, start=st, stop=sp), inc=True)
            bx2c.release(xs, tsm, tx2d)
            bsq2.release(ss, tsm)
        bos.release(ts_, tpm)
        p.wait("ve", tsm)
        p.op("ve", lambda e: e.tensor_scalar(rows2[0:1, 0:512], bank(4, p1=1), 1.0 / H, None, ALU.mult))
        p.op("ve", lambda e: e.tensor_scalar(rows2[0:1, 512:1024], bank(5, p1=1), 1.0 / H, None, ALU.mult))
        p.op("ve", lambda e: e.tensor_mul(rows2[0:1, 1024:1536], rows2[0:1, 0:512], rows2[0:1, 0:512]))
        tvar = p.op("ve", lambda e: e.tensor_sub(rows2[0:1, 1536:2048], rows2[0:1, 512:1024], rows2[0:1, 1024:1536]), inc=True)
        b_st2.release(0, tvar)
        p.wait("ae", tvar)
        p.op("ae", lambda e: e.activation(rows2[0:1, 1024:1536], rows2[0:1, 1536:2048], AF.Ln, bias=NEPS))
        tr2 = p.op("ae", lambda e: e.activation(rows2[0:1, 2048:2560], rows2[0:1, 1024:1536], AF.Exp, scale=-0.5), inc=True)
        tu = p.op("ve", lambda e: e.tensor_scalar(rows2[0:1, 2560:3072], rows2[0:1, 1536:2048], NEPS, None, ALU.add), inc=True)
        p.wait("ve", tu)
        p.wait("ve", tr2)
        tur = p.op("ve", lambda e: e.reciprocal(rows2[0:1, 1024:1536], rows2[0:1, 2560:3072]), inc=True)
        p.wait("ve", tur)
        tq_ = p.op("ve", lambda e: e.tensor_mul(rows2[0:1, 2560:3072], rows2[0:1, 1536:2048], rows2[0:1, 1024:1536]), inc=True)
        p.wait("ae", tq_)
        p.op("ae", lambda e: e.activation(rows2[0:1, 1024:1536], rows2[0:1, 2560:3072], AF.Ln, bias=NEPS))
        trs2 = p.op("ae", lambda e: e.activation(rows2[0:1, 2560:3072], rows2[0:1, 1024:1536], AF.Exp, scale=-0.5), inc=True)
        p.waits("ve", [tr2, trs2])
        p.op("ve", lambda e: e.tensor_mul(rows2[0:1, 1024:1536], rows2[0:1, 2048:2560], rows2[0:1, 2560:3072]))
        p.op("ve", lambda e:
             e.scalar_tensor_tensor(rows2[0:1, 1536:2048], rows2[0:1, 0:512], -1.0, rows2[0:1, 1024:1536], ALU.mult, ALU.mult))
        p.op("ve", lambda e: e.tensor_scalar(rows2[0:1, 512:1024], rows2[0:1, 2048:2560], 1.0, None, ALU.add))
        trows = p.op("ve", lambda e:
                     e.scalar_tensor_tensor(rows2[0:1, 2560:3072], rows2[0:1, 0:512], -1.0, rows2[0:1, 2048:2560], ALU.mult, ALU.mult), inc=True)
        p.wait("sy", trows)
        for ri, sl_ in ((R_A2, (1024, 1536)), (R_B2, (1536, 2048)), (R_A3, (512, 1024)), (R_B3, (2560, 3072))):
            trd = p.dma("sy", rows_d[ri:ri + 1, lo:lo + TG], rows2[0:1, sl_[0]:sl_[1]], "s_rows")

    if debug_stage == 3:
        _final(p)
        return _wire(nc, p)
    PH3 = p.now()

    w1buf = [A2b[:, k * 2048:(k + 1) * 2048] for k in range(3)]
    w2buf = [A2b[:, 6144 + k * 1024: 6144 + (k + 1) * 1024] for k in range(2)]
    w1r = [w.rearrange("p (c n) -> p c n", c=NFC) for w in w1buf]
    w2r = [w.rearrange("p (c n) -> p c n", c=NFC) for w in w2buf]
    sSc = [A2[:, 4096 + i * 1024: 4096 + (i + 1) * 1024] for i in range(2)]
    sGc = [A2[:, 6144 + i * 1024: 6144 + (i + 1) * 1024] for i in range(2)]
    q1c = [A2[:, 8192 + i * 1024: 8192 + (i + 1) * 1024] for i in range(2)]
    scr = [A2[:, 4096 + i * 512: 4096 + (i + 1) * 512] for i in range(5)]
    bcA2 = A2[:, 6656:7168]
    bcB2 = A2[:, 7168:7680]
    bcA3 = A2[:, 7680:8192]
    bcB3 = A2[:, 8192:8704]
    cbtf = [A2[:, 10240 + i * 512: 10240 + (i + 1) * 512] for i in range(4)]
    cbtf2 = [A2[:, 10240 + i * 1024: 10240 + (i + 1) * 1024] for i in range(2)]
    gbank = lambda base: PS[:, base * TG:(base + 2) * TG]
    lbank = lambda base: PS[:, (base + 2) * TG:(base + 4) * TG]
    gw_r = gw_sb.ap().rearrange("p (c n) -> p c n", c=NFC)
    out_r = outT.rearrange("(c p) t -> p c t", p=128)

    for eng in ENG:
        p.waits(eng, PH3)

    t_out_prev = None
    t_w1cons = [None, None, None]
    t_w2cons = [None, None, None]
    last_p_ve = {0: None, 1: None}
    t_w2rd = {0: None, 1: None}
    t_cbtmm_last = None
    for half in range(2):
        hlo = half * HALF
        acc = lambda oc, t0=0, n=HALF: A1[:, oc * 2048 + t0: oc * 2048 + t0 + n]
        tsb = lambda fc, t0=0, n=TG: A3b[:, fc * 2048 + t0: fc * 2048 + t0 + n]
        hsl = lambda jj, ti: A4b[:, (jj * 4 + ti) * 512: (jj * 4 + ti + 1) * 512]
        hsl2 = lambda jj, pr: A4b[:, (jj * 4 + pr * 2) * 512: (jj * 4 + pr * 2 + 2) * 512]

        p.wait("sy", t_out_prev)
        p.waits("ve", p.now("gp", "ae"))
        t_x2l = None
        for oc in range(NFC):
            t_x2l = p.dma("sy", acc(oc), x2_r[oc, :, hlo:hlo + HALF], "s_x2l")
        b_bc = Buf(1)
        tb_prev = None
        for tgi in range(4):
            lo = hlo + tgi * TG
            b_bc.acquire(p, "pe", 0)
            p.waits("pe", p.now("ve", "ae"))
            p.waits("sy", [tb_prev])
            trl = None
            for bi, ri in enumerate((R_A2, R_B2, R_A3, R_B3)):
                trl = p.dma("sy", rows2[0:1, bi * 512:(bi + 1) * 512], rows_d[ri:ri + 1, lo:lo + TG], "s_rld")
            p.wait("pe", trl)
            tb = None
            for bi in range(4):
                tb = p.op("pe", lambda e, o=bank(bi), r=rows2[0:1, bi * 512:(bi + 1) * 512]:
                          e.matmul(o, onesrow[:], r, start=True, stop=True), inc=(bi == 3))
            tb_prev = tb
            p.wait("ve", tb)
            tbe = None
            for bi, dstc in enumerate((bcA2, bcB2, bcA3, bcB3)):
                tbe = p.op("ve", lambda e, o=dstc, b=bank(bi): e.tensor_copy(o, b), inc=(bi == 3))
            b_bc.release(0, tbe)
            p.wait("ve", t_x2l)
            tacc = None
            for oc in range(NFC):
                x2c = acc(oc, tgi * TG, TG)
                p.op("ve", lambda e, o=scr[0], i=x2c: e.tensor_mul(o, i, bcA2))
                p.op("ve", lambda e, o=scr[1], i=scr[0]: e.tensor_add(o, i, bcB2))
                p.op("ve", lambda e, o=tsb(oc, tgi * TG, TG), i=scr[1], s2=nsc[:, oc:oc + 1]:
                     e.tensor_scalar(o, i, s2, None, ALU.mult))
                p.op("ve", lambda e, o=scr[2], i=x2c: e.tensor_mul(o, i, bcA3))
                tacc = p.op("ve", lambda e, o=x2c, i=scr[2]: e.tensor_add(o, i, bcB3), inc=True)
        t_prep = tacc
        tbe_last = tbe
        last_p_ve = {0: t_prep, 1: t_prep}

        b_g, b_tr = Buf(1), Buf(1)
        t_cfm = None
        tgb = None
        for ti in range(16):
            tlo = ti * 128
            b_g.acquire(p, "pe", 0)
            p.wait("pe", t_prep)
            tgm = None
            for fc in range(NFC):
                tgm = p.op("pe", lambda e, o=bank(6)[:, 0:NE], l=tsb(fc, tlo, 128), r=gw_r[:, fc, :],
                           st=(fc == 0), sp=(fc == NFC - 1):
                           e.matmul(o, l, r, start=st, stop=sp), inc=(fc == NFC - 1))
            p.wait("ve", tgm)
            tgb = p.op("ve", lambda e: e.tensor_add(g_sb[:], bank(6)[:, 0:NE], gbb[:]), inc=True)
            b_g.release(0, tgb)
            p.wait("ve", tgb)
            tt8 = p.op("ve", lambda e: e.max(top8[:], g_sb[:]), inc=True)
            p.wait("ve", tt8)
            p.op("ve", lambda e: e.memset(m8[:], 1e30))
            p.op("ve", lambda e: e.tensor_copy(gmod[:], g_sb[:]))
            for r_ in range(4):
                tmv = p.op("ve", lambda e, i=r_: e.tensor_copy(m8[:, 0:1], top8[:, i:i + 1]), inc=True)
                p.wait("ve", tmv)
                tmr = p.op("ve", lambda e: e.match_replace(gmod[:], m8[:], gmod[:], -1e30), inc=True)
            p.wait("ve", tmr)
            tmk = p.op("ve", lambda e: e.tensor_scalar(gmask[:], gmod[:], -1e29, None, ALU.is_le), inc=True)
            tng = p.op("ve", lambda e: e.tensor_scalar(negm1[:], top8[:, 0:1], -1.0, None, ALU.mult), inc=True)
            p.waits("ae", [tng, tgb])
            tex = p.op("ae", lambda e: e.activation(comb[:], g_sb[:], AF.Exp, bias=negm1[:]), inc=True)
            p.waits("ve", [tex, tmk])
            tcu = p.op("ve", lambda e: e.tensor_mul(comb[:], comb[:], gmask[:]), inc=True)
            p.wait("ve", tcu)
            tsu = p.op("ve", lambda e: e.reduce_sum(srec[:], comb[:], AX.X), inc=True)
            p.wait("ve", tsu)
            trc = p.op("ve", lambda e: e.reciprocal(srec[:], srec[:]), inc=True)
            p.wait("ve", trc)
            tcf = p.op("ve", lambda e: e.tensor_scalar(comb[:], comb[:], srec[:], None, ALU.mult), inc=True)
            b_tr.acquire(p, "pe", 0)
            p.wait("pe", tcf)
            ttr = p.op("pe", lambda e: e.transpose(PS[0:NE, 7 * TG: 7 * TG + 128], comb[:], ident[:]), inc=True)
            p.wait("ve", ttr)
            t_cfm = p.op("ve", lambda e, o=cfm[0:NE, tlo:tlo + 128]: e.tensor_copy(o, PS[0:NE, 7 * TG: 7 * TG + 128]), inc=True)
            b_tr.release(0, t_cfm)

        p.waits("gp", [t_cfm, t_cbtmm_last])
        t_crow = p.dma("gp", crow[0:1, 0:HALF], cfm[0:1, :], "s_cr")
        p.waits("pe", [tgb, t_cfm, t_crow])
        tcbt0 = []
        for ti4 in range(4):
            t_cbtmm_last = p.op("pe", lambda e, o=bank(4 + ti4), r=crow[0:1, ti4 * 512:(ti4 + 1) * 512]:
                                e.matmul(o, onesrowb[:], r, start=True, stop=True), inc=True)
            p.wait("ve", t_cbtmm_last)
            tcc = p.op("ve", lambda e, o=cbtf[ti4], b=bank(4 + ti4): e.tensor_copy(o, b), inc=True)
            tcbt0.append(tcc)

        bb2 = Buf(2)
        bb2.release(0, tbe_last)
        bb2.release(1, tbe_last)
        ta2_pb = {}
        p.wait("pe", t_cfm)
        for oc in range(NFC):
            for tgi in range(4):
                pb = (oc * 4 + tgi) % 2
                bb2.acquire(p, "pe", pb)
                tb2 = p.op("pe", lambda e, o=bank(pb), l=b2bf[:, oc * 128:(oc + 1) * 128],
                           r=cfm[0:NE, tgi * TG:(tgi + 1) * TG]:
                           e.matmul(o, l, r, start=True, stop=True), inc=True)
                p.wait("ve", tb2)
                ta2 = p.op("ve", lambda e, o=acc(oc, tgi * TG, TG), b=bank(pb):
                           e.tensor_add(o, o, b), inc=True)
                bb2.release(pb, ta2)
                ta2_pb[pb] = ta2

        if debug_stage == 4 and half == 0:
            p.waits("sy", p.now("ve"))
            for nm, tn in (("dbg_g", g_sb), ("dbg_top8", top8), ("dbg_gmod", gmod),
                           ("dbg_gmask", gmask), ("dbg_comb", comb)):
                p.dma("sy", nc.dram_tensor(nm, list(tn.shape), F32, kind="ExternalOutput").ap(),
                      tn[:], "s_dbg")
            p.dma("sy", nc.dram_tensor("dbg_acc", [128, 16384], F32, kind="ExternalOutput").ap(),
                  A1.ap(), "s_dbg")
            p.dma("sy", nc.dram_tensor("dbg_t", [128, 16384], BF16, kind="ExternalOutput").ap(),
                  A3b[:, 0:16384], "s_dbg")
            p.dma("sy", nc.dram_tensor("dbg_cfm", [NE, HALF], BF16, kind="ExternalOutput").ap(),
                  cfm[:], "s_dbg")
            _final(p)
            return _wire(nc, p)

        b_set = Buf(2)
        b_set.release(0, ta2_pb.get(0), ta2_pb.get(1), tbe_last)
        b_set.release(1, *(tcbt0 + [tgb, t_cfm]))
        t_w1l, t_w2l = {}, {}
        for i in range(3):
            p.wait("gp", t_w1cons[i])
            t_w1l[i] = p.dma("gp", w1buf[i], w1n[0, i], "s_w1")
        for i in range(2):
            p.wait("gp", t_w2cons[i])
            t_w2l[i] = p.dma("gp", w2buf[i], w2n[0, i], "s_w2")
        gctr = 0
        t_hw_done = {0: None, 1: None}
        for e_ in range(_NE_RUN[0]):
            for jj in range(NFC):
                idx = e_ * 8 + jj
                for pr in range(2):
                    g = gctr
                    gctr += 1
                    gpar = g % 2
                    base = gpar * 4
                    b_set.acquire(p, "pe", g)
                    if pr == 0:
                        p.wait("pe", t_w1l[idx])
                        if idx == 0:
                            p.wait("pe", t_prep)
                    tgl = None
                    for t2 in range(2):
                        for fc in range(NFC):
                            tgl = p.op("pe", lambda e, o=bank(base + t2), l=w1r[idx % 3][:, fc, 0:128],
                                       r=tsb(fc, (pr * 2 + t2) * TG, TG), st=(fc == 0), sp=(fc == NFC - 1):
                                       e.matmul(o, l, r, start=st, stop=sp), inc=(fc == NFC - 1 and t2 == 1))
                    tll = None
                    for t2 in range(2):
                        for fc in range(NFC):
                            tll = p.op("pe", lambda e, o=bank(base + 2 + t2), l=w1r[idx % 3][:, fc, 128:256],
                                       r=tsb(fc, (pr * 2 + t2) * TG, TG), st=(fc == 0), sp=(fc == NFC - 1):
                                       e.matmul(o, l, r, start=st, stop=sp), inc=(fc == NFC - 1 and t2 == 1))
                    sS, sG, q1 = sSc[gpar], sGc[gpar], q1c[gpar]
                    bsap = b1sig[:, e_ * 8 + jj: e_ * 8 + jj + 1]
                    b1ap = b1pp[:, e_ * 16 + jj: e_ * 16 + jj + 1]
                    b1lap = b1lin1[:, e_ * 8 + jj: e_ * 8 + jj + 1]
                    p.wait("ae", tgl)
                    p.wait("ae", last_p_ve[gpar])
                    tsS = p.op("ae", lambda e, o=sS, b=gbank(base), bi=bsap:
                               e.activation(o, b, AF.Sigmoid, bias=bi, scale=1.702), inc=True)
                    p.wait("ve", tsS)
                    p.op("ve", lambda e, o=sG, b=gbank(base), bi=b1ap:
                         e.tensor_scalar(o, b, bi, 7.0, ALU.add, ALU.min))
                    tp = p.op("ve", lambda e, o=sG, i=sS, g2=sG:
                              e.scalar_tensor_tensor(o, i, SIGC, g2, ALU.min, ALU.mult), inc=True)
                    last_p_ve[gpar] = tp
                    p.wait("ve", tll)
                    tq1 = p.op("ve", lambda e, o=q1, b=lbank(base), bi=b1lap:
                               e.tensor_scalar(o, b, bi, 8.0, ALU.add, ALU.min), inc=True)
                    p.op("ve", lambda e, o=q1, i=q1, g2=sG:
                         e.scalar_tensor_tensor(o, i, -6.0, g2, ALU.max, ALU.mult))
                    if jj == 0:
                        p.wait("ve", t_w2rd[pr])
                    thw = p.op("ve", lambda e, o=hsl2(jj, pr), i=q1, c=cbtf2[pr]:
                               e.tensor_mul(o, i, c), inc=True)
                    t_hw_done[pr] = thw
                    b_set.release(g, tsS, tq1)
                    if pr == 1:
                        nxt = idx + 3
                        if nxt < NE * NFC:
                            p.wait("gp", tll)
                            t_w1l[nxt] = p.dma("gp", w1buf[nxt % 3], w1n[nxt // 8, nxt % 8], "s_w1")
                        t_w1cons[idx % 3] = tll
            t_crow_next = None
            if e_ < NE - 1:
                p.wait("gp", t_cbtmm_last)
                t_crow_next = p.dma("gp", crow[0:1, 0:HALF], cfm[e_ + 1:e_ + 2, :], "s_cr")
            for oc in range(NFC):
                idx2 = e_ * 8 + oc
                for pr in range(2):
                    g = gctr
                    gctr += 1
                    base = (g % 2) * 4
                    b_set.acquire(p, "pe", g)
                    if pr == 0:
                        p.wait("pe", t_w2l[idx2])
                    if oc == 0:
                        p.wait("pe", t_hw_done[pr])
                    tww = None
                    for fc2 in range(NFC):
                        for t2 in range(2):
                            tww = p.op("pe", lambda e, o=bank(base + t2), l=w2r[idx2 % 2][:, fc2, :],
                                       r=hsl(fc2, pr * 2 + t2), st=(fc2 == 0), sp=(fc2 == NFC - 1):
                                       e.matmul(o, l, r, start=st, stop=sp), inc=(fc2 == NFC - 1 and t2 == 1))
                    rel2 = []
                    if e_ < NE - 1 and pr == 1 and oc < 4:
                        p.wait("pe", t_crow_next)
                        t_cbtmm_last = p.op("pe", lambda e, o=bank(base + 2), r=crow[0:1, oc * 512:(oc + 1) * 512]:
                                            e.matmul(o, onesrowb[:], r, start=True, stop=True), inc=True)
                        p.wait("ve", t_cbtmm_last)
                        tcc = p.op("ve", lambda e, o=cbtf[oc], b=bank(base + 2): e.tensor_copy(o, b), inc=True)
                        rel2.append(tcc)
                    p.wait("ve", tww)
                    ta = p.op("ve", lambda e, o=acc(oc, pr * 2 * TG, 2 * TG), b=gbank(base):
                              e.tensor_add(o, o, b), inc=True)
                    rel2.append(ta)
                    if oc == NFC - 1:
                        t_w2rd[pr] = tww
                    b_set.release(g, *rel2)
                    if pr == 1:
                        nxt2 = idx2 + 2
                        if nxt2 < NE * NFC:
                            p.wait("gp", tww)
                            t_w2l[nxt2] = p.dma("gp", w2buf[nxt2 % 2], w2n[nxt2 // 8, nxt2 % 8], "s_w2")
                        t_w2cons[idx2 % 2] = tww
        p.waits("sy", p.now("ve"))
        t_out_prev = p.dma("sy", out_r[:, :, hlo:hlo + HALF],
                           A1.ap().rearrange("p (c t) -> p c t", c=NFC), "s_out")
    _final(p)
    return _wire(nc, p)


def _wire(nc, p):
    with nc.Block() as block:
        @block.tensor
        def _(t):
            for f in p.st["pe"]:
                f(t)

        @block.vector
        def _(v):
            for f in p.st["ve"]:
                f(v)

        @block.scalar
        def _(s2):
            for f in p.st["ae"]:
                f(s2)

        @block.sync
        def _(s2):
            for f in p.st["sy"]:
                f(s2)

        @block.gpsimd
        def _(g):
            for f in p.st["gp"]:
                f(g)
    return nc, p


_CACHE = {}


def _w1_layout(w1, perm):
    A = w1[:, perm, :].reshape(NE, 16, 128, NFC, 128)
    out = np.empty((NE, 8, 128, NFC, 256), np.float32)
    out[:, :, :, :, 0:128] = A[:, 0:8].transpose(0, 1, 4, 3, 2)
    out[:, :, :, :, 128:256] = A[:, 8:16].transpose(0, 1, 4, 3, 2)
    return np.ascontiguousarray(out.reshape(NE, 8, 128, 2048))


def _w2_layout(w2):
    B = w2.reshape(NE, 8, 128, NFC, 128)
    return np.ascontiguousarray(B.transpose(0, 1, 4, 3, 2).reshape(NE, 8, 128, 1024))


def _host_prep(inputs):
    x = np.asarray(inputs["L0_attn_x"], np.float32)
    Wq = np.asarray(inputs["L0_attn_Wq"], np.float32)
    Wk = np.asarray(inputs["L0_attn_Wk"], np.float32)
    Wv = np.asarray(inputs["L0_attn_Wv"], np.float32)
    Wo = np.asarray(inputs["L0_attn_Wo"], np.float32)
    nscale = np.asarray(inputs["L0_ffn_norm_scale"], np.float32)
    gate_w = np.asarray(inputs["L0_ffn_gate_w"], np.float32)
    gate_b = np.asarray(inputs["L0_ffn_gate_b"], np.float32)
    w1 = np.asarray(inputs["L0_ffn_w1"], np.float32)
    b1 = np.asarray(inputs["L0_ffn_b1"], np.float32)
    w2 = np.asarray(inputs["L0_ffn_w2"], np.float32)
    b2 = np.asarray(inputs["L0_ffn_b2"], np.float32)

    perm = np.concatenate([np.arange(0, 2048, 2), np.arange(1, 2048, 2)])
    com = {
        "wqT": np.ascontiguousarray(Wq.T),
        "wkT": np.ascontiguousarray(Wk.T),
        "wvT": np.ascontiguousarray(Wv.T),
        "woT": np.ascontiguousarray(Wo.T),
        "gwT": np.ascontiguousarray(gate_w.T),
        "gbb": np.ascontiguousarray(np.broadcast_to(gate_b, (128, NE))),
        "w1n": _w1_layout(w1, perm),
        "b1pp": np.ascontiguousarray(
            b1[:, perm].reshape(NE, 16, 128).transpose(2, 0, 1).reshape(128, NE * 16)),
        "w2n": _w2_layout(w2),
        "b2": np.ascontiguousarray(b2),
        "nsc": np.ascontiguousarray(nscale.reshape(NFC, 128).T),
    }
    inv = 1.0 / (10000.0 ** (np.arange(0, HD, 2, dtype=np.float64) / HD))
    pos = np.arange(S, dtype=np.float64)
    fr = np.outer(pos, inv)
    emb = np.concatenate([fr, fr], -1)
    cos1 = np.cos(emb).astype(np.float32).T
    sin1 = np.sin(emb).astype(np.float32).T
    com["cosb"] = np.ascontiguousarray(np.vstack([cos1, cos1]))
    com["sinb"] = np.ascontiguousarray(np.vstack([sin1, sin1]))
    mk = np.zeros((128, 4 * 1024), np.float32)
    kk = np.arange(128)[:, None]
    qq = np.arange(TG)[None, :]
    for r in range(4):
        m_ = np.where(qq >= kk + 128 * r, 0.0, -1e30)
        mk[:, r * 1024: r * 1024 + 512] = m_
        mk[:, r * 1024 + 512:(r + 1) * 1024] = m_
    com["maskb"] = mk
    rt = np.zeros((128, 128), np.float32)
    for base in (0, 64):
        for d in range(32):
            rt[base + d + 32, base + d] = -1.0
            rt[base + d, base + d + 32] = 1.0
    com["rT"] = rt
    com["ident"] = np.eye(128, dtype=np.float32)
    oc3 = np.zeros((128, 3), np.float32)
    oc3[:, 0] = 1.0
    oc3[0:64, 1] = 1.0
    oc3[64:128, 2] = 1.0
    com["onescol"] = oc3
    com["onescolb"] = oc3
    com["onesrow"] = np.ones((1, 128), np.float32)
    com["onesrowb"] = np.ones((1, 128), np.float32)
    cv = np.zeros((128, 4), np.float32)
    cv[:, 1] = NEPS
    com["cv"] = cv

    in_maps = []
    for b in range(x.shape[0]):
        m = dict(com)
        m["xT"] = np.ascontiguousarray(x[b].T)
        in_maps.append(m)
    return in_maps


def kernel(**inputs):
    in_maps = _host_prep(inputs)
    if "nc" not in _CACHE:
        _CACHE["nc"], _ = build_program()
    res = run_bass_kernel_spmd(_CACHE["nc"], in_maps, core_ids=list(range(8)))
    outs = [np.ascontiguousarray(np.asarray(r["outT"]).T) for r in res.results]
    return np.stack(outs, axis=0)


# version 09
# speedup vs baseline: 1.0487x; 1.0047x over previous
import numpy as np
import ml_dtypes
import concourse.bass as bass
import concourse.mybir as mybir
from concourse.bass_utils import run_bass_kernel_spmd

F32 = mybir.dt.float32
BF16 = mybir.dt.bfloat16
AF = mybir.ActivationFunctionType
ALU = mybir.AluOpType
AX = mybir.AxisListType

S = 4096
H = 1024
NH, HD = 16, 64
NE = 32
TG = 512
NTG = S // TG
NFC = H // 128
HALF = S // 2
NEPS = 1e-5
SIGC = float(1.0 / (1.0 + np.exp(-1.702 * 7.0)))

ENG = ("pe", "ve", "ae", "sy", "gp")
_NE_RUN = [NE]
_FLAGS = {"pool": True, "cbtw2": True}


class Prog:
    def __init__(self, nc):
        self.nc = nc
        self.st = {e: [] for e in ENG}
        self.sems = {}
        self.cnt = {}

    def sem(self, name):
        if name not in self.sems:
            self.sems[name] = self.nc.alloc_semaphore(name)
            self.cnt[name] = 0
        return self.sems[name]

    def em(self, eng, f):
        self.st[eng].append(f)

    def wait(self, eng, ticket):
        if ticket is None:
            return
        name, val = ticket
        if val <= 0:
            return
        s = self.sem(name)
        self.em(eng, lambda e, s=s, v=val: e.wait_ge(s, v))

    def waits(self, eng, tickets):
        for t in tickets:
            self.wait(eng, t)

    def dma(self, q, out_ap, in_ap, semname):
        s = self.sem(semname)
        self.cnt[semname] += 16
        v = self.cnt[semname]
        self.em(q, lambda e, o=out_ap, i=in_ap, s=s: e.dma_start(out=o, in_=i).then_inc(s, 16))
        return (semname, v)

    def op(self, eng, fn, inc=False):
        if not inc:
            self.em(eng, lambda e, f=fn: f(e))
            return None
        s = self.sem(eng)
        self.cnt[eng] += 1
        v = self.cnt[eng]
        self.em(eng, lambda e, f=fn, s=s: f(e).then_inc(s, 1))
        return (eng, v)

    def now(self, *sems):
        return [(s, self.cnt.get(s, 0)) for s in (sems or self.cnt.keys())]


class Buf:
    def __init__(self, n):
        self.n = n
        self.free = [[] for _ in range(n)]

    def acquire(self, p, eng, i):
        for t in self.free[i % self.n]:
            p.wait(eng, t)
        self.free[i % self.n] = []

    def release(self, i, *tickets):
        self.free[i % self.n] += [t for t in tickets if t is not None]


def _final(p):
    for name, v in list(p.cnt.items()):
        if name in ENG:
            continue
        p.wait("sy", (name, v))


def build_program(debug_stage=0):
    nc = bass.Bass("TRN2", target_bir_lowering=False)
    p = Prog(nc)

    api = lambda name, shape, dt: nc.dram_tensor(name, shape, dt, kind="ExternalInput").ap()
    xT = api("xT", [H, S], F32)
    wqT = api("wqT", [H, H], F32)
    wkT = api("wkT", [H, H], F32)
    wvT = api("wvT", [H, H], F32)
    woT = api("woT", [H, H], F32)
    gwT = api("gwT", [H, NE], F32)
    gbb_h = api("gbb", [128, NE], F32)
    w1n = api("w1n", [NE, 8, 128, 2048], F32)
    b1pp_h = api("b1pp", [128, NE * 16], F32)
    w2n = api("w2n", [NE, 8, 128, 1024], F32)
    b2_h = api("b2", [NE, H], F32)
    nsc_h = api("nsc", [128, NFC], F32)
    cos_h = api("cosb", [128, S], F32)
    sin_h = api("sinb", [128, S], F32)
    mask_h = api("maskb", [128, 4 * 1024], F32)
    rT_h = api("rT", [128, 128], F32)
    ident_h = api("ident", [128, 128], F32)
    onescol_h = api("onescol", [128, 3], F32)
    onescolb_h = api("onescolb", [128, 3], F32)
    onesrow_h = api("onesrow", [1, 128], F32)
    onesrowb_h = api("onesrowb", [1, 128], F32)
    cv_h = api("cv", [128, 4], F32)

    outT = nc.dram_tensor("outT", [H, S], F32, kind="ExternalOutput").ap()

    def scratch(name, shape, dt, stage):
        kind = "ExternalOutput" if (0 < debug_stage == stage) else "Internal"
        return nc.dram_tensor(name, shape, dt, kind=kind).ap()
    q_s = scratch("q_s", [NH // 2, 128, S], BF16, 1)
    k_s = scratch("k_s", [NH // 2, 128, S], BF16, 1)
    v_s = scratch("v_s", [S, H], BF16, 1)
    o_s = scratch("o_s", [NH // 2, 128, S], BF16, 2)
    x2_s = scratch("x2_s", [H, S], F32, 3)

    A1 = nc.alloc_sbuf_tensor("A1", [128, 16384], F32)
    A2 = nc.alloc_sbuf_tensor("A2", [128, 12320], F32)
    A3 = nc.alloc_sbuf_tensor("A3", [128, 8192], F32)
    A4 = nc.alloc_sbuf_tensor("A4", [128, 8192], F32)
    A1b, A2b, A3b, A4b = (a.bitcast(BF16) for a in (A1, A2, A3, A4))
    masks2 = A3[:, 3072:7168]
    nstage = A1[0:1, 4608:6656]
    cvt = nc.alloc_sbuf_tensor("cvt_sb", [128, 4], F32)
    nsc = nc.alloc_sbuf_tensor("nsc_sb", [128, NFC], F32)
    gbb = nc.alloc_sbuf_tensor("gbb_sb", [128, NE], F32)
    b1pp = nc.alloc_sbuf_tensor("b1pp_sb", [128, NE * 16], F32)
    b1sig = nc.alloc_sbuf_tensor("b1sig_sb", [128, NE * 8], F32)
    b1lin1 = nc.alloc_sbuf_tensor("b1lin1_sb", [128, NE * 8], F32)
    b2bf = nc.alloc_sbuf_tensor("b2bf_sb", [NE, H], BF16)
    gw_sb = nc.alloc_sbuf_tensor("gw_sb", [128, NFC * NE], BF16)
    rT = nc.alloc_sbuf_tensor("rT_sb", [128, 128], BF16)
    ident = nc.alloc_sbuf_tensor("ident_sb", [128, 128], F32)
    onescol = nc.alloc_sbuf_tensor("onescol_sb", [128, 3], F32)
    onescolb = nc.alloc_sbuf_tensor("onescolb_sb", [128, 3], BF16)
    onesrow = nc.alloc_sbuf_tensor("onesrow_sb", [1, 128], F32)
    onesrowb = nc.alloc_sbuf_tensor("onesrowb_sb", [1, 128], BF16)
    rows2 = nc.alloc_sbuf_tensor("rows2", [1, 3072], F32)
    nb_sb = nc.alloc_sbuf_tensor("nb_sb", [128, 16], F32)
    cfm = nc.alloc_sbuf_tensor("cfm", [NE, HALF], BF16)
    crow = nc.alloc_sbuf_tensor("crow", [1, HALF], BF16)
    g_sb = nc.alloc_sbuf_tensor("g_sb", [128, NE], F32)
    m8 = nc.alloc_sbuf_tensor("m8", [128, 8], F32)
    top8 = nc.alloc_sbuf_tensor("top8", [128, 8], F32)
    gmod = nc.alloc_sbuf_tensor("gmod", [128, NE], F32)
    gmask = nc.alloc_sbuf_tensor("gmask", [128, NE], F32)
    negm1 = nc.alloc_sbuf_tensor("negm1", [128, 1], F32)
    srec = nc.alloc_sbuf_tensor("srec", [128, 1], F32)
    comb = nc.alloc_sbuf_tensor("comb", [128, NE], F32)
    nq_d = nc.dram_tensor("nq_d", [1, NH * S], F32, kind="Internal").ap()
    nk_d = nc.dram_tensor("nk_d", [1, NH * S], F32, kind="Internal").ap()
    rows_d = nc.dram_tensor("rows_d", [8, S], F32, kind="Internal").ap()

    nc.const_aps.register(F32, 0.0, cvt[:, 0:1])
    nc.const_aps.register(F32, NEPS, cvt[:, 1:2])

    PS = nc.alloc_psum_tensor("PS", [128, 8 * TG], F32)

    def bank(i, n=1, p0=0, p1=128):
        return PS[p0:p1, i * TG:(i + n) * TG]

    (R_A2, R_B2, R_A3, R_B3) = range(4)

    for dst, src in ((cvt, cv_h), (nsc, nsc_h), (gbb, gbb_h), (b1pp, b1pp_h),
                     (ident, ident_h), (onescol, onescol_h),
                     (onesrow, onesrow_h), (masks2, mask_h)):
        p.dma("sy", dst[:], src, "s_init")
    for dst, src in ((rT, rT_h), (onescolb, onescolb_h), (onesrowb, onesrowb_h)):
        p.dma("gp", dst[:], src, "s_initg")
    p.dma("gp", gw_sb.ap().rearrange("p (c n) -> p c n", c=NFC),
          gwT.rearrange("(c p) n -> p c n", p=128), "s_initg")
    p.dma("gp", b2bf[:], b2_h, "s_initg")
    T_INIT = [("s_init", p.cnt["s_init"]), ("s_initg", p.cnt["s_initg"])]

    b1r = b1pp.ap().rearrange("p (e c) -> p e c", e=NE)
    p.waits("ve", T_INIT)
    p.op("ve", lambda e: e.tensor_scalar(
        b1sig.ap().rearrange("p (e c) -> p e c", e=NE), b1r[:, :, 0:8], 1.702, None, ALU.mult))
    p.op("ve", lambda e: e.tensor_scalar(
        b1lin1.ap().rearrange("p (e c) -> p e c", e=NE), b1r[:, :, 8:16], 1.0, None, ALU.add), inc=True)

    ln1c = lambda fc, lo=0, n=TG: A1b[:, fc * 512 + lo: fc * 512 + lo + n]
    xcell = [A1[:, 2048 + i * 512: 2048 + (i + 1) * 512] for i in range(3)]
    sqcell = [A1[:, 3584 + i * 512: 3584 + (i + 1) * 512] for i in range(2)]
    vw = [A2b[:, i * 4096:(i + 1) * 4096] for i in range(2)]
    wqk = [A2b[:, 8192 + i * 1024: 8192 + (i + 1) * 1024] for i in range(2)]
    cosc = [A2b[:, 10240 + i * 512: 10240 + (i + 1) * 512] for i in range(2)]
    sinc = [A2b[:, 11264 + i * 512: 11264 + (i + 1) * 512] for i in range(2)]
    vcell = [A2b[:, 12288 + i * 512: 12288 + (i + 1) * 512] for i in range(4)]
    qraw = [A2b[:, 14336 + i * 512: 14336 + (i + 1) * 512] for i in range(2)]
    qrope = [A2b[:, 15360 + i * 512: 15360 + (i + 1) * 512] for i in range(2)]
    sqro = [A2b[:, 16384 + i * 512: 16384 + (i + 1) * 512] for i in range(2)]
    qcc = [A2[:, 8704 + i * 512: 8704 + (i + 1) * 512] for i in range(2)]
    rsc = [A2[:, 9728 + i * 512: 9728 + (i + 1) * 512] for i in range(2)]
    cbA = A2[:, 10752:11264]
    cbB = A2[:, 11264:11776]

    xT_r = xT.rearrange("(c p) t -> c p t", p=128)
    wq_r = wqT.rearrange("(c p) n -> p c n", p=128)
    wk_r = wkT.rearrange("(c p) n -> p c n", p=128)
    wv_r = wvT.rearrange("(c p) n -> p c n", p=128)

    bx, bsq = Buf(3), Buf(2)
    bwqk, bvw, bvc = Buf(2), Buf(2), Buf(4)
    bqraw, bqc, bqro, bsqro, bcs = Buf(2), Buf(2), Buf(2), Buf(2), Buf(2)
    b_proj, b_rot, b_stat, b_vps = Buf(2), Buf(2), Buf(1), Buf(2)
    b_ln1, b_cb = Buf(1), Buf(1)
    bnst = Buf(2)

    p.waits("pe", T_INIT)
    p.waits("ae", T_INIT)
    xi = 0
    for tg in range(NTG):
        lo = tg * TG
        cs = tg % 2
        bcs.acquire(p, "gp", cs)
        t_c1 = p.dma("gp", cosc[cs], cos_h[:, lo:lo + TG], f"s_cs{cs}")
        t_c2 = p.dma("gp", sinc[cs], sin_h[:, lo:lo + TG], f"s_cs{cs}")
        b_stat.acquire(p, "pe", 0)
        for fc in range(NFC):
            sl = xi % 3
            xi += 1
            bx.acquire(p, "sy", sl)
            tx = p.dma("sy", xcell[sl], xT_r[fc, :, lo:lo + TG], f"s_x{sl}")
            ss = fc % 2
            bsq.acquire(p, "ae", ss)
            p.wait("ae", tx)
            tq = p.op("ae", lambda e, o=sqcell[ss], i=xcell[sl]: e.activation(o, i, AF.Square), inc=True)
            p.wait("pe", tx)
            p.op("pe", lambda e, i=xcell[sl], st=(fc == 0), sp=(fc == NFC - 1):
                 e.matmul(bank(2, p1=1), onescol[:, 0:1], i, start=st, stop=sp))
            p.wait("pe", tq)
            tmm = p.op("pe", lambda e, i=sqcell[ss], st=(fc == 0), sp=(fc == NFC - 1):
                       e.matmul(bank(3, p1=1), onescol[:, 0:1], i, start=st, stop=sp), inc=True)
            bx.release(sl, tmm)
            bsq.release(ss, tmm)
        p.wait("ve", tmm)
        p.op("ve", lambda e: e.tensor_scalar(rows2[0:1, 0:512], bank(2, p1=1), 1.0 / H, None, ALU.mult))
        p.op("ve", lambda e: e.tensor_scalar(rows2[0:1, 512:1024], bank(3, p1=1), 1.0 / H, None, ALU.mult))
        p.op("ve", lambda e: e.tensor_mul(rows2[0:1, 1024:1536], rows2[0:1, 0:512], rows2[0:1, 0:512]))
        tvar = p.op("ve", lambda e: e.tensor_sub(rows2[0:1, 1536:2048], rows2[0:1, 512:1024], rows2[0:1, 1024:1536]), inc=True)
        b_stat.release(0, tvar)
        p.wait("ae", tvar)
        p.op("ae", lambda e: e.activation(rows2[0:1, 1024:1536], rows2[0:1, 1536:2048], AF.Ln, bias=NEPS))
        trs = p.op("ae", lambda e: e.activation(rows2[0:1, 2048:2560], rows2[0:1, 1024:1536], AF.Exp, scale=-0.5), inc=True)
        p.wait("ve", trs)
        tbr = p.op("ve", lambda e:
                   e.scalar_tensor_tensor(rows2[0:1, 2560:3072], rows2[0:1, 0:512], -1.0, rows2[0:1, 2048:2560], ALU.mult, ALU.mult), inc=True)
        b_stat.acquire(p, "pe", 0)
        p.wait("pe", tbr)
        p.op("pe", lambda e: e.matmul(bank(2), onesrow[:], rows2[0:1, 2048:2560], start=True, stop=True))
        tbc = p.op("pe", lambda e: e.matmul(bank(3), onesrow[:], rows2[0:1, 2560:3072], start=True, stop=True), inc=True)
        b_cb.acquire(p, "ve", 0)
        p.wait("ve", tbc)
        p.op("ve", lambda e: e.tensor_copy(cbA, bank(2)))
        tcb = p.op("ve", lambda e: e.tensor_copy(cbB, bank(3)), inc=True)
        b_stat.release(0, tcb)
        b_ln1.acquire(p, "ve", 0)
        for fc in range(NFC):
            sl = xi % 3
            xi += 1
            bx.acquire(p, "sy", sl)
            tx = p.dma("sy", xcell[sl], xT_r[fc, :, lo:lo + TG], f"s_x{sl}")
            p.wait("ve", tx)
            p.op("ve", lambda e, i=xcell[sl], o=rsc[fc % 2]: e.tensor_mul(o, i, cbA))
            tl = p.op("ve", lambda e, o=ln1c(fc), i=rsc[fc % 2]: e.tensor_add(o, i, cbB), inc=True)
            bx.release(sl, tl)
        t_ln1 = tl
        b_cb.release(0, t_ln1)

        for og in range(2):
            ws = og % 2
            bvw.acquire(p, "gp", ws)
            tw = p.dma("gp", vw[ws].rearrange("p (c n) -> p c n", c=NFC),
                       wv_r[:, :, og * 512:(og + 1) * 512], f"s_vw{ws}")
            p.wait("pe", tw)
            p.wait("pe", t_ln1)
            for tt in range(4):
                vb = tt % 2
                b_vps.acquire(p, "pe", vb)
                for fc in range(NFC):
                    tpm = p.op("pe", lambda e, l=ln1c(fc, tt * 128, 128),
                               r=vw[ws].rearrange("p (c n) -> p c n", c=NFC)[:, fc, :],
                               o=bank(4 + vb), st=(fc == 0), sp=(fc == NFC - 1):
                               e.matmul(o, l, r, start=st, stop=sp), inc=(fc == NFC - 1))
                vc = (og * 4 + tt) % 4
                bvc.acquire(p, "ve", vc)
                p.wait("ve", tpm)
                tve = p.op("ve", lambda e, o=vcell[vc], b=bank(4 + vb): e.tensor_copy(o, b), inc=True)
                b_vps.release(vb, tve)
                p.wait("sy", tve)
                tvd = p.dma("sy", v_s[lo + tt * 128: lo + (tt + 1) * 128, og * 512:(og + 1) * 512],
                            vcell[vc], f"s_vs{vc}")
                bvc.release(vc, tvd)
            bvw.release(ws, tpm)

        for wi, (wh_r, dst, ndst) in enumerate(((wq_r, q_s, nq_d), (wk_r, k_s, nk_d))):
            for oc in range(NFC):
                ws = oc % 2
                bwqk.acquire(p, "gp", ws)
                tw = p.dma("gp", wqk[ws].rearrange("p (c n) -> p c n", c=NFC),
                           wh_r[:, :, oc * 128:(oc + 1) * 128], f"s_w{ws}")
                pb = oc % 2
                b_proj.acquire(p, "pe", pb)
                p.wait("pe", tw)
                if oc == 0 and wi == 0:
                    p.wait("pe", t_ln1)
                for fc in range(NFC):
                    tpm = p.op("pe", lambda e, l=wqk[ws].rearrange("p (c n) -> p c n", c=NFC)[:, fc, :],
                               r=ln1c(fc), o=bank(pb), st=(fc == 0), sp=(fc == NFC - 1):
                               e.matmul(o, l, r, start=st, stop=sp), inc=(fc == NFC - 1))
                bwqk.release(ws, tpm)
                qs2 = oc % 2
                bqraw.acquire(p, "ae", qs2)
                p.wait("ae", tpm)
                tqr = p.op("ae", lambda e, o=qraw[qs2], b=bank(pb): e.copy(o, b), inc=True)
                bqc.acquire(p, "ve", qs2)
                p.wait("ve", tqr)
                if oc < 2 and wi == 0:
                    p.waits("ve", [t_c1, t_c2])
                tqc = p.op("ve", lambda e, o=qcc[qs2], b=bank(pb), c=cosc[cs]: e.tensor_mul(o, b, c), inc=True)
                b_proj.release(pb, tqc)
                rb_ = oc % 2
                b_rot.acquire(p, "pe", rb_)
                p.wait("pe", tqc)
                trm = p.op("pe", lambda e, o=bank(6 + rb_), i=qraw[qs2]:
                           e.matmul(o, rT[:], i, start=True, stop=True), inc=True)
                bqraw.release(qs2, trm)
                bqro.acquire(p, "ve", qs2)
                p.wait("ve", trm)
                p.op("ve", lambda e, o=rsc[qs2], b=bank(6 + rb_), s2=sinc[cs]: e.tensor_mul(o, b, s2))
                tqo = p.op("ve", lambda e, o=qrope[qs2], a=qcc[qs2], b=rsc[qs2]: e.tensor_add(o, a, b), inc=True)
                b_rot.release(rb_, tqo)
                bqc.release(qs2, tqo)
                bsqro.acquire(p, "ae", qs2)
                p.wait("ae", tqo)
                tq2 = p.op("ae", lambda e, o=sqro[qs2], i=qrope[qs2]: e.activation(o, i, AF.Square), inc=True)
                b_stat.acquire(p, "pe", 0)
                p.wait("pe", tq2)
                p.op("pe", lambda e, i=sqro[qs2]: e.matmul(bank(2, p1=1), onescolb[:, 1:2], i, start=True, stop=True))
                tnm = p.op("pe", lambda e, i=sqro[qs2]: e.matmul(bank(3, p1=1), onescolb[:, 2:3], i, start=True, stop=True), inc=True)
                bsqro.release(qs2, tnm)
                p.wait("ve", tnm)
                c0 = (2 * oc) * S + lo
                c1 = (2 * oc + 1) * S + lo
                bnst.acquire(p, "ve", qs2)
                p.op("ve", lambda e, o=nstage[0:1, qs2 * 1024: qs2 * 1024 + 512]: e.tensor_copy(o, bank(2, p1=1)))
                tne = p.op("ve", lambda e, o=nstage[0:1, qs2 * 1024 + 512: qs2 * 1024 + 1024]: e.tensor_copy(o, bank(3, p1=1)), inc=True)
                b_stat.release(0, tne)
                p.wait("sy", tne)
                p.dma("sy", ndst[0:1, c0:c0 + TG], nstage[0:1, qs2 * 1024: qs2 * 1024 + 512], f"s_nq{qs2}")
                tnd = p.dma("sy", ndst[0:1, c1:c1 + TG], nstage[0:1, qs2 * 1024 + 512: qs2 * 1024 + 1024], f"s_nq{qs2}")
                bnst.release(qs2, tnd)
                p.wait("sy", tqo)
                tqd = p.dma("sy", dst[oc, :, lo:lo + TG], qrope[qs2][:, :], f"s_qk{qs2}")
                bqro.release(qs2, tqd)
            if wi == 1:
                b_ln1.release(0, tpm)
        bcs.release(cs, tqo)

    if debug_stage == 1:
        _final(p)
        return _wire(nc, p)

    PH1 = p.now()

    qh2 = [A1b[:, i * 4096:(i + 1) * 4096] for i in range(2)]
    kh2 = [A1b[:, 8192 + i * 4096: 8192 + (i + 1) * 4096] for i in range(2)]
    vh2 = [[A1b[:, 16384 + (2 * i + j) * 2080: 16384 + (2 * i + j + 1) * 2080]
            for j in range(2)] for i in range(2)]
    pcell = [A1b[:, 24704 + i * 512: 24704 + (i + 1) * 512] for i in range(4)]
    pcp2 = [A1b[:, 24704 + i * 1024: 24704 + (i + 1) * 1024] for i in range(2)]
    osbc = [A1b[:, 26752 + i * 512: 26752 + (i + 1) * 512] for i in range(4)]
    nstage = [A2[0:1, i * 2048:(i + 1) * 2048] for i in range(2)]
    rqrow = A2[0:1, 4096:4128]
    rkrow = A2[0:1, 4128:4160]
    nbsc = A2[0:1, 4224:4288]
    zraw = A2[0:1, 4608:5632]
    zrecb = A2b[0:1, 11264:12288]
    zln = A2[0:1, 8704:9728]
    rbc = [A2[:, 6656 + i * 512: 6656 + (i + 1) * 512] for i in range(2)]
    msb = [A2[:, 7680 + i * 512: 7680 + (i + 1) * 512] for i in range(2)]
    msb2 = A2[:, 7680:8704]

    msbp = [A2[:, 9728 + i * 1024: 9728 + (i + 1) * 1024] for i in range(2)]
    bqk, bvh = Buf(2), Buf(2)
    bpcp, bmsb2, bosb = Buf(2), Buf(2), Buf(2)
    bavz, bnst = Buf(2), Buf(2)
    sguard = {0: [], 1: [], 2: [], 3: []}
    rot = [0]

    def claim_slot(extra=()):
        r = rot[0] % 2
        rot[0] += 1
        bA, bB = 2 * r, 2 * r + 1
        p.waits("pe", sguard[bA] + sguard[bB] + [t for t in extra if t is not None])
        sguard[bA], sguard[bB] = [], []
        return bA, bB

    for eng in ENG:
        p.waits(eng, PH1)

    nsl = 0
    pend = [None]
    tzeh, trbh = [None], [None]
    trd = None
    for hh in range(NH):
        for src_d, rrow in ((nq_d, rqrow), (nk_d, rkrow)):
            for part in range(2):
                bnst.acquire(p, "ae", nsl)
                tld = p.dma("ae", nstage[nsl % 2],
                            src_d[0:1, hh * S + part * 2048: hh * S + (part + 1) * 2048], "s_nrm")
                p.wait("ve", tld)
                trd = p.op("ve", lambda e, o=rrow[0:1, hh * 2 + part: hh * 2 + part + 1],
                           i=nstage[nsl % 2]: e.tensor_reduce(o, i, AX.X, ALU.max), inc=True)
                bnst.release(nsl, trd)
                nsl += 1
    p.wait("ve", trd)
    tmx = None
    for hh in range(NH):
        p.op("ve", lambda e, o=nbsc[0:1, hh:hh + 1], a=rqrow[0:1, 2 * hh:2 * hh + 1],
             b=rqrow[0:1, 2 * hh + 1:2 * hh + 2]: e.tensor_max(o, a, b))
        tmx = p.op("ve", lambda e, o=nbsc[0:1, 16 + hh:17 + hh], a=rkrow[0:1, 2 * hh:2 * hh + 1],
                   b=rkrow[0:1, 2 * hh + 1:2 * hh + 2]: e.tensor_max(o, a, b), inc=(hh == NH - 1))
    p.wait("ve", tmx)
    tml = p.op("ve", lambda e: e.tensor_mul(nbsc[0:1, 32:48], nbsc[0:1, 0:16], nbsc[0:1, 16:32]), inc=True)
    p.wait("ve", tml)
    tpm = None
    for pj in range(NH // 2):
        tpm = p.op("ve", lambda e, o=nbsc[0:1, 48 + pj:49 + pj], a=nbsc[0:1, 32 + 2 * pj:33 + 2 * pj],
                   b=nbsc[0:1, 33 + 2 * pj:34 + 2 * pj]: e.tensor_max(o, a, b), inc=(pj == NH // 2 - 1))
    p.wait("ae", tpm)
    p.op("ae", lambda e: e.activation(nbsc[0:1, 56:64], nbsc[0:1, 48:56], AF.Ln))
    tsq = p.op("ae", lambda e: e.activation(nbsc[0:1, 32:40], nbsc[0:1, 56:64], AF.Exp, scale=0.5), inc=True)
    p.wait("ve", tsq)
    tnb = p.op("ve", lambda e, o=A2b[0:1, 8576:8584]:
               e.tensor_scalar(o, nbsc[0:1, 32:40], -0.125, None, ALU.mult), inc=True)
    rA, rB = claim_slot([tnb])
    tnm = p.op("pe", lambda e, o=bank(rA)[:, 0:8]: e.matmul(o, onesrowb[:], A2b[0:1, 8576:8584],
                                                            start=True, stop=True), inc=True)
    p.wait("ve", tnm)
    tnbc = p.op("ve", lambda e, o=nb_sb[:, 0:8]: e.tensor_copy(o, bank(rA)[:, 0:8]), inc=True)
    sguard[rA].append(tnbc)

    for pp in range(NH // 2):
        s = pp % 2
        bqk.acquire(p, "sy", s)
        tq = p.dma("sy", qh2[s], q_s[pp], f"s_qh{s}")
        tk = p.dma("sy", kh2[s], k_s[pp], f"s_qh{s}")
        bvh.acquire(p, "sy", s)
        vr = [vh2[s][j].rearrange("p (c d) -> p c d", d=65) for j in range(2)]
        tvh = None
        for j, hh in enumerate((2 * pp, 2 * pp + 1)):
            tvh = p.dma("sy", vr[j][:, :, 0:64],
                        v_s.rearrange("(c p) o -> p c o", p=128)[:, :, 64 * hh: 64 * hh + 64], f"s_vh{s}")
        p.wait("ve", tvh)
        tvm = None
        for j in range(2):
            tvm = p.op("ve", lambda e, o=vr[j][:, :, 64:65]: e.memset(o, 1.0), inc=(j == 1))

        nbc = nb_sb[:, pp:pp + 1]
        qk_last = av_last = None
        for qg in range(NTG):
            nkc = 4 * (qg + 1)
            qpar = qg % 2
            avA, avB = 4 + 2 * qpar, 5 + 2 * qpar
            kinfo = []

            def emit_av(kc, te, qg=qg, qpar=qpar, avA=avA, avB=avB, nkc=nkc, pp=pp, s=s):
                if kc == 0:
                    bavz.acquire(p, "pe", qpar)
                p.wait("pe", te)
                if qg == 0 and kc == 0:
                    p.waits("pe", [tvh, tvm])
                st_, sp_ = (kc == 0), (kc == nkc - 1)
                sp2 = kc % 2
                vrl = [vh2[s][j].rearrange("p (c d) -> p c d", d=65) for j in range(2)]
                p.op("pe", lambda e, o=PS[0:65, avA * TG:(avA + 1) * TG], l=vrl[0][:, kc, :],
                     r=pcell[2 * sp2], st=st_, sp=sp_: e.matmul(o, l, r, start=st, stop=sp))
                t_ = p.op("pe", lambda e, o=PS[0:65, avB * TG:(avB + 1) * TG], l=vrl[1][:, kc, :],
                          r=pcell[2 * sp2 + 1], st=st_, sp=sp_: e.matmul(o, l, r, start=st, stop=sp),
                          inc=True)
                bpcp.release(sp2, t_)
                return t_

            for kc in range(nkc):
                first = (qg == 0 and kc == 0)
                bA, bB = claim_slot([tq, tk] if first else ())
                p.op("pe", lambda e, o=bank(bA), l=kh2[s][0:64, kc * 128:(kc + 1) * 128],
                     r=qh2[s][0:64, qg * TG:(qg + 1) * TG]:
                     e.matmul(o, l, r, start=True, stop=True))
                qk_last = p.op("pe", lambda e, o=bank(bB), l=kh2[s][64:128, kc * 128:(kc + 1) * 128],
                               r=qh2[s][64:128, qg * TG:(qg + 1) * TG]:
                               e.matmul(o, l, r, start=True, stop=True), inc=True)
                sp2 = kc % 2
                bpcp.acquire(p, "ae", sp2)
                if kc >= 4 * qg:
                    r_ = kc - 4 * qg
                    bmsb2.acquire(p, "ve", sp2)
                    p.wait("ve", qk_last)
                    tmB = p.op("ve", lambda e, o=msbp[sp2], b=PS[:, bA * TG:(bB + 1) * TG],
                               m=masks2[:, r_ * 1024:(r_ + 1) * 1024]:
                               e.tensor_add(o, b, m), inc=True)
                    sguard[bA].append(tmB)
                    sguard[bB].append(tmB)
                    p.wait("ae", tmB)
                    if first:
                        p.wait("ae", tnbc)
                    te = p.op("ae", lambda e, o=pcp2[sp2], i=msbp[sp2], bi=nbc:
                              e.activation(o, i, AF.Exp, bias=bi, scale=0.125), inc=True)
                    bmsb2.release(sp2, te)
                else:
                    p.wait("ae", qk_last)
                    te = p.op("ae", lambda e, o=pcp2[sp2], b=PS[:, bA * TG:(bB + 1) * TG], bi=nbc:
                              e.activation(o, b, AF.Exp, bias=bi, scale=0.125), inc=True)
                    sguard[bA].append(te)
                    sguard[bB].append(te)
                kinfo.append(te)
                if kc >= 1:
                    av_last = emit_av(kc - 1, kinfo[kc - 1])
                if kc == 3 and pend[0] is not None:
                    pend[0]()
                    pend[0] = None
            av_last = emit_av(nkc - 1, kinfo[nkc - 1])

            def make_tail(pp=pp, qg=qg, qpar=qpar, avA=avA, avB=avB, av_last=av_last):
                p.wait("ve", av_last)
                p.wait("ve", tzeh[0])
                p.op("ve", lambda e, o=zraw[0:1, 0:512], b=PS[64:65, avA * TG:(avA + 1) * TG]:
                     e.tensor_scalar(o, b, 1e-30, None, ALU.add))
                tz2 = p.op("ve", lambda e, o=zraw[0:1, 512:1024], b=PS[64:65, avB * TG:(avB + 1) * TG]:
                           e.tensor_scalar(o, b, 1e-30, None, ALU.add), inc=True)
                p.waits("ae", [tz2, trbh[0]])
                p.op("ae", lambda e: e.activation(zln[0:1, 0:1024], zraw[0:1, 0:1024], AF.Ln))
                tze = p.op("ae", lambda e: e.activation(zrecb[0:1, 0:1024], zln[0:1, 0:1024],
                                                        AF.Exp, scale=-1.0), inc=True)
                tzeh[0] = tze

                def emit_pe():
                    rA2, rB2 = claim_slot([tze])
                    p.op("pe", lambda e, o=PS[0:64, rA2 * TG:(rA2 + 1) * TG]:
                         e.matmul(o, onesrowb[0:1, 0:64], zrecb[0:1, 0:512], start=True, stop=True))
                    trb = p.op("pe", lambda e, o=PS[0:64, rB2 * TG:(rB2 + 1) * TG]:
                               e.matmul(o, onesrowb[0:1, 0:64], zrecb[0:1, 512:1024],
                                        start=True, stop=True), inc=True)
                    trbh[0] = trb
                    p.wait("ve", trb)
                    p.op("ve", lambda e, o=rbc[0][0:64, :]: e.tensor_copy(o, PS[0:64, rA2 * TG:(rA2 + 1) * TG]))
                    trbc = p.op("ve", lambda e, o=rbc[1][0:64, :]: e.tensor_copy(o, PS[0:64, rB2 * TG:(rB2 + 1) * TG]), inc=True)
                    sguard[rA2].append(trbc)
                    sguard[rB2].append(trbc)
                    bosb.acquire(p, "ve", qpar)
                    p.op("ve", lambda e, o=osbc[2 * qpar][0:64, :], b=PS[0:64, avA * TG:(avA + 1) * TG],
                         r=rbc[0][0:64, :]: e.tensor_mul(o, b, r))
                    tosb = p.op("ve", lambda e, o=osbc[2 * qpar + 1][0:64, :], b=PS[0:64, avB * TG:(avB + 1) * TG],
                                r=rbc[1][0:64, :]: e.tensor_mul(o, b, r), inc=True)
                    bavz.release(qpar, tosb, tz2)
                    p.wait("sy", tosb)
                    p.dma("sy", o_s[pp, 0:64, qg * TG:(qg + 1) * TG], osbc[2 * qpar][0:64, :], f"s_os{qpar}")
                    tod = p.dma("sy", o_s[pp, 64:128, qg * TG:(qg + 1) * TG], osbc[2 * qpar + 1][0:64, :], f"s_os{qpar}")
                    bosb.release(qpar, tod)
                return emit_pe

            pend[0] = make_tail()
        bqk.release(s, qk_last)
        bvh.release(s, av_last)
    if pend[0] is not None:
        pend[0]()
        pend[0] = None

    if debug_stage == 2:
        p.waits("sy", p.now("ve", "ae"))
        p.dma("sy", nc.dram_tensor("dbg_nb", [128, 16], F32, kind="ExternalOutput").ap(),
              nb_sb[:], "s_dbg")
        p.dma("sy", nc.dram_tensor("dbg_rows", [1, 256], F32, kind="ExternalOutput").ap(),
              A2[0:1, 4096:4352], "s_dbg")
        _final(p)
        return _wire(nc, p)
    PH2 = p.now()

    wo_sb = A2b[:, 0:8192]
    o_sb = [A2b[:, 8192 + i * 4096: 8192 + (i + 1) * 4096] for i in range(2)]
    x3cell = [A2[:, 8192 + i * 512: 8192 + (i + 1) * 512] for i in range(3)]
    x2cell = [A2[:, 9728 + i * 512: 9728 + (i + 1) * 512] for i in range(2)]
    sq2 = [A2[:, 10752 + i * 512: 10752 + (i + 1) * 512] for i in range(2)]

    for eng in ENG:
        p.waits(eng, PH2)
    t_wo = p.dma("gp", wo_sb.rearrange("p (c n) -> p c n", c=NFC),
                 woT.rearrange("(c p) n -> p c n", p=128), "s_wo")
    bos, bx3, bx2c, bsq2 = Buf(2), Buf(3), Buf(2), Buf(2)
    b_wops, b_st2 = Buf(2), Buf(1)
    wo_r = wo_sb.rearrange("p (c n) -> p c n", c=NFC)
    x2_r = x2_s.rearrange("(c p) t -> c p t", p=128)

    p.wait("pe", t_wo)
    xi = 0
    for tg in range(NTG):
        lo = tg * TG
        ts_ = tg % 2
        bos.acquire(p, "sy", ts_)
        tol = None
        for pp in range(NH // 2):
            dst = o_sb[ts_].rearrange("p (c n) -> p c n", c=NFC)[:, pp, :]
            tol = p.dma("sy", dst, o_s[pp, :, lo:lo + TG], f"s_ol{ts_}")
        b_st2.acquire(p, "pe", 0)
        tpm = None
        for oc in range(NFC):
            pb = oc % 2
            b_wops.acquire(p, "pe", pb)
            p.wait("pe", tol)
            for fc in range(NFC):
                tpm = p.op("pe", lambda e, o=bank(pb), l=wo_r[:, fc, oc * 128:(oc + 1) * 128],
                           r=o_sb[ts_].rearrange("p (c n) -> p c n", c=NFC)[:, fc, :],
                           st=(fc == 0), sp=(fc == NFC - 1):
                           e.matmul(o, l, r, start=st, stop=sp), inc=(fc == NFC - 1))
            sl = xi % 3
            xi += 1
            bx3.acquire(p, "sy", sl)
            tx = p.dma("sy", x3cell[sl], xT_r[oc, :, lo:lo + TG], f"s_x3{sl}")
            xs = oc % 2
            bx2c.acquire(p, "ve", xs)
            p.waits("ve", [tpm, tx])
            tx2 = p.op("ve", lambda e, o=x2cell[xs], b=bank(pb), x=x3cell[sl]: e.tensor_add(o, b, x), inc=True)
            b_wops.release(pb, tx2)
            bx3.release(sl, tx2)
            p.wait("sy", tx2)
            tx2d = p.dma("sy", x2_r[oc, :, lo:lo + TG], x2cell[xs], f"s_x2{xs}")
            ss = oc % 2
            bsq2.acquire(p, "ae", ss)
            p.wait("ae", tx2)
            tq2 = p.op("ae", lambda e, o=sq2[ss], i=x2cell[xs]: e.activation(o, i, AF.Square), inc=True)
            p.wait("pe", tx2)
            p.op("pe", lambda e, i=x2cell[xs], st=(oc == 0), sp=(oc == NFC - 1):
                 e.matmul(bank(4, p1=1), onescol[:, 0:1], i, start=st, stop=sp))
            p.wait("pe", tq2)
            tsm = p.op("pe", lambda e, i=sq2[ss], st=(oc == 0), sp=(oc == NFC - 1):
                       e.matmul(bank(5, p1=1), onescol[:, 0:1], i, start=st, stop=sp), inc=True)
            bx2c.release(xs, tsm, tx2d)
            bsq2.release(ss, tsm)
        bos.release(ts_, tpm)
        p.wait("ve", tsm)
        p.op("ve", lambda e: e.tensor_scalar(rows2[0:1, 0:512], bank(4, p1=1), 1.0 / H, None, ALU.mult))
        p.op("ve", lambda e: e.tensor_scalar(rows2[0:1, 512:1024], bank(5, p1=1), 1.0 / H, None, ALU.mult))
        p.op("ve", lambda e: e.tensor_mul(rows2[0:1, 1024:1536], rows2[0:1, 0:512], rows2[0:1, 0:512]))
        tvar = p.op("ve", lambda e: e.tensor_sub(rows2[0:1, 1536:2048], rows2[0:1, 512:1024], rows2[0:1, 1024:1536]), inc=True)
        b_st2.release(0, tvar)
        p.wait("ae", tvar)
        p.op("ae", lambda e: e.activation(rows2[0:1, 1024:1536], rows2[0:1, 1536:2048], AF.Ln, bias=NEPS))
        tr2 = p.op("ae", lambda e: e.activation(rows2[0:1, 2048:2560], rows2[0:1, 1024:1536], AF.Exp, scale=-0.5), inc=True)
        tu = p.op("ve", lambda e: e.tensor_scalar(rows2[0:1, 2560:3072], rows2[0:1, 1536:2048], NEPS, None, ALU.add), inc=True)
        p.wait("ve", tu)
        p.wait("ve", tr2)
        tur = p.op("ve", lambda e: e.reciprocal(rows2[0:1, 1024:1536], rows2[0:1, 2560:3072]), inc=True)
        p.wait("ve", tur)
        tq_ = p.op("ve", lambda e: e.tensor_mul(rows2[0:1, 2560:3072], rows2[0:1, 1536:2048], rows2[0:1, 1024:1536]), inc=True)
        p.wait("ae", tq_)
        p.op("ae", lambda e: e.activation(rows2[0:1, 1024:1536], rows2[0:1, 2560:3072], AF.Ln, bias=NEPS))
        trs2 = p.op("ae", lambda e: e.activation(rows2[0:1, 2560:3072], rows2[0:1, 1024:1536], AF.Exp, scale=-0.5), inc=True)
        p.waits("ve", [tr2, trs2])
        p.op("ve", lambda e: e.tensor_mul(rows2[0:1, 1024:1536], rows2[0:1, 2048:2560], rows2[0:1, 2560:3072]))
        p.op("ve", lambda e:
             e.scalar_tensor_tensor(rows2[0:1, 1536:2048], rows2[0:1, 0:512], -1.0, rows2[0:1, 1024:1536], ALU.mult, ALU.mult))
        p.op("ve", lambda e: e.tensor_scalar(rows2[0:1, 512:1024], rows2[0:1, 2048:2560], 1.0, None, ALU.add))
        trows = p.op("ve", lambda e:
                     e.scalar_tensor_tensor(rows2[0:1, 2560:3072], rows2[0:1, 0:512], -1.0, rows2[0:1, 2048:2560], ALU.mult, ALU.mult), inc=True)
        p.wait("sy", trows)
        for ri, sl_ in ((R_A2, (1024, 1536)), (R_B2, (1536, 2048)), (R_A3, (512, 1024)), (R_B3, (2560, 3072))):
            trd = p.dma("sy", rows_d[ri:ri + 1, lo:lo + TG], rows2[0:1, sl_[0]:sl_[1]], "s_rows")

    if debug_stage == 3:
        _final(p)
        return _wire(nc, p)
    PH3 = p.now()

    w1buf = [A2b[:, k * 2048:(k + 1) * 2048] for k in range(3)]
    w2buf = [A2b[:, 6144 + k * 1024: 6144 + (k + 1) * 1024] for k in range(2)]
    w1r = [w.rearrange("p (c n) -> p c n", c=NFC) for w in w1buf]
    w2r = [w.rearrange("p (c n) -> p c n", c=NFC) for w in w2buf]
    sSc = [A2[:, 4096 + i * 1024: 4096 + (i + 1) * 1024] for i in range(2)]
    sGc = [A2[:, 6144 + i * 1024: 6144 + (i + 1) * 1024] for i in range(2)]
    q1c = [A2[:, 8192 + i * 1024: 8192 + (i + 1) * 1024] for i in range(2)]
    scr = [A2[:, 4096 + i * 512: 4096 + (i + 1) * 512] for i in range(5)]
    bcA2 = A2[:, 6656:7168]
    bcB2 = A2[:, 7168:7680]
    bcA3 = A2[:, 7680:8192]
    bcB3 = A2[:, 8192:8704]
    cbtf = [A2[:, 10240 + i * 512: 10240 + (i + 1) * 512] for i in range(4)]
    cbtf2 = [A2[:, 10240 + i * 1024: 10240 + (i + 1) * 1024] for i in range(2)]
    gbank = lambda base: PS[:, base * TG:(base + 2) * TG]
    lbank = lambda base: PS[:, (base + 2) * TG:(base + 4) * TG]
    gw_r = gw_sb.ap().rearrange("p (c n) -> p c n", c=NFC)
    out_r = outT.rearrange("(c p) t -> p c t", p=128)

    for eng in ENG:
        p.waits(eng, PH3)

    t_out_prev = None
    t_w1cons = [None, None, None]
    t_w2cons = [None, None, None]
    last_p_ve = {0: None, 1: None}
    t_w2rd = {0: None, 1: None}
    t_cbtmm_last = None
    for half in range(2):
        hlo = half * HALF
        acc = lambda oc, t0=0, n=HALF: A1[:, oc * 2048 + t0: oc * 2048 + t0 + n]
        tsb = lambda fc, t0=0, n=TG: A3b[:, fc * 2048 + t0: fc * 2048 + t0 + n]
        hsl = lambda jj, ti: A4b[:, (jj * 4 + ti) * 512: (jj * 4 + ti + 1) * 512]
        hsl2 = lambda jj, pr: A4b[:, (jj * 4 + pr * 2) * 512: (jj * 4 + pr * 2 + 2) * 512]

        p.wait("sy", t_out_prev)
        p.waits("ve", p.now("gp", "ae"))
        t_x2l = None
        for oc in range(NFC):
            t_x2l = p.dma("sy", acc(oc), x2_r[oc, :, hlo:hlo + HALF], "s_x2l")
        b_bc = Buf(1)
        tb_prev = None
        for tgi in range(4):
            lo = hlo + tgi * TG
            b_bc.acquire(p, "pe", 0)
            p.waits("pe", p.now("ve", "ae"))
            p.waits("sy", [tb_prev])
            trl = None
            for bi, ri in enumerate((R_A2, R_B2, R_A3, R_B3)):
                trl = p.dma("sy", rows2[0:1, bi * 512:(bi + 1) * 512], rows_d[ri:ri + 1, lo:lo + TG], "s_rld")
            p.wait("pe", trl)
            tb = None
            for bi in range(4):
                tb = p.op("pe", lambda e, o=bank(bi), r=rows2[0:1, bi * 512:(bi + 1) * 512]:
                          e.matmul(o, onesrow[:], r, start=True, stop=True), inc=(bi == 3))
            tb_prev = tb
            p.wait("ve", tb)
            tbe = None
            for bi, dstc in enumerate((bcA2, bcB2, bcA3, bcB3)):
                tbe = p.op("ve", lambda e, o=dstc, b=bank(bi): e.tensor_copy(o, b), inc=(bi == 3))
            b_bc.release(0, tbe)
            p.wait("ve", t_x2l)
            tacc = None
            for oc in range(NFC):
                x2c = acc(oc, tgi * TG, TG)
                p.op("ve", lambda e, o=scr[0], i=x2c: e.tensor_mul(o, i, bcA2))
                p.op("ve", lambda e, o=scr[1], i=scr[0]: e.tensor_add(o, i, bcB2))
                p.op("ve", lambda e, o=tsb(oc, tgi * TG, TG), i=scr[1], s2=nsc[:, oc:oc + 1]:
                     e.tensor_scalar(o, i, s2, None, ALU.mult))
                p.op("ve", lambda e, o=scr[2], i=x2c: e.tensor_mul(o, i, bcA3))
                tacc = p.op("ve", lambda e, o=x2c, i=scr[2]: e.tensor_add(o, i, bcB3), inc=True)
        t_prep = tacc
        tbe_last = tbe
        last_p_ve = {0: t_prep, 1: t_prep}

        b_g, b_tr = Buf(1), Buf(1)
        t_cfm = None
        tgb = None
        for ti in range(16):
            tlo = ti * 128
            b_g.acquire(p, "pe", 0)
            p.wait("pe", t_prep)
            tgm = None
            for fc in range(NFC):
                tgm = p.op("pe", lambda e, o=bank(6)[:, 0:NE], l=tsb(fc, tlo, 128), r=gw_r[:, fc, :],
                           st=(fc == 0), sp=(fc == NFC - 1):
                           e.matmul(o, l, r, start=st, stop=sp), inc=(fc == NFC - 1))
            p.wait("ve", tgm)
            tgb = p.op("ve", lambda e: e.tensor_add(g_sb[:], bank(6)[:, 0:NE], gbb[:]), inc=True)
            b_g.release(0, tgb)
            p.wait("ve", tgb)
            tt8 = p.op("ve", lambda e: e.max(top8[:], g_sb[:]), inc=True)
            p.wait("ve", tt8)
            p.op("ve", lambda e: e.memset(m8[:], 1e30))
            p.op("ve", lambda e: e.tensor_copy(gmod[:], g_sb[:]))
            for r_ in range(4):
                tmv = p.op("ve", lambda e, i=r_: e.tensor_copy(m8[:, 0:1], top8[:, i:i + 1]), inc=True)
                p.wait("ve", tmv)
                tmr = p.op("ve", lambda e: e.match_replace(gmod[:], m8[:], gmod[:], -1e30), inc=True)
            p.wait("ve", tmr)
            tmk = p.op("ve", lambda e: e.tensor_scalar(gmask[:], gmod[:], -1e29, None, ALU.is_le), inc=True)
            tng = p.op("ve", lambda e: e.tensor_scalar(negm1[:], top8[:, 0:1], -1.0, None, ALU.mult), inc=True)
            p.waits("ae", [tng, tgb])
            tex = p.op("ae", lambda e: e.activation(comb[:], g_sb[:], AF.Exp, bias=negm1[:]), inc=True)
            p.waits("ve", [tex, tmk])
            tcu = p.op("ve", lambda e: e.tensor_mul(comb[:], comb[:], gmask[:]), inc=True)
            p.wait("ve", tcu)
            tsu = p.op("ve", lambda e: e.reduce_sum(srec[:], comb[:], AX.X), inc=True)
            p.wait("ve", tsu)
            trc = p.op("ve", lambda e: e.reciprocal(srec[:], srec[:]), inc=True)
            p.wait("ve", trc)
            tcf = p.op("ve", lambda e: e.tensor_scalar(comb[:], comb[:], srec[:], None, ALU.mult), inc=True)
            b_tr.acquire(p, "pe", 0)
            p.wait("pe", tcf)
            ttr = p.op("pe", lambda e: e.transpose(PS[0:NE, 7 * TG: 7 * TG + 128], comb[:], ident[:]), inc=True)
            p.wait("ve", ttr)
            t_cfm = p.op("ve", lambda e, o=cfm[0:NE, tlo:tlo + 128]: e.tensor_copy(o, PS[0:NE, 7 * TG: 7 * TG + 128]), inc=True)
            b_tr.release(0, t_cfm)

        p.waits("gp", [t_cfm, t_cbtmm_last])
        t_crow = p.dma("gp", crow[0:1, 0:HALF], cfm[0:1, :], "s_cr")
        p.waits("pe", [tgb, t_cfm, t_crow])
        tcbt0 = []
        for ti4 in range(4):
            t_cbtmm_last = p.op("pe", lambda e, o=bank(4 + ti4), r=crow[0:1, ti4 * 512:(ti4 + 1) * 512]:
                                e.matmul(o, onesrowb[:], r, start=True, stop=True), inc=True)
            p.wait("ve", t_cbtmm_last)
            tcc = p.op("ve", lambda e, o=cbtf[ti4], b=bank(4 + ti4): e.tensor_copy(o, b), inc=True)
            tcbt0.append(tcc)

        bb2 = Buf(2)
        bb2.release(0, tbe_last)
        bb2.release(1, tbe_last)
        ta2_pb = {}
        p.wait("pe", t_cfm)
        for oc in range(NFC):
            for tgi in range(4):
                pb = (oc * 4 + tgi) % 2
                bb2.acquire(p, "pe", pb)
                tb2 = p.op("pe", lambda e, o=bank(pb), l=b2bf[:, oc * 128:(oc + 1) * 128],
                           r=cfm[0:NE, tgi * TG:(tgi + 1) * TG]:
                           e.matmul(o, l, r, start=True, stop=True), inc=True)
                p.wait("ve", tb2)
                ta2 = p.op("ve", lambda e, o=acc(oc, tgi * TG, TG), b=bank(pb):
                           e.tensor_add(o, o, b), inc=True)
                bb2.release(pb, ta2)
                ta2_pb[pb] = ta2

        if debug_stage == 4 and half == 0:
            p.waits("sy", p.now("ve"))
            for nm, tn in (("dbg_g", g_sb), ("dbg_top8", top8), ("dbg_gmod", gmod),
                           ("dbg_gmask", gmask), ("dbg_comb", comb)):
                p.dma("sy", nc.dram_tensor(nm, list(tn.shape), F32, kind="ExternalOutput").ap(),
                      tn[:], "s_dbg")
            p.dma("sy", nc.dram_tensor("dbg_acc", [128, 16384], F32, kind="ExternalOutput").ap(),
                  A1.ap(), "s_dbg")
            p.dma("sy", nc.dram_tensor("dbg_t", [128, 16384], BF16, kind="ExternalOutput").ap(),
                  A3b[:, 0:16384], "s_dbg")
            p.dma("sy", nc.dram_tensor("dbg_cfm", [NE, HALF], BF16, kind="ExternalOutput").ap(),
                  cfm[:], "s_dbg")
            _final(p)
            return _wire(nc, p)

        b_set = Buf(2)
        b_set.release(0, ta2_pb.get(0), ta2_pb.get(1), tbe_last)
        b_set.release(1, *(tcbt0 + [tgb, t_cfm]))
        t_w1l, t_w2l = {}, {}
        for i in range(3):
            p.wait("gp", t_w1cons[i])
            t_w1l[i] = p.dma("gp", w1buf[i], w1n[0, i], "s_w1")
        for i in range(2):
            p.wait("gp", t_w2cons[i])
            t_w2l[i] = p.dma("gp", w2buf[i], w2n[0, i], "s_w2")
        gctr = 0
        t_hw_done = {0: None, 1: None}
        for e_ in range(_NE_RUN[0]):
            for jj in range(NFC):
                idx = e_ * 8 + jj
                for pr in range(2):
                    g = gctr
                    gctr += 1
                    gpar = g % 2
                    base = gpar * 4
                    b_set.acquire(p, "pe", g)
                    if pr == 0:
                        p.wait("pe", t_w1l[idx])
                        if idx == 0:
                            p.wait("pe", t_prep)
                    tgl = None
                    for t2 in range(2):
                        for fc in range(NFC):
                            tgl = p.op("pe", lambda e, o=bank(base + t2), l=w1r[idx % 3][:, fc, 0:128],
                                       r=tsb(fc, (pr * 2 + t2) * TG, TG), st=(fc == 0), sp=(fc == NFC - 1):
                                       e.matmul(o, l, r, start=st, stop=sp), inc=(fc == NFC - 1 and t2 == 1))
                    tll = None
                    for t2 in range(2):
                        for fc in range(NFC):
                            tll = p.op("pe", lambda e, o=bank(base + 2 + t2), l=w1r[idx % 3][:, fc, 128:256],
                                       r=tsb(fc, (pr * 2 + t2) * TG, TG), st=(fc == 0), sp=(fc == NFC - 1):
                                       e.matmul(o, l, r, start=st, stop=sp), inc=(fc == NFC - 1 and t2 == 1))
                    sS, sG, q1 = sSc[gpar], sGc[gpar], q1c[gpar]
                    bsap = b1sig[:, e_ * 8 + jj: e_ * 8 + jj + 1]
                    b1ap = b1pp[:, e_ * 16 + jj: e_ * 16 + jj + 1]
                    b1lap = b1lin1[:, e_ * 8 + jj: e_ * 8 + jj + 1]
                    p.wait("ae", tgl)
                    p.wait("ae", last_p_ve[gpar])
                    tsS = p.op("ae", lambda e, o=sS, b=gbank(base), bi=bsap:
                               e.activation(o, b, AF.Sigmoid, bias=bi, scale=1.702), inc=True)
                    p.wait("ve", tsS)
                    p.op("ve", lambda e, o=sG, b=gbank(base), bi=b1ap:
                         e.tensor_scalar(o, b, bi, 7.0, ALU.add, ALU.min))
                    tp = p.op("ve", lambda e, o=sG, i=sS, g2=sG:
                              e.scalar_tensor_tensor(o, i, SIGC, g2, ALU.min, ALU.mult), inc=True)
                    last_p_ve[gpar] = tp
                    p.wait("ve", tll)
                    tq1 = p.op("ve", lambda e, o=q1, b=lbank(base), bi=b1lap:
                               e.tensor_scalar(o, b, bi, 8.0, ALU.add, ALU.min), inc=True)
                    p.op("ve", lambda e, o=q1, i=q1, g2=sG:
                         e.scalar_tensor_tensor(o, i, -6.0, g2, ALU.max, ALU.mult))
                    if jj == 0:
                        p.wait("ve", t_w2rd[pr])
                    thw = p.op("ve", lambda e, o=hsl2(jj, pr), i=q1, c=cbtf2[pr]:
                               e.tensor_mul(o, i, c), inc=True)
                    t_hw_done[pr] = thw
                    b_set.release(g, tsS, tq1)
                    if pr == 1:
                        nxt = idx + 3
                        if nxt < NE * NFC:
                            p.wait("gp", tll)
                            t_w1l[nxt] = p.dma("gp", w1buf[nxt % 3], w1n[nxt // 8, nxt % 8], "s_w1")
                        t_w1cons[idx % 3] = tll
            t_crow_next = None
            if e_ < NE - 1:
                p.wait("gp", t_cbtmm_last)
                t_crow_next = p.dma("gp", crow[0:1, 0:HALF], cfm[e_ + 1:e_ + 2, :], "s_cr")
            for oc in range(NFC):
                idx2 = e_ * 8 + oc
                for pr in range(2):
                    g = gctr
                    gctr += 1
                    base = (g % 2) * 4
                    b_set.acquire(p, "pe", g)
                    if pr == 0:
                        p.wait("pe", t_w2l[idx2])
                    if oc == 0:
                        p.wait("pe", t_hw_done[pr])
                    tww = None
                    for fc2 in range(NFC):
                        for t2 in range(2):
                            tww = p.op("pe", lambda e, o=bank(base + t2), l=w2r[idx2 % 2][:, fc2, :],
                                       r=hsl(fc2, pr * 2 + t2), st=(fc2 == 0), sp=(fc2 == NFC - 1):
                                       e.matmul(o, l, r, start=st, stop=sp), inc=(fc2 == NFC - 1 and t2 == 1))
                    rel2 = []
                    if e_ < NE - 1 and pr == 1 and oc < 4:
                        p.wait("pe", t_crow_next)
                        t_cbtmm_last = p.op("pe", lambda e, o=bank(base + 2), r=crow[0:1, oc * 512:(oc + 1) * 512]:
                                            e.matmul(o, onesrowb[:], r, start=True, stop=True), inc=True)
                        p.wait("ve", t_cbtmm_last)
                        tcc = p.op("ve", lambda e, o=cbtf[oc], b=bank(base + 2): e.tensor_copy(o, b), inc=True)
                        rel2.append(tcc)
                    p.wait("ve", tww)
                    ta = p.op("ve", lambda e, o=acc(oc, pr * 2 * TG, 2 * TG), b=gbank(base):
                              e.tensor_add(o, o, b), inc=True)
                    rel2.append(ta)
                    if oc == NFC - 1:
                        t_w2rd[pr] = tww
                    b_set.release(g, *rel2)
                    if pr == 1:
                        nxt2 = idx2 + 2
                        if nxt2 < NE * NFC:
                            p.wait("gp", tww)
                            t_w2l[nxt2] = p.dma("gp", w2buf[nxt2 % 2], w2n[nxt2 // 8, nxt2 % 8], "s_w2")
                        t_w2cons[idx2 % 2] = tww
        p.waits("sy", p.now("ve"))
        t_out_prev = p.dma("sy", out_r[:, :, hlo:hlo + HALF],
                           A1.ap().rearrange("p (c t) -> p c t", c=NFC), "s_out")
    _final(p)
    return _wire(nc, p)


def _wire(nc, p):
    with nc.Block() as block:
        @block.tensor
        def _(t):
            for f in p.st["pe"]:
                f(t)

        @block.vector
        def _(v):
            for f in p.st["ve"]:
                f(v)

        @block.scalar
        def _(s2):
            for f in p.st["ae"]:
                f(s2)

        @block.sync
        def _(s2):
            for f in p.st["sy"]:
                f(s2)

        @block.gpsimd
        def _(g):
            for f in p.st["gp"]:
                f(g)
    return nc, p


_CACHE = {}


def _w1_layout(w1, perm):
    A = w1[:, perm, :].reshape(NE, 16, 128, NFC, 128)
    out = np.empty((NE, 8, 128, NFC, 256), np.float32)
    out[:, :, :, :, 0:128] = A[:, 0:8].transpose(0, 1, 4, 3, 2)
    out[:, :, :, :, 128:256] = A[:, 8:16].transpose(0, 1, 4, 3, 2)
    return np.ascontiguousarray(out.reshape(NE, 8, 128, 2048))


def _w2_layout(w2):
    B = w2.reshape(NE, 8, 128, NFC, 128)
    return np.ascontiguousarray(B.transpose(0, 1, 4, 3, 2).reshape(NE, 8, 128, 1024))


def _host_prep(inputs):
    x = np.asarray(inputs["L0_attn_x"], np.float32)
    Wq = np.asarray(inputs["L0_attn_Wq"], np.float32)
    Wk = np.asarray(inputs["L0_attn_Wk"], np.float32)
    Wv = np.asarray(inputs["L0_attn_Wv"], np.float32)
    Wo = np.asarray(inputs["L0_attn_Wo"], np.float32)
    nscale = np.asarray(inputs["L0_ffn_norm_scale"], np.float32)
    gate_w = np.asarray(inputs["L0_ffn_gate_w"], np.float32)
    gate_b = np.asarray(inputs["L0_ffn_gate_b"], np.float32)
    w1 = np.asarray(inputs["L0_ffn_w1"], np.float32)
    b1 = np.asarray(inputs["L0_ffn_b1"], np.float32)
    w2 = np.asarray(inputs["L0_ffn_w2"], np.float32)
    b2 = np.asarray(inputs["L0_ffn_b2"], np.float32)

    perm = np.concatenate([np.arange(0, 2048, 2), np.arange(1, 2048, 2)])
    com = {
        "wqT": np.ascontiguousarray(Wq.T),
        "wkT": np.ascontiguousarray(Wk.T),
        "wvT": np.ascontiguousarray(Wv.T),
        "woT": np.ascontiguousarray(Wo.T),
        "gwT": np.ascontiguousarray(gate_w.T),
        "gbb": np.ascontiguousarray(np.broadcast_to(gate_b, (128, NE))),
        "w1n": _w1_layout(w1, perm),
        "b1pp": np.ascontiguousarray(
            b1[:, perm].reshape(NE, 16, 128).transpose(2, 0, 1).reshape(128, NE * 16)),
        "w2n": _w2_layout(w2),
        "b2": np.ascontiguousarray(b2),
        "nsc": np.ascontiguousarray(nscale.reshape(NFC, 128).T),
    }
    inv = 1.0 / (10000.0 ** (np.arange(0, HD, 2, dtype=np.float64) / HD))
    pos = np.arange(S, dtype=np.float64)
    fr = np.outer(pos, inv)
    emb = np.concatenate([fr, fr], -1)
    cos1 = np.cos(emb).astype(np.float32).T
    sin1 = np.sin(emb).astype(np.float32).T
    com["cosb"] = np.ascontiguousarray(np.vstack([cos1, cos1]))
    com["sinb"] = np.ascontiguousarray(np.vstack([sin1, sin1]))
    mk = np.zeros((128, 4 * 1024), np.float32)
    kk = np.arange(128)[:, None]
    qq = np.arange(TG)[None, :]
    for r in range(4):
        m_ = np.where(qq >= kk + 128 * r, 0.0, -1e30)
        mk[:, r * 1024: r * 1024 + 512] = m_
        mk[:, r * 1024 + 512:(r + 1) * 1024] = m_
    com["maskb"] = mk
    rt = np.zeros((128, 128), np.float32)
    for base in (0, 64):
        for d in range(32):
            rt[base + d + 32, base + d] = -1.0
            rt[base + d, base + d + 32] = 1.0
    com["rT"] = rt
    com["ident"] = np.eye(128, dtype=np.float32)
    oc3 = np.zeros((128, 3), np.float32)
    oc3[:, 0] = 1.0
    oc3[0:64, 1] = 1.0
    oc3[64:128, 2] = 1.0
    com["onescol"] = oc3
    com["onescolb"] = oc3
    com["onesrow"] = np.ones((1, 128), np.float32)
    com["onesrowb"] = np.ones((1, 128), np.float32)
    cv = np.zeros((128, 4), np.float32)
    cv[:, 1] = NEPS
    com["cv"] = cv

    in_maps = []
    for b in range(x.shape[0]):
        m = dict(com)
        m["xT"] = np.ascontiguousarray(x[b].T)
        in_maps.append(m)
    return in_maps


def kernel(**inputs):
    in_maps = _host_prep(inputs)
    if "nc" not in _CACHE:
        _CACHE["nc"], _ = build_program()
    res = run_bass_kernel_spmd(_CACHE["nc"], in_maps, core_ids=list(range(8)))
    outs = [np.ascontiguousarray(np.asarray(r["outT"]).T) for r in res.results]
    return np.stack(outs, axis=0)


# version 10
# speedup vs baseline: 1.0893x; 1.0387x over previous
import numpy as np
import ml_dtypes
import concourse.bass as bass
import concourse.mybir as mybir
from concourse.bass_utils import run_bass_kernel_spmd

F32 = mybir.dt.float32
BF16 = mybir.dt.bfloat16
AF = mybir.ActivationFunctionType
ALU = mybir.AluOpType
AX = mybir.AxisListType

S = 4096
H = 1024
NH, HD = 16, 64
NE = 32
TG = 512
NTG = S // TG
NFC = H // 128
HALF = S // 2
NEPS = 1e-5
SIGC = float(1.0 / (1.0 + np.exp(-1.702 * 7.0)))

ENG = ("pe", "ve", "ae", "sy", "gp")
_NE_RUN = [NE]
_FLAGS = {"pool": True, "cbtw2": True}


class Prog:
    def __init__(self, nc):
        self.nc = nc
        self.st = {e: [] for e in ENG}
        self.sems = {}
        self.cnt = {}

    def sem(self, name):
        if name not in self.sems:
            self.sems[name] = self.nc.alloc_semaphore(name)
            self.cnt[name] = 0
        return self.sems[name]

    def em(self, eng, f):
        self.st[eng].append(f)

    def wait(self, eng, ticket):
        if ticket is None:
            return
        name, val = ticket
        if val <= 0:
            return
        s = self.sem(name)
        self.em(eng, lambda e, s=s, v=val: e.wait_ge(s, v))

    def waits(self, eng, tickets):
        for t in tickets:
            self.wait(eng, t)

    def dma(self, q, out_ap, in_ap, semname):
        s = self.sem(semname)
        self.cnt[semname] += 16
        v = self.cnt[semname]
        self.em(q, lambda e, o=out_ap, i=in_ap, s=s: e.dma_start(out=o, in_=i).then_inc(s, 16))
        return (semname, v)

    def op(self, eng, fn, inc=False):
        if not inc:
            self.em(eng, lambda e, f=fn: f(e))
            return None
        s = self.sem(eng)
        self.cnt[eng] += 1
        v = self.cnt[eng]
        self.em(eng, lambda e, f=fn, s=s: f(e).then_inc(s, 1))
        return (eng, v)

    def now(self, *sems):
        return [(s, self.cnt.get(s, 0)) for s in (sems or self.cnt.keys())]


class Buf:
    def __init__(self, n):
        self.n = n
        self.free = [[] for _ in range(n)]

    def acquire(self, p, eng, i):
        for t in self.free[i % self.n]:
            p.wait(eng, t)
        self.free[i % self.n] = []

    def release(self, i, *tickets):
        self.free[i % self.n] += [t for t in tickets if t is not None]


def _final(p):
    for name, v in list(p.cnt.items()):
        if name in ENG:
            continue
        p.wait("sy", (name, v))


def build_program(debug_stage=0):
    nc = bass.Bass("TRN2", target_bir_lowering=False)
    p = Prog(nc)

    api = lambda name, shape, dt: nc.dram_tensor(name, shape, dt, kind="ExternalInput").ap()
    xT = api("xT", [H, S], F32)
    wqT = api("wqT", [H, H], F32)
    wkT = api("wkT", [H, H], F32)
    wvT = api("wvT", [H, H], F32)
    woT = api("woT", [H, H], F32)
    gwT = api("gwT", [H, NE], F32)
    gbb_h = api("gbb", [128, NE], F32)
    w1n = api("w1n", [NE, 8, 128, 2048], F32)
    b1pp_h = api("b1pp", [128, NE * 16], F32)
    w2n = api("w2n", [NE, 8, 128, 1024], F32)
    b2_h = api("b2", [NE, H], F32)
    nsc_h = api("nsc", [128, NFC], F32)
    cos_h = api("cosb", [128, S], F32)
    sin_h = api("sinb", [128, S], F32)
    mask_h = api("maskb", [128, 4 * 1024], F32)
    rT_h = api("rT", [128, 128], F32)
    ident_h = api("ident", [128, 128], F32)
    onescol_h = api("onescol", [128, 3], F32)
    onescolb_h = api("onescolb", [128, 3], F32)
    onesrow_h = api("onesrow", [1, 128], F32)
    onesrowb_h = api("onesrowb", [1, 128], F32)
    cv_h = api("cv", [128, 4], F32)

    outT = nc.dram_tensor("outT", [H, S], F32, kind="ExternalOutput").ap()

    def scratch(name, shape, dt, stage):
        kind = "ExternalOutput" if (0 < debug_stage == stage) else "Internal"
        return nc.dram_tensor(name, shape, dt, kind=kind).ap()
    q_s = scratch("q_s", [NH // 2, 128, S], BF16, 1)
    k_s = scratch("k_s", [NH // 2, 128, S], BF16, 1)
    v_s = scratch("v_s", [S, H], BF16, 1)
    o_s = scratch("o_s", [NH // 2, 128, S], BF16, 2)
    x2_s = scratch("x2_s", [H, S], F32, 3)

    A1 = nc.alloc_sbuf_tensor("A1", [128, 16384], F32)
    A2 = nc.alloc_sbuf_tensor("A2", [128, 12320], F32)
    A3 = nc.alloc_sbuf_tensor("A3", [128, 8192], F32)
    A4 = nc.alloc_sbuf_tensor("A4", [128, 8192], F32)
    A1b, A2b, A3b, A4b = (a.bitcast(BF16) for a in (A1, A2, A3, A4))
    masks2 = A3[:, 3072:7168]
    nrow = A3[0:1, 7168:7424]
    cvt = nc.alloc_sbuf_tensor("cvt_sb", [128, 4], F32)
    nsc = nc.alloc_sbuf_tensor("nsc_sb", [128, NFC], F32)
    gbb = nc.alloc_sbuf_tensor("gbb_sb", [128, NE], F32)
    b1pp = nc.alloc_sbuf_tensor("b1pp_sb", [128, NE * 16], F32)
    b1sig = nc.alloc_sbuf_tensor("b1sig_sb", [128, NE * 8], F32)
    b1lin1 = nc.alloc_sbuf_tensor("b1lin1_sb", [128, NE * 8], F32)
    b2bf = nc.alloc_sbuf_tensor("b2bf_sb", [NE, H], BF16)
    gw_sb = nc.alloc_sbuf_tensor("gw_sb", [128, NFC * NE], BF16)
    rT = nc.alloc_sbuf_tensor("rT_sb", [128, 128], BF16)
    ident = nc.alloc_sbuf_tensor("ident_sb", [128, 128], F32)
    onescol = nc.alloc_sbuf_tensor("onescol_sb", [128, 3], F32)
    onescolb = nc.alloc_sbuf_tensor("onescolb_sb", [128, 3], BF16)
    onesrow = nc.alloc_sbuf_tensor("onesrow_sb", [1, 128], F32)
    onesrowb = nc.alloc_sbuf_tensor("onesrowb_sb", [1, 128], BF16)
    rows2 = nc.alloc_sbuf_tensor("rows2", [1, 3072], F32)
    nb_sb = nc.alloc_sbuf_tensor("nb_sb", [128, 16], F32)
    cfm = nc.alloc_sbuf_tensor("cfm", [NE, HALF], BF16)
    crow = nc.alloc_sbuf_tensor("crow", [1, HALF], BF16)
    g_sb = nc.alloc_sbuf_tensor("g_sb", [128, NE], F32)
    m8 = nc.alloc_sbuf_tensor("m8", [128, 8], F32)
    top8 = nc.alloc_sbuf_tensor("top8", [128, 8], F32)
    gmod = nc.alloc_sbuf_tensor("gmod", [128, NE], F32)
    gmask = nc.alloc_sbuf_tensor("gmask", [128, NE], F32)
    negm1 = nc.alloc_sbuf_tensor("negm1", [128, 1], F32)
    srec = nc.alloc_sbuf_tensor("srec", [128, 1], F32)
    comb = nc.alloc_sbuf_tensor("comb", [128, NE], F32)
    rows_d = nc.dram_tensor("rows_d", [8, S], F32, kind="Internal").ap()

    nc.const_aps.register(F32, 0.0, cvt[:, 0:1])
    nc.const_aps.register(F32, NEPS, cvt[:, 1:2])

    PS = nc.alloc_psum_tensor("PS", [128, 8 * TG], F32)

    def bank(i, n=1, p0=0, p1=128):
        return PS[p0:p1, i * TG:(i + n) * TG]

    (R_A2, R_B2, R_A3, R_B3) = range(4)

    for dst, src in ((cvt, cv_h), (nsc, nsc_h), (gbb, gbb_h), (b1pp, b1pp_h),
                     (ident, ident_h), (onescol, onescol_h),
                     (onesrow, onesrow_h), (masks2, mask_h)):
        p.dma("sy", dst[:], src, "s_init")
    for dst, src in ((rT, rT_h), (onescolb, onescolb_h), (onesrowb, onesrowb_h)):
        p.dma("gp", dst[:], src, "s_initg")
    p.dma("gp", gw_sb.ap().rearrange("p (c n) -> p c n", c=NFC),
          gwT.rearrange("(c p) n -> p c n", p=128), "s_initg")
    p.dma("gp", b2bf[:], b2_h, "s_initg")
    T_INIT = [("s_init", p.cnt["s_init"]), ("s_initg", p.cnt["s_initg"])]

    b1r = b1pp.ap().rearrange("p (e c) -> p e c", e=NE)
    p.waits("ve", T_INIT)
    p.op("ve", lambda e: e.tensor_scalar(
        b1sig.ap().rearrange("p (e c) -> p e c", e=NE), b1r[:, :, 0:8], 1.702, None, ALU.mult))
    p.op("ve", lambda e: e.tensor_scalar(
        b1lin1.ap().rearrange("p (e c) -> p e c", e=NE), b1r[:, :, 8:16], 1.0, None, ALU.add), inc=True)

    ln1c = lambda fc, lo=0, n=TG: A1b[:, fc * 512 + lo: fc * 512 + lo + n]
    xcell = [A1[:, 2048 + i * 512: 2048 + (i + 1) * 512] for i in range(3)]
    sqcell = [A1[:, 3584 + i * 512: 3584 + (i + 1) * 512] for i in range(2)]
    vw = [A2b[:, i * 4096:(i + 1) * 4096] for i in range(2)]
    wqk = [A2b[:, 8192 + i * 1024: 8192 + (i + 1) * 1024] for i in range(2)]
    cosc = [A2b[:, 10240 + i * 512: 10240 + (i + 1) * 512] for i in range(2)]
    sinc = [A2b[:, 11264 + i * 512: 11264 + (i + 1) * 512] for i in range(2)]
    vcell = [A2b[:, 12288 + i * 512: 12288 + (i + 1) * 512] for i in range(4)]
    qraw = [A2b[:, 14336 + i * 512: 14336 + (i + 1) * 512] for i in range(2)]
    qrope = [A2b[:, 15360 + i * 512: 15360 + (i + 1) * 512] for i in range(2)]
    sqro = [A2b[:, 16384 + i * 512: 16384 + (i + 1) * 512] for i in range(2)]
    qcc = [A2[:, 8704 + i * 512: 8704 + (i + 1) * 512] for i in range(2)]
    rsc = [A2[:, 9728 + i * 512: 9728 + (i + 1) * 512] for i in range(2)]
    cbA = A2[:, 10752:11264]
    cbB = A2[:, 11264:11776]

    xT_r = xT.rearrange("(c p) t -> c p t", p=128)
    wq_r = wqT.rearrange("(c p) n -> p c n", p=128)
    wk_r = wkT.rearrange("(c p) n -> p c n", p=128)
    wv_r = wvT.rearrange("(c p) n -> p c n", p=128)

    bx, bsq = Buf(3), Buf(2)
    bwqk, bvw, bvc = Buf(2), Buf(2), Buf(4)
    bqraw, bqc, bqro, bsqro, bcs = Buf(2), Buf(2), Buf(2), Buf(2), Buf(2)
    b_proj, b_rot, b_stat, b_vps = Buf(2), Buf(2), Buf(1), Buf(2)
    b_ln1, b_cb = Buf(1), Buf(1)
    bnst = Buf(2)

    p.waits("pe", T_INIT)
    p.waits("ae", T_INIT)
    xi = 0
    for tg in range(NTG):
        lo = tg * TG
        cs = tg % 2
        bcs.acquire(p, "gp", cs)
        t_c1 = p.dma("gp", cosc[cs], cos_h[:, lo:lo + TG], f"s_cs{cs}")
        t_c2 = p.dma("gp", sinc[cs], sin_h[:, lo:lo + TG], f"s_cs{cs}")
        b_stat.acquire(p, "pe", 0)
        for fc in range(NFC):
            sl = xi % 3
            xi += 1
            bx.acquire(p, "sy", sl)
            tx = p.dma("sy", xcell[sl], xT_r[fc, :, lo:lo + TG], f"s_x{sl}")
            ss = fc % 2
            bsq.acquire(p, "ae", ss)
            p.wait("ae", tx)
            tq = p.op("ae", lambda e, o=sqcell[ss], i=xcell[sl]: e.activation(o, i, AF.Square), inc=True)
            p.wait("pe", tx)
            p.op("pe", lambda e, i=xcell[sl], st=(fc == 0), sp=(fc == NFC - 1):
                 e.matmul(bank(2, p1=1), onescol[:, 0:1], i, start=st, stop=sp))
            p.wait("pe", tq)
            tmm = p.op("pe", lambda e, i=sqcell[ss], st=(fc == 0), sp=(fc == NFC - 1):
                       e.matmul(bank(3, p1=1), onescol[:, 0:1], i, start=st, stop=sp), inc=True)
            bx.release(sl, tmm)
            bsq.release(ss, tmm)
        p.wait("ve", tmm)
        p.op("ve", lambda e: e.tensor_scalar(rows2[0:1, 0:512], bank(2, p1=1), 1.0 / H, None, ALU.mult))
        p.op("ve", lambda e: e.tensor_scalar(rows2[0:1, 512:1024], bank(3, p1=1), 1.0 / H, None, ALU.mult))
        p.op("ve", lambda e: e.tensor_mul(rows2[0:1, 1024:1536], rows2[0:1, 0:512], rows2[0:1, 0:512]))
        tvar = p.op("ve", lambda e: e.tensor_sub(rows2[0:1, 1536:2048], rows2[0:1, 512:1024], rows2[0:1, 1024:1536]), inc=True)
        b_stat.release(0, tvar)
        p.wait("ae", tvar)
        p.op("ae", lambda e: e.activation(rows2[0:1, 1024:1536], rows2[0:1, 1536:2048], AF.Ln, bias=NEPS))
        trs = p.op("ae", lambda e: e.activation(rows2[0:1, 2048:2560], rows2[0:1, 1024:1536], AF.Exp, scale=-0.5), inc=True)
        p.wait("ve", trs)
        tbr = p.op("ve", lambda e:
                   e.scalar_tensor_tensor(rows2[0:1, 2560:3072], rows2[0:1, 0:512], -1.0, rows2[0:1, 2048:2560], ALU.mult, ALU.mult), inc=True)
        b_stat.acquire(p, "pe", 0)
        p.wait("pe", tbr)
        p.op("pe", lambda e: e.matmul(bank(2), onesrow[:], rows2[0:1, 2048:2560], start=True, stop=True))
        tbc = p.op("pe", lambda e: e.matmul(bank(3), onesrow[:], rows2[0:1, 2560:3072], start=True, stop=True), inc=True)
        b_cb.acquire(p, "ve", 0)
        p.wait("ve", tbc)
        p.op("ve", lambda e: e.tensor_copy(cbA, bank(2)))
        tcb = p.op("ve", lambda e: e.tensor_copy(cbB, bank(3)), inc=True)
        b_stat.release(0, tcb)
        b_ln1.acquire(p, "ve", 0)
        for fc in range(NFC):
            sl = xi % 3
            xi += 1
            bx.acquire(p, "sy", sl)
            tx = p.dma("sy", xcell[sl], xT_r[fc, :, lo:lo + TG], f"s_x{sl}")
            p.wait("ve", tx)
            p.op("ve", lambda e, i=xcell[sl], o=rsc[fc % 2]: e.tensor_mul(o, i, cbA))
            tl = p.op("ve", lambda e, o=ln1c(fc), i=rsc[fc % 2]: e.tensor_add(o, i, cbB), inc=True)
            bx.release(sl, tl)
        t_ln1 = tl
        b_cb.release(0, t_ln1)

        for og in range(2):
            ws = og % 2
            bvw.acquire(p, "gp", ws)
            tw = p.dma("gp", vw[ws].rearrange("p (c n) -> p c n", c=NFC),
                       wv_r[:, :, og * 512:(og + 1) * 512], f"s_vw{ws}")
            p.wait("pe", tw)
            p.wait("pe", t_ln1)
            for tt in range(4):
                vb = tt % 2
                b_vps.acquire(p, "pe", vb)
                for fc in range(NFC):
                    tpm = p.op("pe", lambda e, l=ln1c(fc, tt * 128, 128),
                               r=vw[ws].rearrange("p (c n) -> p c n", c=NFC)[:, fc, :],
                               o=bank(4 + vb), st=(fc == 0), sp=(fc == NFC - 1):
                               e.matmul(o, l, r, start=st, stop=sp), inc=(fc == NFC - 1))
                vc = (og * 4 + tt) % 4
                bvc.acquire(p, "ve", vc)
                p.wait("ve", tpm)
                tve = p.op("ve", lambda e, o=vcell[vc], b=bank(4 + vb): e.tensor_copy(o, b), inc=True)
                b_vps.release(vb, tve)
                p.wait("sy", tve)
                tvd = p.dma("sy", v_s[lo + tt * 128: lo + (tt + 1) * 128, og * 512:(og + 1) * 512],
                            vcell[vc], f"s_vs{vc}")
                bvc.release(vc, tvd)
            bvw.release(ws, tpm)

        for wi, (wh_r, dst) in enumerate(((wq_r, q_s), (wk_r, k_s))):
            for oc in range(NFC):
                ws = oc % 2
                bwqk.acquire(p, "gp", ws)
                tw = p.dma("gp", wqk[ws].rearrange("p (c n) -> p c n", c=NFC),
                           wh_r[:, :, oc * 128:(oc + 1) * 128], f"s_w{ws}")
                pb = oc % 2
                b_proj.acquire(p, "pe", pb)
                p.wait("pe", tw)
                if oc == 0 and wi == 0:
                    p.wait("pe", t_ln1)
                for fc in range(NFC):
                    tpm = p.op("pe", lambda e, l=wqk[ws].rearrange("p (c n) -> p c n", c=NFC)[:, fc, :],
                               r=ln1c(fc), o=bank(pb), st=(fc == 0), sp=(fc == NFC - 1):
                               e.matmul(o, l, r, start=st, stop=sp), inc=(fc == NFC - 1))
                bwqk.release(ws, tpm)
                qs2 = oc % 2
                bqraw.acquire(p, "ae", qs2)
                p.wait("ae", tpm)
                tqr = p.op("ae", lambda e, o=qraw[qs2], b=bank(pb): e.copy(o, b), inc=True)
                bqc.acquire(p, "ve", qs2)
                p.wait("ve", tqr)
                if oc < 2 and wi == 0:
                    p.waits("ve", [t_c1, t_c2])
                tqc = p.op("ve", lambda e, o=qcc[qs2], b=bank(pb), c=cosc[cs]: e.tensor_mul(o, b, c), inc=True)
                b_proj.release(pb, tqc)
                rb_ = oc % 2
                b_rot.acquire(p, "pe", rb_)
                p.wait("pe", tqc)
                trm = p.op("pe", lambda e, o=bank(6 + rb_), i=qraw[qs2]:
                           e.matmul(o, rT[:], i, start=True, stop=True), inc=True)
                bqraw.release(qs2, trm)
                bqro.acquire(p, "ve", qs2)
                p.wait("ve", trm)
                p.op("ve", lambda e, o=rsc[qs2], b=bank(6 + rb_), s2=sinc[cs]: e.tensor_mul(o, b, s2))
                tqo = p.op("ve", lambda e, o=qrope[qs2], a=qcc[qs2], b=rsc[qs2]: e.tensor_add(o, a, b), inc=True)
                b_rot.release(rb_, tqo)
                bqc.release(qs2, tqo)
                bsqro.acquire(p, "ae", qs2)
                p.wait("ae", tqo)
                tq2 = p.op("ae", lambda e, o=sqro[qs2], i=qrope[qs2]: e.activation(o, i, AF.Square), inc=True)
                b_stat.acquire(p, "pe", 0)
                p.wait("pe", tq2)
                p.op("pe", lambda e, i=sqro[qs2]: e.matmul(bank(2, p1=1), onescolb[:, 1:2], i, start=True, stop=True))
                tnm = p.op("pe", lambda e, i=sqro[qs2]: e.matmul(bank(3, p1=1), onescolb[:, 2:3], i, start=True, stop=True), inc=True)
                bsqro.release(qs2, tnm)
                p.wait("ve", tnm)
                p.op("ve", lambda e, o=nrow[0:1, wi * 128 + 2 * oc * 8 + tg: wi * 128 + 2 * oc * 8 + tg + 1]:
                     e.tensor_reduce(o, bank(2, p1=1), AX.X, ALU.max))
                tne = p.op("ve", lambda e, o=nrow[0:1, wi * 128 + (2 * oc + 1) * 8 + tg: wi * 128 + (2 * oc + 1) * 8 + tg + 1]:
                           e.tensor_reduce(o, bank(3, p1=1), AX.X, ALU.max), inc=True)
                b_stat.release(0, tne)
                p.wait("sy", tqo)
                tqd = p.dma("sy", dst[oc, :, lo:lo + TG], qrope[qs2][:, :], f"s_qk{qs2}")
                bqro.release(qs2, tqd)
            if wi == 1:
                b_ln1.release(0, tpm)
        bcs.release(cs, tqo)

    if debug_stage == 1:
        _final(p)
        return _wire(nc, p)

    PH1 = p.now()

    qh2 = [A1b[:, i * 4096:(i + 1) * 4096] for i in range(2)]
    kh2 = [A1b[:, 8192 + i * 4096: 8192 + (i + 1) * 4096] for i in range(2)]
    vh2 = [[A1b[:, 16384 + (2 * i + j) * 2080: 16384 + (2 * i + j + 1) * 2080]
            for j in range(2)] for i in range(2)]
    pcell = [A1b[:, 24704 + i * 512: 24704 + (i + 1) * 512] for i in range(4)]
    pcp2 = [A1b[:, 24704 + i * 1024: 24704 + (i + 1) * 1024] for i in range(2)]
    osbc = [A1b[:, 26752 + i * 512: 26752 + (i + 1) * 512] for i in range(4)]
    nstage = [A2[0:1, i * 2048:(i + 1) * 2048] for i in range(2)]
    rqrow = A2[0:1, 4096:4128]
    rkrow = A2[0:1, 4128:4160]
    nbsc = A2[0:1, 4224:4288]
    zraw = A2[0:1, 4608:5632]
    zrecb = A2b[0:1, 11264:12288]
    zln = A2[0:1, 8704:9728]
    rbc = [A2[:, 6656 + i * 512: 6656 + (i + 1) * 512] for i in range(2)]
    msb = [A2[:, 7680 + i * 512: 7680 + (i + 1) * 512] for i in range(2)]
    msb2 = A2[:, 7680:8704]

    msbp = [A2[:, 9728 + i * 1024: 9728 + (i + 1) * 1024] for i in range(2)]
    bqk, bvh = Buf(2), Buf(2)
    bpcp, bmsb2, bosb = Buf(2), Buf(2), Buf(2)
    bavz, bnst = Buf(2), Buf(2)
    sguard = {0: [], 1: [], 2: [], 3: []}
    rot = [0]

    def claim_slot(extra=()):
        r = rot[0] % 2
        rot[0] += 1
        bA, bB = 2 * r, 2 * r + 1
        p.waits("pe", sguard[bA] + sguard[bB] + [t for t in extra if t is not None])
        sguard[bA], sguard[bB] = [], []
        return bA, bB

    for eng in ENG:
        p.waits(eng, PH1)

    pend = [None]
    tzeh, trbh = [None], [None]
    tmx = None
    for hh in range(NH):
        p.op("ve", lambda e, o=nbsc[0:1, hh:hh + 1], i=nrow[0:1, hh * 8:(hh + 1) * 8]:
             e.tensor_reduce(o, i, AX.X, ALU.max))
        tmx = p.op("ve", lambda e, o=nbsc[0:1, 16 + hh:17 + hh],
                   i=nrow[0:1, 128 + hh * 8: 128 + (hh + 1) * 8]:
                   e.tensor_reduce(o, i, AX.X, ALU.max), inc=(hh == NH - 1))
    p.wait("ve", tmx)
    tml = p.op("ve", lambda e: e.tensor_mul(nbsc[0:1, 32:48], nbsc[0:1, 0:16], nbsc[0:1, 16:32]), inc=True)
    p.wait("ve", tml)
    tpm = None
    for pj in range(NH // 2):
        tpm = p.op("ve", lambda e, o=nbsc[0:1, 48 + pj:49 + pj], a=nbsc[0:1, 32 + 2 * pj:33 + 2 * pj],
                   b=nbsc[0:1, 33 + 2 * pj:34 + 2 * pj]: e.tensor_max(o, a, b), inc=(pj == NH // 2 - 1))
    p.wait("ae", tpm)
    p.op("ae", lambda e: e.activation(nbsc[0:1, 56:64], nbsc[0:1, 48:56], AF.Ln))
    tsq = p.op("ae", lambda e: e.activation(nbsc[0:1, 32:40], nbsc[0:1, 56:64], AF.Exp, scale=0.5), inc=True)
    p.wait("ve", tsq)
    tnb = p.op("ve", lambda e, o=A2b[0:1, 8576:8584]:
               e.tensor_scalar(o, nbsc[0:1, 32:40], -0.125, None, ALU.mult), inc=True)
    rA, rB = claim_slot([tnb])
    tnm = p.op("pe", lambda e, o=bank(rA)[:, 0:8]: e.matmul(o, onesrowb[:], A2b[0:1, 8576:8584],
                                                            start=True, stop=True), inc=True)
    p.wait("ve", tnm)
    tnbc = p.op("ve", lambda e, o=nb_sb[:, 0:8]: e.tensor_copy(o, bank(rA)[:, 0:8]), inc=True)
    sguard[rA].append(tnbc)

    for pp in range(NH // 2):
        s = pp % 2
        bqk.acquire(p, "sy", s)
        tq = p.dma("sy", qh2[s], q_s[pp], f"s_qh{s}")
        tk = p.dma("sy", kh2[s], k_s[pp], f"s_qh{s}")
        bvh.acquire(p, "sy", s)
        vr = [vh2[s][j].rearrange("p (c d) -> p c d", d=65) for j in range(2)]
        tvh = None
        for j, hh in enumerate((2 * pp, 2 * pp + 1)):
            tvh = p.dma("sy", vr[j][:, :, 0:64],
                        v_s.rearrange("(c p) o -> p c o", p=128)[:, :, 64 * hh: 64 * hh + 64], f"s_vh{s}")
        p.wait("ve", tvh)
        tvm = None
        for j in range(2):
            tvm = p.op("ve", lambda e, o=vr[j][:, :, 64:65]: e.memset(o, 1.0), inc=(j == 1))

        nbc = nb_sb[:, pp:pp + 1]
        qk_last = av_last = None
        for qg in range(NTG):
            nkc = 4 * (qg + 1)
            qpar = qg % 2
            avA, avB = 4 + 2 * qpar, 5 + 2 * qpar
            kinfo = []

            def emit_av(kc, te, qg=qg, qpar=qpar, avA=avA, avB=avB, nkc=nkc, pp=pp, s=s):
                if kc == 0:
                    bavz.acquire(p, "pe", qpar)
                p.wait("pe", te)
                if qg == 0 and kc == 0:
                    p.waits("pe", [tvh, tvm])
                st_, sp_ = (kc == 0), (kc == nkc - 1)
                sp2 = kc % 2
                vrl = [vh2[s][j].rearrange("p (c d) -> p c d", d=65) for j in range(2)]
                p.op("pe", lambda e, o=PS[0:65, avA * TG:(avA + 1) * TG], l=vrl[0][:, kc, :],
                     r=pcell[2 * sp2], st=st_, sp=sp_: e.matmul(o, l, r, start=st, stop=sp))
                t_ = p.op("pe", lambda e, o=PS[0:65, avB * TG:(avB + 1) * TG], l=vrl[1][:, kc, :],
                          r=pcell[2 * sp2 + 1], st=st_, sp=sp_: e.matmul(o, l, r, start=st, stop=sp),
                          inc=True)
                bpcp.release(sp2, t_)
                return t_

            for kc in range(nkc):
                first = (qg == 0 and kc == 0)
                bA, bB = claim_slot([tq, tk] if first else ())
                p.op("pe", lambda e, o=bank(bA), l=kh2[s][0:64, kc * 128:(kc + 1) * 128],
                     r=qh2[s][0:64, qg * TG:(qg + 1) * TG]:
                     e.matmul(o, l, r, start=True, stop=True))
                qk_last = p.op("pe", lambda e, o=bank(bB), l=kh2[s][64:128, kc * 128:(kc + 1) * 128],
                               r=qh2[s][64:128, qg * TG:(qg + 1) * TG]:
                               e.matmul(o, l, r, start=True, stop=True), inc=True)
                sp2 = kc % 2
                bpcp.acquire(p, "ae", sp2)
                if kc >= 4 * qg:
                    r_ = kc - 4 * qg
                    bmsb2.acquire(p, "ve", sp2)
                    p.wait("ve", qk_last)
                    tmB = p.op("ve", lambda e, o=msbp[sp2], b=PS[:, bA * TG:(bB + 1) * TG],
                               m=masks2[:, r_ * 1024:(r_ + 1) * 1024]:
                               e.tensor_add(o, b, m), inc=True)
                    sguard[bA].append(tmB)
                    sguard[bB].append(tmB)
                    p.wait("ae", tmB)
                    if first:
                        p.wait("ae", tnbc)
                    te = p.op("ae", lambda e, o=pcp2[sp2], i=msbp[sp2], bi=nbc:
                              e.activation(o, i, AF.Exp, bias=bi, scale=0.125), inc=True)
                    bmsb2.release(sp2, te)
                else:
                    p.wait("ae", qk_last)
                    te = p.op("ae", lambda e, o=pcp2[sp2], b=PS[:, bA * TG:(bB + 1) * TG], bi=nbc:
                              e.activation(o, b, AF.Exp, bias=bi, scale=0.125), inc=True)
                    sguard[bA].append(te)
                    sguard[bB].append(te)
                kinfo.append(te)
                if kc >= 1:
                    av_last = emit_av(kc - 1, kinfo[kc - 1])
                if kc == 3 and pend[0] is not None:
                    pend[0]()
                    pend[0] = None
            av_last = emit_av(nkc - 1, kinfo[nkc - 1])

            def make_tail(pp=pp, qg=qg, qpar=qpar, avA=avA, avB=avB, av_last=av_last):
                p.wait("ve", av_last)
                p.wait("ve", tzeh[0])
                p.op("ve", lambda e, o=zraw[0:1, 0:512], b=PS[64:65, avA * TG:(avA + 1) * TG]:
                     e.tensor_scalar(o, b, 1e-30, None, ALU.add))
                tz2 = p.op("ve", lambda e, o=zraw[0:1, 512:1024], b=PS[64:65, avB * TG:(avB + 1) * TG]:
                           e.tensor_scalar(o, b, 1e-30, None, ALU.add), inc=True)
                p.waits("ae", [tz2, trbh[0]])
                p.op("ae", lambda e: e.activation(zln[0:1, 0:1024], zraw[0:1, 0:1024], AF.Ln))
                tze = p.op("ae", lambda e: e.activation(zrecb[0:1, 0:1024], zln[0:1, 0:1024],
                                                        AF.Exp, scale=-1.0), inc=True)
                tzeh[0] = tze

                def emit_pe():
                    rA2, rB2 = claim_slot([tze])
                    p.op("pe", lambda e, o=PS[0:64, rA2 * TG:(rA2 + 1) * TG]:
                         e.matmul(o, onesrowb[0:1, 0:64], zrecb[0:1, 0:512], start=True, stop=True))
                    trb = p.op("pe", lambda e, o=PS[0:64, rB2 * TG:(rB2 + 1) * TG]:
                               e.matmul(o, onesrowb[0:1, 0:64], zrecb[0:1, 512:1024],
                                        start=True, stop=True), inc=True)
                    trbh[0] = trb
                    p.wait("ve", trb)
                    p.op("ve", lambda e, o=rbc[0][0:64, :]: e.tensor_copy(o, PS[0:64, rA2 * TG:(rA2 + 1) * TG]))
                    trbc = p.op("ve", lambda e, o=rbc[1][0:64, :]: e.tensor_copy(o, PS[0:64, rB2 * TG:(rB2 + 1) * TG]), inc=True)
                    sguard[rA2].append(trbc)
                    sguard[rB2].append(trbc)
                    bosb.acquire(p, "ve", qpar)
                    p.op("ve", lambda e, o=osbc[2 * qpar][0:64, :], b=PS[0:64, avA * TG:(avA + 1) * TG],
                         r=rbc[0][0:64, :]: e.tensor_mul(o, b, r))
                    tosb = p.op("ve", lambda e, o=osbc[2 * qpar + 1][0:64, :], b=PS[0:64, avB * TG:(avB + 1) * TG],
                                r=rbc[1][0:64, :]: e.tensor_mul(o, b, r), inc=True)
                    bavz.release(qpar, tosb, tz2)
                    p.wait("sy", tosb)
                    p.dma("sy", o_s[pp, 0:64, qg * TG:(qg + 1) * TG], osbc[2 * qpar][0:64, :], f"s_os{qpar}")
                    tod = p.dma("sy", o_s[pp, 64:128, qg * TG:(qg + 1) * TG], osbc[2 * qpar + 1][0:64, :], f"s_os{qpar}")
                    bosb.release(qpar, tod)
                return emit_pe

            pend[0] = make_tail()
        bqk.release(s, qk_last)
        bvh.release(s, av_last)
    if pend[0] is not None:
        pend[0]()
        pend[0] = None

    if debug_stage == 2:
        p.waits("sy", p.now("ve", "ae"))
        p.dma("sy", nc.dram_tensor("dbg_nb", [128, 16], F32, kind="ExternalOutput").ap(),
              nb_sb[:], "s_dbg")
        p.dma("sy", nc.dram_tensor("dbg_rows", [1, 256], F32, kind="ExternalOutput").ap(),
              A2[0:1, 4096:4352], "s_dbg")
        _final(p)
        return _wire(nc, p)
    PH2 = p.now()

    wo_sb = A2b[:, 0:8192]
    o_sb = [A2b[:, 8192 + i * 4096: 8192 + (i + 1) * 4096] for i in range(2)]
    x3cell = [A2[:, 8192 + i * 512: 8192 + (i + 1) * 512] for i in range(3)]
    x2cell = [A2[:, 9728 + i * 512: 9728 + (i + 1) * 512] for i in range(2)]
    sq2 = [A2[:, 10752 + i * 512: 10752 + (i + 1) * 512] for i in range(2)]

    for eng in ENG:
        p.waits(eng, PH2)
    t_wo = p.dma("gp", wo_sb.rearrange("p (c n) -> p c n", c=NFC),
                 woT.rearrange("(c p) n -> p c n", p=128), "s_wo")
    bos, bx3, bx2c, bsq2 = Buf(2), Buf(3), Buf(2), Buf(2)
    b_wops, b_st2 = Buf(2), Buf(1)
    wo_r = wo_sb.rearrange("p (c n) -> p c n", c=NFC)
    x2_r = x2_s.rearrange("(c p) t -> c p t", p=128)

    p.wait("pe", t_wo)
    xi = 0
    for tg in range(NTG):
        lo = tg * TG
        ts_ = tg % 2
        bos.acquire(p, "sy", ts_)
        tol = None
        for pp in range(NH // 2):
            dst = o_sb[ts_].rearrange("p (c n) -> p c n", c=NFC)[:, pp, :]
            tol = p.dma("sy", dst, o_s[pp, :, lo:lo + TG], f"s_ol{ts_}")
        b_st2.acquire(p, "pe", 0)
        tpm = None
        for oc in range(NFC):
            pb = oc % 2
            b_wops.acquire(p, "pe", pb)
            p.wait("pe", tol)
            for fc in range(NFC):
                tpm = p.op("pe", lambda e, o=bank(pb), l=wo_r[:, fc, oc * 128:(oc + 1) * 128],
                           r=o_sb[ts_].rearrange("p (c n) -> p c n", c=NFC)[:, fc, :],
                           st=(fc == 0), sp=(fc == NFC - 1):
                           e.matmul(o, l, r, start=st, stop=sp), inc=(fc == NFC - 1))
            sl = xi % 3
            xi += 1
            bx3.acquire(p, "sy", sl)
            tx = p.dma("sy", x3cell[sl], xT_r[oc, :, lo:lo + TG], f"s_x3{sl}")
            xs = oc % 2
            bx2c.acquire(p, "ve", xs)
            p.waits("ve", [tpm, tx])
            tx2 = p.op("ve", lambda e, o=x2cell[xs], b=bank(pb), x=x3cell[sl]: e.tensor_add(o, b, x), inc=True)
            b_wops.release(pb, tx2)
            bx3.release(sl, tx2)
            p.wait("sy", tx2)
            tx2d = p.dma("sy", x2_r[oc, :, lo:lo + TG], x2cell[xs], f"s_x2{xs}")
            ss = oc % 2
            bsq2.acquire(p, "ae", ss)
            p.wait("ae", tx2)
            tq2 = p.op("ae", lambda e, o=sq2[ss], i=x2cell[xs]: e.activation(o, i, AF.Square), inc=True)
            p.wait("pe", tx2)
            p.op("pe", lambda e, i=x2cell[xs], st=(oc == 0), sp=(oc == NFC - 1):
                 e.matmul(bank(4, p1=1), onescol[:, 0:1], i, start=st, stop=sp))
            p.wait("pe", tq2)
            tsm = p.op("pe", lambda e, i=sq2[ss], st=(oc == 0), sp=(oc == NFC - 1):
                       e.matmul(bank(5, p1=1), onescol[:, 0:1], i, start=st, stop=sp), inc=True)
            bx2c.release(xs, tsm, tx2d)
            bsq2.release(ss, tsm)
        bos.release(ts_, tpm)
        p.wait("ve", tsm)
        p.op("ve", lambda e: e.tensor_scalar(rows2[0:1, 0:512], bank(4, p1=1), 1.0 / H, None, ALU.mult))
        p.op("ve", lambda e: e.tensor_scalar(rows2[0:1, 512:1024], bank(5, p1=1), 1.0 / H, None, ALU.mult))
        p.op("ve", lambda e: e.tensor_mul(rows2[0:1, 1024:1536], rows2[0:1, 0:512], rows2[0:1, 0:512]))
        tvar = p.op("ve", lambda e: e.tensor_sub(rows2[0:1, 1536:2048], rows2[0:1, 512:1024], rows2[0:1, 1024:1536]), inc=True)
        b_st2.release(0, tvar)
        p.wait("ae", tvar)
        p.op("ae", lambda e: e.activation(rows2[0:1, 1024:1536], rows2[0:1, 1536:2048], AF.Ln, bias=NEPS))
        tr2 = p.op("ae", lambda e: e.activation(rows2[0:1, 2048:2560], rows2[0:1, 1024:1536], AF.Exp, scale=-0.5), inc=True)
        tu = p.op("ve", lambda e: e.tensor_scalar(rows2[0:1, 2560:3072], rows2[0:1, 1536:2048], NEPS, None, ALU.add), inc=True)
        p.wait("ve", tu)
        p.wait("ve", tr2)
        tur = p.op("ve", lambda e: e.reciprocal(rows2[0:1, 1024:1536], rows2[0:1, 2560:3072]), inc=True)
        p.wait("ve", tur)
        tq_ = p.op("ve", lambda e: e.tensor_mul(rows2[0:1, 2560:3072], rows2[0:1, 1536:2048], rows2[0:1, 1024:1536]), inc=True)
        p.wait("ae", tq_)
        p.op("ae", lambda e: e.activation(rows2[0:1, 1024:1536], rows2[0:1, 2560:3072], AF.Ln, bias=NEPS))
        trs2 = p.op("ae", lambda e: e.activation(rows2[0:1, 2560:3072], rows2[0:1, 1024:1536], AF.Exp, scale=-0.5), inc=True)
        p.waits("ve", [tr2, trs2])
        p.op("ve", lambda e: e.tensor_mul(rows2[0:1, 1024:1536], rows2[0:1, 2048:2560], rows2[0:1, 2560:3072]))
        p.op("ve", lambda e:
             e.scalar_tensor_tensor(rows2[0:1, 1536:2048], rows2[0:1, 0:512], -1.0, rows2[0:1, 1024:1536], ALU.mult, ALU.mult))
        p.op("ve", lambda e: e.tensor_scalar(rows2[0:1, 512:1024], rows2[0:1, 2048:2560], 1.0, None, ALU.add))
        trows = p.op("ve", lambda e:
                     e.scalar_tensor_tensor(rows2[0:1, 2560:3072], rows2[0:1, 0:512], -1.0, rows2[0:1, 2048:2560], ALU.mult, ALU.mult), inc=True)
        p.wait("sy", trows)
        for ri, sl_ in ((R_A2, (1024, 1536)), (R_B2, (1536, 2048)), (R_A3, (512, 1024)), (R_B3, (2560, 3072))):
            trd = p.dma("sy", rows_d[ri:ri + 1, lo:lo + TG], rows2[0:1, sl_[0]:sl_[1]], "s_rows")

    if debug_stage == 3:
        _final(p)
        return _wire(nc, p)
    PH3 = p.now()

    w1buf = [A2b[:, k * 2048:(k + 1) * 2048] for k in range(3)]
    w2buf = [A2b[:, 6144 + k * 1024: 6144 + (k + 1) * 1024] for k in range(2)]
    w1r = [w.rearrange("p (c n) -> p c n", c=NFC) for w in w1buf]
    w2r = [w.rearrange("p (c n) -> p c n", c=NFC) for w in w2buf]
    sSc = [A2[:, 4096 + i * 1024: 4096 + (i + 1) * 1024] for i in range(2)]
    sGc = [A2[:, 6144 + i * 1024: 6144 + (i + 1) * 1024] for i in range(2)]
    q1c = [A2[:, 8192 + i * 1024: 8192 + (i + 1) * 1024] for i in range(2)]
    scr = [A2[:, 4096 + i * 512: 4096 + (i + 1) * 512] for i in range(5)]
    bcA2 = A2[:, 6656:7168]
    bcB2 = A2[:, 7168:7680]
    bcA3 = A2[:, 7680:8192]
    bcB3 = A2[:, 8192:8704]
    cbtf = [A2[:, 10240 + i * 512: 10240 + (i + 1) * 512] for i in range(4)]
    cbtf2 = [A2[:, 10240 + i * 1024: 10240 + (i + 1) * 1024] for i in range(2)]
    gbank = lambda base: PS[:, base * TG:(base + 2) * TG]
    lbank = lambda base: PS[:, (base + 2) * TG:(base + 4) * TG]
    gw_r = gw_sb.ap().rearrange("p (c n) -> p c n", c=NFC)
    out_r = outT.rearrange("(c p) t -> p c t", p=128)

    for eng in ENG:
        p.waits(eng, PH3)

    t_out_prev = None
    t_w1cons = [None, None, None]
    t_w2cons = [None, None, None]
    last_p_ve = {0: None, 1: None}
    t_w2rd = {0: None, 1: None}
    t_cbtmm_last = None
    for half in range(2):
        hlo = half * HALF
        acc = lambda oc, t0=0, n=HALF: A1[:, oc * 2048 + t0: oc * 2048 + t0 + n]
        tsb = lambda fc, t0=0, n=TG: A3b[:, fc * 2048 + t0: fc * 2048 + t0 + n]
        hsl = lambda jj, ti: A4b[:, (jj * 4 + ti) * 512: (jj * 4 + ti + 1) * 512]
        hsl2 = lambda jj, pr: A4b[:, (jj * 4 + pr * 2) * 512: (jj * 4 + pr * 2 + 2) * 512]

        p.wait("sy", t_out_prev)
        p.waits("ve", p.now("gp", "ae"))
        t_x2l = None
        for oc in range(NFC):
            t_x2l = p.dma("sy", acc(oc), x2_r[oc, :, hlo:hlo + HALF], "s_x2l")
        b_bc = Buf(1)
        tb_prev = None
        for tgi in range(4):
            lo = hlo + tgi * TG
            b_bc.acquire(p, "pe", 0)
            p.waits("pe", p.now("ve", "ae"))
            p.waits("sy", [tb_prev])
            trl = None
            for bi, ri in enumerate((R_A2, R_B2, R_A3, R_B3)):
                trl = p.dma("sy", rows2[0:1, bi * 512:(bi + 1) * 512], rows_d[ri:ri + 1, lo:lo + TG], "s_rld")
            p.wait("pe", trl)
            tb = None
            for bi in range(4):
                tb = p.op("pe", lambda e, o=bank(bi), r=rows2[0:1, bi * 512:(bi + 1) * 512]:
                          e.matmul(o, onesrow[:], r, start=True, stop=True), inc=(bi == 3))
            tb_prev = tb
            p.wait("ve", tb)
            tbe = None
            for bi, dstc in enumerate((bcA2, bcB2, bcA3, bcB3)):
                tbe = p.op("ve", lambda e, o=dstc, b=bank(bi): e.tensor_copy(o, b), inc=(bi == 3))
            b_bc.release(0, tbe)
            p.wait("ve", t_x2l)
            tacc = None
            for oc in range(NFC):
                x2c = acc(oc, tgi * TG, TG)
                p.op("ve", lambda e, o=scr[0], i=x2c: e.tensor_mul(o, i, bcA2))
                p.op("ve", lambda e, o=scr[1], i=scr[0]: e.tensor_add(o, i, bcB2))
                p.op("ve", lambda e, o=tsb(oc, tgi * TG, TG), i=scr[1], s2=nsc[:, oc:oc + 1]:
                     e.tensor_scalar(o, i, s2, None, ALU.mult))
                p.op("ve", lambda e, o=scr[2], i=x2c: e.tensor_mul(o, i, bcA3))
                tacc = p.op("ve", lambda e, o=x2c, i=scr[2]: e.tensor_add(o, i, bcB3), inc=True)
        t_prep = tacc
        tbe_last = tbe
        last_p_ve = {0: t_prep, 1: t_prep}

        b_g, b_tr = Buf(1), Buf(1)
        t_cfm = None
        tgb = None
        for ti in range(16):
            tlo = ti * 128
            b_g.acquire(p, "pe", 0)
            p.wait("pe", t_prep)
            tgm = None
            for fc in range(NFC):
                tgm = p.op("pe", lambda e, o=bank(6)[:, 0:NE], l=tsb(fc, tlo, 128), r=gw_r[:, fc, :],
                           st=(fc == 0), sp=(fc == NFC - 1):
                           e.matmul(o, l, r, start=st, stop=sp), inc=(fc == NFC - 1))
            p.wait("ve", tgm)
            tgb = p.op("ve", lambda e: e.tensor_add(g_sb[:], bank(6)[:, 0:NE], gbb[:]), inc=True)
            b_g.release(0, tgb)
            p.wait("ve", tgb)
            tt8 = p.op("ve", lambda e: e.max(top8[:], g_sb[:]), inc=True)
            p.wait("ve", tt8)
            p.op("ve", lambda e: e.memset(m8[:], 1e30))
            p.op("ve", lambda e: e.tensor_copy(gmod[:], g_sb[:]))
            for r_ in range(4):
                tmv = p.op("ve", lambda e, i=r_: e.tensor_copy(m8[:, 0:1], top8[:, i:i + 1]), inc=True)
                p.wait("ve", tmv)
                tmr = p.op("ve", lambda e: e.match_replace(gmod[:], m8[:], gmod[:], -1e30), inc=True)
            p.wait("ve", tmr)
            tmk = p.op("ve", lambda e: e.tensor_scalar(gmask[:], gmod[:], -1e29, None, ALU.is_le), inc=True)
            tng = p.op("ve", lambda e: e.tensor_scalar(negm1[:], top8[:, 0:1], -1.0, None, ALU.mult), inc=True)
            p.waits("ae", [tng, tgb])
            tex = p.op("ae", lambda e: e.activation(comb[:], g_sb[:], AF.Exp, bias=negm1[:]), inc=True)
            p.waits("ve", [tex, tmk])
            tcu = p.op("ve", lambda e: e.tensor_mul(comb[:], comb[:], gmask[:]), inc=True)
            p.wait("ve", tcu)
            tsu = p.op("ve", lambda e: e.reduce_sum(srec[:], comb[:], AX.X), inc=True)
            p.wait("ve", tsu)
            trc = p.op("ve", lambda e: e.reciprocal(srec[:], srec[:]), inc=True)
            p.wait("ve", trc)
            tcf = p.op("ve", lambda e: e.tensor_scalar(comb[:], comb[:], srec[:], None, ALU.mult), inc=True)
            b_tr.acquire(p, "pe", 0)
            p.wait("pe", tcf)
            ttr = p.op("pe", lambda e: e.transpose(PS[0:NE, 7 * TG: 7 * TG + 128], comb[:], ident[:]), inc=True)
            p.wait("ve", ttr)
            t_cfm = p.op("ve", lambda e, o=cfm[0:NE, tlo:tlo + 128]: e.tensor_copy(o, PS[0:NE, 7 * TG: 7 * TG + 128]), inc=True)
            b_tr.release(0, t_cfm)

        p.waits("gp", [t_cfm, t_cbtmm_last])
        t_crow = p.dma("gp", crow[0:1, 0:HALF], cfm[0:1, :], "s_cr")
        p.waits("pe", [tgb, t_cfm, t_crow])
        tcbt0 = []
        for ti4 in range(4):
            t_cbtmm_last = p.op("pe", lambda e, o=bank(4 + ti4), r=crow[0:1, ti4 * 512:(ti4 + 1) * 512]:
                                e.matmul(o, onesrowb[:], r, start=True, stop=True), inc=True)
            p.wait("ve", t_cbtmm_last)
            tcc = p.op("ve", lambda e, o=cbtf[ti4], b=bank(4 + ti4): e.tensor_copy(o, b), inc=True)
            tcbt0.append(tcc)

        bb2 = Buf(2)
        bb2.release(0, tbe_last)
        bb2.release(1, tbe_last)
        ta2_pb = {}
        p.wait("pe", t_cfm)
        for oc in range(NFC):
            for tgi in range(4):
                pb = (oc * 4 + tgi) % 2
                bb2.acquire(p, "pe", pb)
                tb2 = p.op("pe", lambda e, o=bank(pb), l=b2bf[:, oc * 128:(oc + 1) * 128],
                           r=cfm[0:NE, tgi * TG:(tgi + 1) * TG]:
                           e.matmul(o, l, r, start=True, stop=True), inc=True)
                p.wait("ve", tb2)
                ta2 = p.op("ve", lambda e, o=acc(oc, tgi * TG, TG), b=bank(pb):
                           e.tensor_add(o, o, b), inc=True)
                bb2.release(pb, ta2)
                ta2_pb[pb] = ta2

        if debug_stage == 4 and half == 0:
            p.waits("sy", p.now("ve"))
            for nm, tn in (("dbg_g", g_sb), ("dbg_top8", top8), ("dbg_gmod", gmod),
                           ("dbg_gmask", gmask), ("dbg_comb", comb)):
                p.dma("sy", nc.dram_tensor(nm, list(tn.shape), F32, kind="ExternalOutput").ap(),
                      tn[:], "s_dbg")
            p.dma("sy", nc.dram_tensor("dbg_acc", [128, 16384], F32, kind="ExternalOutput").ap(),
                  A1.ap(), "s_dbg")
            p.dma("sy", nc.dram_tensor("dbg_t", [128, 16384], BF16, kind="ExternalOutput").ap(),
                  A3b[:, 0:16384], "s_dbg")
            p.dma("sy", nc.dram_tensor("dbg_cfm", [NE, HALF], BF16, kind="ExternalOutput").ap(),
                  cfm[:], "s_dbg")
            _final(p)
            return _wire(nc, p)

        b_set = Buf(2)
        b_set.release(0, ta2_pb.get(0), ta2_pb.get(1), tbe_last)
        b_set.release(1, *(tcbt0 + [tgb, t_cfm]))
        t_w1l, t_w2l = {}, {}
        for i in range(3):
            p.wait("gp", t_w1cons[i])
            t_w1l[i] = p.dma("gp", w1buf[i], w1n[0, i], "s_w1")
        for i in range(2):
            p.wait("gp", t_w2cons[i])
            t_w2l[i] = p.dma("gp", w2buf[i], w2n[0, i], "s_w2")
        gctr = 0
        t_hw_done = {0: None, 1: None}
        for e_ in range(_NE_RUN[0]):
            for jj in range(NFC):
                idx = e_ * 8 + jj
                for pr in range(2):
                    g = gctr
                    gctr += 1
                    gpar = g % 2
                    base = gpar * 4
                    b_set.acquire(p, "pe", g)
                    if pr == 0:
                        p.wait("pe", t_w1l[idx])
                        if idx == 0:
                            p.wait("pe", t_prep)
                    tgl = None
                    for t2 in range(2):
                        for fc in range(NFC):
                            tgl = p.op("pe", lambda e, o=bank(base + t2), l=w1r[idx % 3][:, fc, 0:128],
                                       r=tsb(fc, (pr * 2 + t2) * TG, TG), st=(fc == 0), sp=(fc == NFC - 1):
                                       e.matmul(o, l, r, start=st, stop=sp), inc=(fc == NFC - 1 and t2 == 1))
                    tll = None
                    for t2 in range(2):
                        for fc in range(NFC):
                            tll = p.op("pe", lambda e, o=bank(base + 2 + t2), l=w1r[idx % 3][:, fc, 128:256],
                                       r=tsb(fc, (pr * 2 + t2) * TG, TG), st=(fc == 0), sp=(fc == NFC - 1):
                                       e.matmul(o, l, r, start=st, stop=sp), inc=(fc == NFC - 1 and t2 == 1))
                    sS, sG, q1 = sSc[gpar], sGc[gpar], q1c[gpar]
                    bsap = b1sig[:, e_ * 8 + jj: e_ * 8 + jj + 1]
                    b1ap = b1pp[:, e_ * 16 + jj: e_ * 16 + jj + 1]
                    b1lap = b1lin1[:, e_ * 8 + jj: e_ * 8 + jj + 1]
                    p.wait("ae", tgl)
                    p.wait("ae", last_p_ve[gpar])
                    tsS = p.op("ae", lambda e, o=sS, b=gbank(base), bi=bsap:
                               e.activation(o, b, AF.Sigmoid, bias=bi, scale=1.702), inc=True)
                    p.wait("ve", tsS)
                    p.op("ve", lambda e, o=sG, b=gbank(base), bi=b1ap:
                         e.tensor_scalar(o, b, bi, 7.0, ALU.add, ALU.min))
                    tp = p.op("ve", lambda e, o=sG, i=sS, g2=sG:
                              e.scalar_tensor_tensor(o, i, SIGC, g2, ALU.min, ALU.mult), inc=True)
                    last_p_ve[gpar] = tp
                    p.wait("ve", tll)
                    tq1 = p.op("ve", lambda e, o=q1, b=lbank(base), bi=b1lap:
                               e.tensor_scalar(o, b, bi, 8.0, ALU.add, ALU.min), inc=True)
                    p.op("ve", lambda e, o=q1, i=q1, g2=sG:
                         e.scalar_tensor_tensor(o, i, -6.0, g2, ALU.max, ALU.mult))
                    if jj == 0:
                        p.wait("ve", t_w2rd[pr])
                    thw = p.op("ve", lambda e, o=hsl2(jj, pr), i=q1, c=cbtf2[pr]:
                               e.tensor_mul(o, i, c), inc=True)
                    t_hw_done[pr] = thw
                    b_set.release(g, tsS, tq1)
                    if pr == 1:
                        nxt = idx + 3
                        if nxt < NE * NFC:
                            p.wait("gp", tll)
                            t_w1l[nxt] = p.dma("gp", w1buf[nxt % 3], w1n[nxt // 8, nxt % 8], "s_w1")
                        t_w1cons[idx % 3] = tll
            t_crow_next = None
            if e_ < NE - 1:
                p.wait("gp", t_cbtmm_last)
                t_crow_next = p.dma("gp", crow[0:1, 0:HALF], cfm[e_ + 1:e_ + 2, :], "s_cr")
            for oc in range(NFC):
                idx2 = e_ * 8 + oc
                for pr in range(2):
                    g = gctr
                    gctr += 1
                    base = (g % 2) * 4
                    b_set.acquire(p, "pe", g)
                    if pr == 0:
                        p.wait("pe", t_w2l[idx2])
                    if oc == 0:
                        p.wait("pe", t_hw_done[pr])
                    tww = None
                    for fc2 in range(NFC):
                        for t2 in range(2):
                            tww = p.op("pe", lambda e, o=bank(base + t2), l=w2r[idx2 % 2][:, fc2, :],
                                       r=hsl(fc2, pr * 2 + t2), st=(fc2 == 0), sp=(fc2 == NFC - 1):
                                       e.matmul(o, l, r, start=st, stop=sp), inc=(fc2 == NFC - 1 and t2 == 1))
                    rel2 = []
                    if e_ < NE - 1 and pr == 1 and oc < 4:
                        p.wait("pe", t_crow_next)
                        t_cbtmm_last = p.op("pe", lambda e, o=bank(base + 2), r=crow[0:1, oc * 512:(oc + 1) * 512]:
                                            e.matmul(o, onesrowb[:], r, start=True, stop=True), inc=True)
                        p.wait("ve", t_cbtmm_last)
                        tcc = p.op("ve", lambda e, o=cbtf[oc], b=bank(base + 2): e.tensor_copy(o, b), inc=True)
                        rel2.append(tcc)
                    p.wait("ve", tww)
                    ta = p.op("ve", lambda e, o=acc(oc, pr * 2 * TG, 2 * TG), b=gbank(base):
                              e.tensor_add(o, o, b), inc=True)
                    rel2.append(ta)
                    if oc == NFC - 1:
                        t_w2rd[pr] = tww
                    b_set.release(g, *rel2)
                    if pr == 1:
                        nxt2 = idx2 + 2
                        if nxt2 < NE * NFC:
                            p.wait("gp", tww)
                            t_w2l[nxt2] = p.dma("gp", w2buf[nxt2 % 2], w2n[nxt2 // 8, nxt2 % 8], "s_w2")
                        t_w2cons[idx2 % 2] = tww
        p.waits("sy", p.now("ve"))
        t_out_prev = p.dma("sy", out_r[:, :, hlo:hlo + HALF],
                           A1.ap().rearrange("p (c t) -> p c t", c=NFC), "s_out")
    _final(p)
    return _wire(nc, p)


def _wire(nc, p):
    with nc.Block() as block:
        @block.tensor
        def _(t):
            for f in p.st["pe"]:
                f(t)

        @block.vector
        def _(v):
            for f in p.st["ve"]:
                f(v)

        @block.scalar
        def _(s2):
            for f in p.st["ae"]:
                f(s2)

        @block.sync
        def _(s2):
            for f in p.st["sy"]:
                f(s2)

        @block.gpsimd
        def _(g):
            for f in p.st["gp"]:
                f(g)
    return nc, p


_CACHE = {}


def _w1_layout(w1, perm):
    A = w1[:, perm, :].reshape(NE, 16, 128, NFC, 128)
    out = np.empty((NE, 8, 128, NFC, 256), np.float32)
    out[:, :, :, :, 0:128] = A[:, 0:8].transpose(0, 1, 4, 3, 2)
    out[:, :, :, :, 128:256] = A[:, 8:16].transpose(0, 1, 4, 3, 2)
    return np.ascontiguousarray(out.reshape(NE, 8, 128, 2048))


def _w2_layout(w2):
    B = w2.reshape(NE, 8, 128, NFC, 128)
    return np.ascontiguousarray(B.transpose(0, 1, 4, 3, 2).reshape(NE, 8, 128, 1024))


def _host_prep(inputs):
    x = np.asarray(inputs["L0_attn_x"], np.float32)
    Wq = np.asarray(inputs["L0_attn_Wq"], np.float32)
    Wk = np.asarray(inputs["L0_attn_Wk"], np.float32)
    Wv = np.asarray(inputs["L0_attn_Wv"], np.float32)
    Wo = np.asarray(inputs["L0_attn_Wo"], np.float32)
    nscale = np.asarray(inputs["L0_ffn_norm_scale"], np.float32)
    gate_w = np.asarray(inputs["L0_ffn_gate_w"], np.float32)
    gate_b = np.asarray(inputs["L0_ffn_gate_b"], np.float32)
    w1 = np.asarray(inputs["L0_ffn_w1"], np.float32)
    b1 = np.asarray(inputs["L0_ffn_b1"], np.float32)
    w2 = np.asarray(inputs["L0_ffn_w2"], np.float32)
    b2 = np.asarray(inputs["L0_ffn_b2"], np.float32)

    perm = np.concatenate([np.arange(0, 2048, 2), np.arange(1, 2048, 2)])
    com = {
        "wqT": np.ascontiguousarray(Wq.T),
        "wkT": np.ascontiguousarray(Wk.T),
        "wvT": np.ascontiguousarray(Wv.T),
        "woT": np.ascontiguousarray(Wo.T),
        "gwT": np.ascontiguousarray(gate_w.T),
        "gbb": np.ascontiguousarray(np.broadcast_to(gate_b, (128, NE))),
        "w1n": _w1_layout(w1, perm),
        "b1pp": np.ascontiguousarray(
            b1[:, perm].reshape(NE, 16, 128).transpose(2, 0, 1).reshape(128, NE * 16)),
        "w2n": _w2_layout(w2),
        "b2": np.ascontiguousarray(b2),
        "nsc": np.ascontiguousarray(nscale.reshape(NFC, 128).T),
    }
    inv = 1.0 / (10000.0 ** (np.arange(0, HD, 2, dtype=np.float64) / HD))
    pos = np.arange(S, dtype=np.float64)
    fr = np.outer(pos, inv)
    emb = np.concatenate([fr, fr], -1)
    cos1 = np.cos(emb).astype(np.float32).T
    sin1 = np.sin(emb).astype(np.float32).T
    com["cosb"] = np.ascontiguousarray(np.vstack([cos1, cos1]))
    com["sinb"] = np.ascontiguousarray(np.vstack([sin1, sin1]))
    mk = np.zeros((128, 4 * 1024), np.float32)
    kk = np.arange(128)[:, None]
    qq = np.arange(TG)[None, :]
    for r in range(4):
        m_ = np.where(qq >= kk + 128 * r, 0.0, -1e30)
        mk[:, r * 1024: r * 1024 + 512] = m_
        mk[:, r * 1024 + 512:(r + 1) * 1024] = m_
    com["maskb"] = mk
    rt = np.zeros((128, 128), np.float32)
    for base in (0, 64):
        for d in range(32):
            rt[base + d + 32, base + d] = -1.0
            rt[base + d, base + d + 32] = 1.0
    com["rT"] = rt
    com["ident"] = np.eye(128, dtype=np.float32)
    oc3 = np.zeros((128, 3), np.float32)
    oc3[:, 0] = 1.0
    oc3[0:64, 1] = 1.0
    oc3[64:128, 2] = 1.0
    com["onescol"] = oc3
    com["onescolb"] = oc3
    com["onesrow"] = np.ones((1, 128), np.float32)
    com["onesrowb"] = np.ones((1, 128), np.float32)
    cv = np.zeros((128, 4), np.float32)
    cv[:, 1] = NEPS
    com["cv"] = cv

    in_maps = []
    for b in range(x.shape[0]):
        m = dict(com)
        m["xT"] = np.ascontiguousarray(x[b].T)
        in_maps.append(m)
    return in_maps


def kernel(**inputs):
    in_maps = _host_prep(inputs)
    if "nc" not in _CACHE:
        _CACHE["nc"], _ = build_program()
    res = run_bass_kernel_spmd(_CACHE["nc"], in_maps, core_ids=list(range(8)))
    outs = [np.ascontiguousarray(np.asarray(r["outT"]).T) for r in res.results]
    return np.stack(outs, axis=0)


# version 11
# speedup vs baseline: 1.1032x; 1.0128x over previous
import numpy as np
import ml_dtypes
import concourse.bass as bass
import concourse.mybir as mybir
from concourse.bass_utils import run_bass_kernel_spmd

F32 = mybir.dt.float32
BF16 = mybir.dt.bfloat16
AF = mybir.ActivationFunctionType
ALU = mybir.AluOpType
AX = mybir.AxisListType

S = 4096
H = 1024
NH, HD = 16, 64
NE = 32
TG = 512
NTG = S // TG
NFC = H // 128
HALF = S // 2
NEPS = 1e-5
SIGC = float(1.0 / (1.0 + np.exp(-1.702 * 7.0)))

ENG = ("pe", "ve", "ae", "sy", "gp")
_NE_RUN = [NE]
_FLAGS = {"pool": True, "cbtw2": True}


class Prog:
    def __init__(self, nc):
        self.nc = nc
        self.st = {e: [] for e in ENG}
        self.sems = {}
        self.cnt = {}

    def sem(self, name):
        if name not in self.sems:
            self.sems[name] = self.nc.alloc_semaphore(name)
            self.cnt[name] = 0
        return self.sems[name]

    def em(self, eng, f):
        self.st[eng].append(f)

    def wait(self, eng, ticket):
        if ticket is None:
            return
        name, val = ticket
        if val <= 0:
            return
        s = self.sem(name)
        self.em(eng, lambda e, s=s, v=val: e.wait_ge(s, v))

    def waits(self, eng, tickets):
        for t in tickets:
            self.wait(eng, t)

    def dma(self, q, out_ap, in_ap, semname):
        s = self.sem(semname)
        self.cnt[semname] += 16
        v = self.cnt[semname]
        self.em(q, lambda e, o=out_ap, i=in_ap, s=s: e.dma_start(out=o, in_=i).then_inc(s, 16))
        return (semname, v)

    def op(self, eng, fn, inc=False):
        if not inc:
            self.em(eng, lambda e, f=fn: f(e))
            return None
        s = self.sem(eng)
        self.cnt[eng] += 1
        v = self.cnt[eng]
        self.em(eng, lambda e, f=fn, s=s: f(e).then_inc(s, 1))
        return (eng, v)

    def now(self, *sems):
        return [(s, self.cnt.get(s, 0)) for s in (sems or self.cnt.keys())]


class Buf:
    def __init__(self, n):
        self.n = n
        self.free = [[] for _ in range(n)]

    def acquire(self, p, eng, i):
        for t in self.free[i % self.n]:
            p.wait(eng, t)
        self.free[i % self.n] = []

    def release(self, i, *tickets):
        self.free[i % self.n] += [t for t in tickets if t is not None]


def _final(p):
    for name, v in list(p.cnt.items()):
        if name in ENG:
            continue
        p.wait("sy", (name, v))


def build_program(debug_stage=0):
    nc = bass.Bass("TRN2", target_bir_lowering=False)
    p = Prog(nc)

    api = lambda name, shape, dt: nc.dram_tensor(name, shape, dt, kind="ExternalInput").ap()
    xT = api("xT", [H, S], F32)
    wqT = api("wqT", [H, H], F32)
    wkT = api("wkT", [H, H], F32)
    wvT = api("wvT", [H, H], F32)
    woT = api("woT", [H, H], F32)
    gwT = api("gwT", [H, NE], F32)
    gbb_h = api("gbb", [128, NE], F32)
    w1n = api("w1n", [NE, 8, 128, 2048], F32)
    b1pp_h = api("b1pp", [128, NE * 16], F32)
    w2n = api("w2n", [NE, 8, 128, 1024], F32)
    b2_h = api("b2", [NE, H], F32)
    nsc_h = api("nsc", [128, NFC], F32)
    cos_h = api("cosb", [128, S], F32)
    sin_h = api("sinb", [128, S], F32)
    mask_h = api("maskb", [128, 4 * 1024], F32)
    rT_h = api("rT", [128, 128], F32)
    ident_h = api("ident", [128, 128], F32)
    onescol_h = api("onescol", [128, 3], F32)
    onescolb_h = api("onescolb", [128, 3], F32)
    onesrow_h = api("onesrow", [1, 128], F32)
    onesrowb_h = api("onesrowb", [1, 128], F32)
    cv_h = api("cv", [128, 4], F32)

    outT = nc.dram_tensor("outT", [H, S], F32, kind="ExternalOutput").ap()

    def scratch(name, shape, dt, stage):
        kind = "ExternalOutput" if (0 < debug_stage == stage) else "Internal"
        return nc.dram_tensor(name, shape, dt, kind=kind).ap()
    q_s = scratch("q_s", [NH // 2, 128, S], BF16, 1)
    k_s = scratch("k_s", [NH // 2, 128, S], BF16, 1)
    v_s = scratch("v_s", [S, H], BF16, 1)
    o_s = scratch("o_s", [NH // 2, 128, S], BF16, 2)
    x2_s = scratch("x2_s", [H, S], F32, 3)

    A1 = nc.alloc_sbuf_tensor("A1", [128, 16384], F32)
    A2 = nc.alloc_sbuf_tensor("A2", [128, 12320], F32)
    A3 = nc.alloc_sbuf_tensor("A3", [128, 8192], F32)
    A4 = nc.alloc_sbuf_tensor("A4", [128, 8192], F32)
    A1b, A2b, A3b, A4b = (a.bitcast(BF16) for a in (A1, A2, A3, A4))
    masks2 = A3[:, 3072:7168]
    nrow = A3[0:1, 7168:7424]
    cvt = nc.alloc_sbuf_tensor("cvt_sb", [128, 4], F32)
    nsc = nc.alloc_sbuf_tensor("nsc_sb", [128, NFC], F32)
    gbb = nc.alloc_sbuf_tensor("gbb_sb", [128, NE], F32)
    b1pp = nc.alloc_sbuf_tensor("b1pp_sb", [128, NE * 16], F32)
    b1sig = nc.alloc_sbuf_tensor("b1sig_sb", [128, NE * 8], F32)
    b1lin1 = nc.alloc_sbuf_tensor("b1lin1_sb", [128, NE * 8], F32)
    b2bf = nc.alloc_sbuf_tensor("b2bf_sb", [NE, H], BF16)
    gw_sb = nc.alloc_sbuf_tensor("gw_sb", [128, NFC * NE], BF16)
    rT = nc.alloc_sbuf_tensor("rT_sb", [128, 128], BF16)
    ident = nc.alloc_sbuf_tensor("ident_sb", [128, 128], F32)
    onescol = nc.alloc_sbuf_tensor("onescol_sb", [128, 3], F32)
    onescolb = nc.alloc_sbuf_tensor("onescolb_sb", [128, 3], BF16)
    onesrow = nc.alloc_sbuf_tensor("onesrow_sb", [1, 128], F32)
    onesrowb = nc.alloc_sbuf_tensor("onesrowb_sb", [1, 128], BF16)
    rows2 = nc.alloc_sbuf_tensor("rows2", [1, 3072], F32)
    nb_sb = nc.alloc_sbuf_tensor("nb_sb", [128, 16], F32)
    cfm = nc.alloc_sbuf_tensor("cfm", [NE, HALF], BF16)
    crow = nc.alloc_sbuf_tensor("crow", [1, HALF], BF16)
    g_sb = nc.alloc_sbuf_tensor("g_sb", [128, NE], F32)
    m8 = nc.alloc_sbuf_tensor("m8", [128, 8], F32)
    top8 = nc.alloc_sbuf_tensor("top8", [128, 8], F32)
    gmod = nc.alloc_sbuf_tensor("gmod", [128, NE], F32)
    gmask = nc.alloc_sbuf_tensor("gmask", [128, NE], F32)
    negm1 = nc.alloc_sbuf_tensor("negm1", [128, 1], F32)
    srec = nc.alloc_sbuf_tensor("srec", [128, 1], F32)
    comb = nc.alloc_sbuf_tensor("comb", [128, NE], F32)
    rows_d = nc.dram_tensor("rows_d", [8, S], F32, kind="Internal").ap()

    nc.const_aps.register(F32, 0.0, cvt[:, 0:1])
    nc.const_aps.register(F32, NEPS, cvt[:, 1:2])

    PS = nc.alloc_psum_tensor("PS", [128, 8 * TG], F32)

    def bank(i, n=1, p0=0, p1=128):
        return PS[p0:p1, i * TG:(i + n) * TG]

    (R_A2, R_B2, R_A3, R_B3) = range(4)

    for dst, src in ((cvt, cv_h), (nsc, nsc_h), (gbb, gbb_h), (b1pp, b1pp_h),
                     (ident, ident_h), (onescol, onescol_h),
                     (onesrow, onesrow_h), (masks2, mask_h)):
        p.dma("sy", dst[:], src, "s_init")
    for dst, src in ((rT, rT_h), (onescolb, onescolb_h), (onesrowb, onesrowb_h)):
        p.dma("gp", dst[:], src, "s_initg")
    p.dma("gp", gw_sb.ap().rearrange("p (c n) -> p c n", c=NFC),
          gwT.rearrange("(c p) n -> p c n", p=128), "s_initg")
    p.dma("gp", b2bf[:], b2_h, "s_initg")
    T_INIT = [("s_init", p.cnt["s_init"]), ("s_initg", p.cnt["s_initg"])]

    b1r = b1pp.ap().rearrange("p (e c) -> p e c", e=NE)
    p.waits("ve", T_INIT)
    p.op("ve", lambda e: e.tensor_scalar(
        b1sig.ap().rearrange("p (e c) -> p e c", e=NE), b1r[:, :, 0:8], 1.702, None, ALU.mult))
    p.op("ve", lambda e: e.tensor_scalar(
        b1lin1.ap().rearrange("p (e c) -> p e c", e=NE), b1r[:, :, 8:16], 1.0, None, ALU.add), inc=True)

    ln1c = lambda fc, lo=0, n=TG: A1b[:, fc * 512 + lo: fc * 512 + lo + n]
    xcell = [A1[:, 2048 + i * 512: 2048 + (i + 1) * 512] for i in range(3)]
    sqcell = [A1[:, 3584 + i * 512: 3584 + (i + 1) * 512] for i in range(2)]
    vw = [A2b[:, i * 4096:(i + 1) * 4096] for i in range(2)]
    wqk = [A2b[:, 8192 + i * 1024: 8192 + (i + 1) * 1024] for i in range(2)]
    cosc = [A2b[:, 10240 + i * 512: 10240 + (i + 1) * 512] for i in range(2)]
    sinc = [A2b[:, 11264 + i * 512: 11264 + (i + 1) * 512] for i in range(2)]
    vcell = [A2b[:, 12288 + i * 512: 12288 + (i + 1) * 512] for i in range(4)]
    qraw = [A2b[:, 14336 + i * 512: 14336 + (i + 1) * 512] for i in range(2)]
    qrope = [A2b[:, 15360 + i * 512: 15360 + (i + 1) * 512] for i in range(2)]
    sqro = [A2b[:, 16384 + i * 512: 16384 + (i + 1) * 512] for i in range(2)]
    qcc = [A2[:, 8704 + i * 512: 8704 + (i + 1) * 512] for i in range(2)]
    rsc = [A2[:, 9728 + i * 512: 9728 + (i + 1) * 512] for i in range(2)]
    cbA = A2[:, 10752:11264]
    cbB = A2[:, 11264:11776]

    xT_r = xT.rearrange("(c p) t -> c p t", p=128)
    wq_r = wqT.rearrange("(c p) n -> p c n", p=128)
    wk_r = wkT.rearrange("(c p) n -> p c n", p=128)
    wv_r = wvT.rearrange("(c p) n -> p c n", p=128)

    bx, bsq = Buf(3), Buf(2)
    bwqk, bvw, bvc = Buf(2), Buf(2), Buf(4)
    bqraw, bqc, bqro, bsqro, bcs = Buf(2), Buf(2), Buf(2), Buf(2), Buf(2)
    b_proj, b_rot, b_stat, b_vps = Buf(2), Buf(2), Buf(1), Buf(2)
    b_ln1, b_cb = Buf(1), Buf(1)
    bnst = Buf(2)

    p.waits("pe", T_INIT)
    p.waits("ae", T_INIT)
    xi = 0
    for tg in range(NTG):
        lo = tg * TG
        cs = tg % 2
        bcs.acquire(p, "gp", cs)
        t_c1 = p.dma("gp", cosc[cs], cos_h[:, lo:lo + TG], f"s_cs{cs}")
        t_c2 = p.dma("gp", sinc[cs], sin_h[:, lo:lo + TG], f"s_cs{cs}")
        b_stat.acquire(p, "pe", 0)
        for fc in range(NFC):
            sl = xi % 3
            xi += 1
            bx.acquire(p, "sy", sl)
            tx = p.dma("sy", xcell[sl], xT_r[fc, :, lo:lo + TG], f"s_x{sl}")
            ss = fc % 2
            bsq.acquire(p, "ae", ss)
            p.wait("ae", tx)
            tq = p.op("ae", lambda e, o=sqcell[ss], i=xcell[sl]: e.activation(o, i, AF.Square), inc=True)
            p.wait("pe", tx)
            p.op("pe", lambda e, i=xcell[sl], st=(fc == 0), sp=(fc == NFC - 1):
                 e.matmul(bank(2, p1=1), onescol[:, 0:1], i, start=st, stop=sp))
            p.wait("pe", tq)
            tmm = p.op("pe", lambda e, i=sqcell[ss], st=(fc == 0), sp=(fc == NFC - 1):
                       e.matmul(bank(3, p1=1), onescol[:, 0:1], i, start=st, stop=sp), inc=True)
            bx.release(sl, tmm)
            bsq.release(ss, tmm)
        p.wait("ve", tmm)
        p.op("ve", lambda e: e.tensor_scalar(rows2[0:1, 0:512], bank(2, p1=1), 1.0 / H, None, ALU.mult))
        p.op("ve", lambda e: e.tensor_scalar(rows2[0:1, 512:1024], bank(3, p1=1), 1.0 / H, None, ALU.mult))
        p.op("ve", lambda e: e.tensor_mul(rows2[0:1, 1024:1536], rows2[0:1, 0:512], rows2[0:1, 0:512]))
        tvar = p.op("ve", lambda e: e.tensor_sub(rows2[0:1, 1536:2048], rows2[0:1, 512:1024], rows2[0:1, 1024:1536]), inc=True)
        b_stat.release(0, tvar)
        p.wait("ae", tvar)
        p.op("ae", lambda e: e.activation(rows2[0:1, 1024:1536], rows2[0:1, 1536:2048], AF.Ln, bias=NEPS))
        trs = p.op("ae", lambda e: e.activation(rows2[0:1, 2048:2560], rows2[0:1, 1024:1536], AF.Exp, scale=-0.5), inc=True)
        p.wait("ve", trs)
        tbr = p.op("ve", lambda e:
                   e.scalar_tensor_tensor(rows2[0:1, 2560:3072], rows2[0:1, 0:512], -1.0, rows2[0:1, 2048:2560], ALU.mult, ALU.mult), inc=True)
        b_stat.acquire(p, "pe", 0)
        p.wait("pe", tbr)
        p.op("pe", lambda e: e.matmul(bank(2), onesrow[:], rows2[0:1, 2048:2560], start=True, stop=True))
        tbc = p.op("pe", lambda e: e.matmul(bank(3), onesrow[:], rows2[0:1, 2560:3072], start=True, stop=True), inc=True)
        b_cb.acquire(p, "ve", 0)
        p.wait("ve", tbc)
        p.op("ve", lambda e: e.tensor_copy(cbA, bank(2)))
        tcb = p.op("ve", lambda e: e.tensor_copy(cbB, bank(3)), inc=True)
        b_stat.release(0, tcb)
        b_ln1.acquire(p, "ve", 0)
        for fc in range(NFC):
            sl = xi % 3
            xi += 1
            bx.acquire(p, "sy", sl)
            tx = p.dma("sy", xcell[sl], xT_r[fc, :, lo:lo + TG], f"s_x{sl}")
            p.wait("ve", tx)
            p.op("ve", lambda e, i=xcell[sl], o=rsc[fc % 2]: e.tensor_mul(o, i, cbA))
            tl = p.op("ve", lambda e, o=ln1c(fc), i=rsc[fc % 2]: e.tensor_add(o, i, cbB), inc=True)
            bx.release(sl, tl)
        t_ln1 = tl
        b_cb.release(0, t_ln1)

        for og in range(2):
            ws = og % 2
            bvw.acquire(p, "gp", ws)
            tw = p.dma("gp", vw[ws].rearrange("p (c n) -> p c n", c=NFC),
                       wv_r[:, :, og * 512:(og + 1) * 512], f"s_vw{ws}")
            p.wait("pe", tw)
            p.wait("pe", t_ln1)
            for tt in range(4):
                vb = tt % 2
                b_vps.acquire(p, "pe", vb)
                for fc in range(NFC):
                    tpm = p.op("pe", lambda e, l=ln1c(fc, tt * 128, 128),
                               r=vw[ws].rearrange("p (c n) -> p c n", c=NFC)[:, fc, :],
                               o=bank(4 + vb), st=(fc == 0), sp=(fc == NFC - 1):
                               e.matmul(o, l, r, start=st, stop=sp), inc=(fc == NFC - 1))
                vc = (og * 4 + tt) % 4
                bvc.acquire(p, "ve", vc)
                p.wait("ve", tpm)
                tve = p.op("ve", lambda e, o=vcell[vc], b=bank(4 + vb): e.tensor_copy(o, b), inc=True)
                b_vps.release(vb, tve)
                p.wait("sy", tve)
                tvd = p.dma("sy", v_s[lo + tt * 128: lo + (tt + 1) * 128, og * 512:(og + 1) * 512],
                            vcell[vc], f"s_vs{vc}")
                bvc.release(vc, tvd)
            bvw.release(ws, tpm)

        for wi, (wh_r, dst) in enumerate(((wq_r, q_s), (wk_r, k_s))):
            for oc in range(NFC):
                ws = oc % 2
                bwqk.acquire(p, "gp", ws)
                tw = p.dma("gp", wqk[ws].rearrange("p (c n) -> p c n", c=NFC),
                           wh_r[:, :, oc * 128:(oc + 1) * 128], f"s_w{ws}")
                pb = oc % 2
                b_proj.acquire(p, "pe", pb)
                p.wait("pe", tw)
                if oc == 0 and wi == 0:
                    p.wait("pe", t_ln1)
                for fc in range(NFC):
                    tpm = p.op("pe", lambda e, l=wqk[ws].rearrange("p (c n) -> p c n", c=NFC)[:, fc, :],
                               r=ln1c(fc), o=bank(pb), st=(fc == 0), sp=(fc == NFC - 1):
                               e.matmul(o, l, r, start=st, stop=sp), inc=(fc == NFC - 1))
                bwqk.release(ws, tpm)
                qs2 = oc % 2
                bqraw.acquire(p, "ae", qs2)
                p.wait("ae", tpm)
                tqr = p.op("ae", lambda e, o=qraw[qs2], b=bank(pb): e.copy(o, b), inc=True)
                bqc.acquire(p, "ve", qs2)
                p.wait("ve", tqr)
                if oc < 2 and wi == 0:
                    p.waits("ve", [t_c1, t_c2])
                tqc = p.op("ve", lambda e, o=qcc[qs2], b=bank(pb), c=cosc[cs]: e.tensor_mul(o, b, c), inc=True)
                b_proj.release(pb, tqc)
                rb_ = oc % 2
                b_rot.acquire(p, "pe", rb_)
                p.wait("pe", tqc)
                trm = p.op("pe", lambda e, o=bank(6 + rb_), i=qraw[qs2]:
                           e.matmul(o, rT[:], i, start=True, stop=True), inc=True)
                bqraw.release(qs2, trm)
                bqro.acquire(p, "ve", qs2)
                p.wait("ve", trm)
                p.op("ve", lambda e, o=rsc[qs2], b=bank(6 + rb_), s2=sinc[cs]: e.tensor_mul(o, b, s2))
                tqo = p.op("ve", lambda e, o=qrope[qs2], a=qcc[qs2], b=rsc[qs2]: e.tensor_add(o, a, b), inc=True)
                b_rot.release(rb_, tqo)
                bqc.release(qs2, tqo)
                bsqro.acquire(p, "ae", qs2)
                p.wait("ae", tqo)
                tq2 = p.op("ae", lambda e, o=sqro[qs2], i=qrope[qs2]: e.activation(o, i, AF.Square), inc=True)
                b_stat.acquire(p, "pe", 0)
                p.wait("pe", tq2)
                p.op("pe", lambda e, i=sqro[qs2]: e.matmul(bank(2, p1=1), onescolb[:, 1:2], i, start=True, stop=True))
                tnm = p.op("pe", lambda e, i=sqro[qs2]: e.matmul(bank(3, p1=1), onescolb[:, 2:3], i, start=True, stop=True), inc=True)
                bsqro.release(qs2, tnm)
                p.wait("ve", tnm)
                p.op("ve", lambda e, o=nrow[0:1, wi * 128 + 2 * oc * 8 + tg: wi * 128 + 2 * oc * 8 + tg + 1]:
                     e.tensor_reduce(o, bank(2, p1=1), AX.X, ALU.max))
                tne = p.op("ve", lambda e, o=nrow[0:1, wi * 128 + (2 * oc + 1) * 8 + tg: wi * 128 + (2 * oc + 1) * 8 + tg + 1]:
                           e.tensor_reduce(o, bank(3, p1=1), AX.X, ALU.max), inc=True)
                b_stat.release(0, tne)
                p.wait("sy", tqo)
                tqd = p.dma("sy", dst[oc, :, lo:lo + TG], qrope[qs2][:, :], f"s_qk{qs2}")
                bqro.release(qs2, tqd)
            if wi == 1:
                b_ln1.release(0, tpm)
        bcs.release(cs, tqo)

    if debug_stage == 1:
        _final(p)
        return _wire(nc, p)

    PH1 = p.now()

    qh2 = [A1b[:, i * 4096:(i + 1) * 4096] for i in range(2)]
    kh2 = [A1b[:, 8192 + i * 4096: 8192 + (i + 1) * 4096] for i in range(2)]
    vh2 = [[A1b[:, 16384 + (2 * i + j) * 2080: 16384 + (2 * i + j + 1) * 2080]
            for j in range(2)] for i in range(2)]
    pcell = [A1b[:, 24704 + i * 512: 24704 + (i + 1) * 512] for i in range(4)]
    pcp2 = [A1b[:, 24704 + i * 1024: 24704 + (i + 1) * 1024] for i in range(2)]
    osbc = [A1b[:, 26752 + i * 512: 26752 + (i + 1) * 512] for i in range(4)]
    nstage = [A2[0:1, i * 2048:(i + 1) * 2048] for i in range(2)]
    rqrow = A2[0:1, 4096:4128]
    rkrow = A2[0:1, 4128:4160]
    nbsc = A2[0:1, 4224:4288]
    zraw = A2[0:1, 4608:5632]
    zrecb = A2b[0:1, 11264:12288]
    zln = A2[0:1, 8704:9728]
    rbc = [A2[:, 6656 + i * 512: 6656 + (i + 1) * 512] for i in range(2)]
    msb = [A2[:, 7680 + i * 512: 7680 + (i + 1) * 512] for i in range(2)]
    msb2 = A2[:, 7680:8704]

    msbp = [A2[:, 9728 + i * 1024: 9728 + (i + 1) * 1024] for i in range(2)]
    bqk, bvh = Buf(2), Buf(2)
    bpcp, bmsb2, bosb = Buf(2), Buf(2), Buf(2)
    bavz, bnst = Buf(2), Buf(2)
    sguard = {0: [], 1: [], 2: [], 3: []}
    rot = [0]

    def claim_slot(extra=()):
        r = rot[0] % 2
        rot[0] += 1
        bA, bB = 2 * r, 2 * r + 1
        p.waits("pe", sguard[bA] + sguard[bB] + [t for t in extra if t is not None])
        sguard[bA], sguard[bB] = [], []
        return bA, bB

    for eng in ENG:
        p.waits(eng, PH1)

    pend = [None]
    tzeh, trbh = [None], [None]
    tmx = None
    for hh in range(NH):
        p.op("ve", lambda e, o=nbsc[0:1, hh:hh + 1], i=nrow[0:1, hh * 8:(hh + 1) * 8]:
             e.tensor_reduce(o, i, AX.X, ALU.max))
        tmx = p.op("ve", lambda e, o=nbsc[0:1, 16 + hh:17 + hh],
                   i=nrow[0:1, 128 + hh * 8: 128 + (hh + 1) * 8]:
                   e.tensor_reduce(o, i, AX.X, ALU.max), inc=(hh == NH - 1))
    p.wait("ve", tmx)
    tml = p.op("ve", lambda e: e.tensor_mul(nbsc[0:1, 32:48], nbsc[0:1, 0:16], nbsc[0:1, 16:32]), inc=True)
    p.wait("ve", tml)
    tpm = None
    for pj in range(NH // 2):
        tpm = p.op("ve", lambda e, o=nbsc[0:1, 48 + pj:49 + pj], a=nbsc[0:1, 32 + 2 * pj:33 + 2 * pj],
                   b=nbsc[0:1, 33 + 2 * pj:34 + 2 * pj]: e.tensor_max(o, a, b), inc=(pj == NH // 2 - 1))
    p.wait("ae", tpm)
    p.op("ae", lambda e: e.activation(nbsc[0:1, 56:64], nbsc[0:1, 48:56], AF.Ln))
    tsq = p.op("ae", lambda e: e.activation(nbsc[0:1, 32:40], nbsc[0:1, 56:64], AF.Exp, scale=0.5), inc=True)
    p.wait("ve", tsq)
    tnb = p.op("ve", lambda e, o=A2b[0:1, 8576:8584]:
               e.tensor_scalar(o, nbsc[0:1, 32:40], -0.125, None, ALU.mult), inc=True)
    rA, rB = claim_slot([tnb])
    tnm = p.op("pe", lambda e, o=bank(rA)[:, 0:8]: e.matmul(o, onesrowb[:], A2b[0:1, 8576:8584],
                                                            start=True, stop=True), inc=True)
    p.wait("ve", tnm)
    tnbc = p.op("ve", lambda e, o=nb_sb[:, 0:8]: e.tensor_copy(o, bank(rA)[:, 0:8]), inc=True)
    sguard[rA].append(tnbc)

    for pp in range(NH // 2):
        s = pp % 2
        bqk.acquire(p, "sy", s)
        tq = p.dma("sy", qh2[s], q_s[pp], f"s_qh{s}")
        tk = p.dma("sy", kh2[s], k_s[pp], f"s_qh{s}")
        bvh.acquire(p, "sy", s)
        vr = [vh2[s][j].rearrange("p (c d) -> p c d", d=65) for j in range(2)]
        tvh = None
        for j, hh in enumerate((2 * pp, 2 * pp + 1)):
            tvh = p.dma("sy", vr[j][:, :, 0:64],
                        v_s.rearrange("(c p) o -> p c o", p=128)[:, :, 64 * hh: 64 * hh + 64], f"s_vh{s}")
        p.wait("ve", tvh)
        tvm = None
        for j in range(2):
            tvm = p.op("ve", lambda e, o=vr[j][:, :, 64:65]: e.memset(o, 1.0), inc=(j == 1))

        nbc = nb_sb[:, pp:pp + 1]
        qk_last = av_last = None
        for qgi, qg in enumerate((0, 7, 1, 6, 2, 5, 3, 4)):
            nkc = 4 * (qg + 1)
            qpar = qgi % 2
            avA, avB = 4 + 2 * qpar, 5 + 2 * qpar
            kinfo = []

            def emit_av(kc, te, qg=qg, qpar=qpar, avA=avA, avB=avB, nkc=nkc, pp=pp, s=s):
                if kc == 0:
                    bavz.acquire(p, "pe", qpar)
                p.wait("pe", te)
                if qg == 0 and kc == 0:
                    p.waits("pe", [tvh, tvm])
                st_, sp_ = (kc == 0), (kc == nkc - 1)
                sp2 = kc % 2
                vrl = [vh2[s][j].rearrange("p (c d) -> p c d", d=65) for j in range(2)]
                p.op("pe", lambda e, o=PS[0:65, avA * TG:(avA + 1) * TG], l=vrl[0][:, kc, :],
                     r=pcell[2 * sp2], st=st_, sp=sp_: e.matmul(o, l, r, start=st, stop=sp))
                t_ = p.op("pe", lambda e, o=PS[0:65, avB * TG:(avB + 1) * TG], l=vrl[1][:, kc, :],
                          r=pcell[2 * sp2 + 1], st=st_, sp=sp_: e.matmul(o, l, r, start=st, stop=sp),
                          inc=True)
                bpcp.release(sp2, t_)
                return t_

            for kc in range(nkc):
                first = (qg == 0 and kc == 0)
                bA, bB = claim_slot([tq, tk] if first else ())
                p.op("pe", lambda e, o=bank(bA), l=kh2[s][0:64, kc * 128:(kc + 1) * 128],
                     r=qh2[s][0:64, qg * TG:(qg + 1) * TG]:
                     e.matmul(o, l, r, start=True, stop=True))
                qk_last = p.op("pe", lambda e, o=bank(bB), l=kh2[s][64:128, kc * 128:(kc + 1) * 128],
                               r=qh2[s][64:128, qg * TG:(qg + 1) * TG]:
                               e.matmul(o, l, r, start=True, stop=True), inc=True)
                sp2 = kc % 2
                bpcp.acquire(p, "ae", sp2)
                if kc >= 4 * qg:
                    r_ = kc - 4 * qg
                    bmsb2.acquire(p, "ve", sp2)
                    p.wait("ve", qk_last)
                    tmB = p.op("ve", lambda e, o=msbp[sp2], b=PS[:, bA * TG:(bB + 1) * TG],
                               m=masks2[:, r_ * 1024:(r_ + 1) * 1024]:
                               e.tensor_add(o, b, m), inc=True)
                    sguard[bA].append(tmB)
                    sguard[bB].append(tmB)
                    p.wait("ae", tmB)
                    if first:
                        p.wait("ae", tnbc)
                    te = p.op("ae", lambda e, o=pcp2[sp2], i=msbp[sp2], bi=nbc:
                              e.activation(o, i, AF.Exp, bias=bi, scale=0.125), inc=True)
                    bmsb2.release(sp2, te)
                else:
                    p.wait("ae", qk_last)
                    te = p.op("ae", lambda e, o=pcp2[sp2], b=PS[:, bA * TG:(bB + 1) * TG], bi=nbc:
                              e.activation(o, b, AF.Exp, bias=bi, scale=0.125), inc=True)
                    sguard[bA].append(te)
                    sguard[bB].append(te)
                kinfo.append(te)
                if kc >= 1:
                    av_last = emit_av(kc - 1, kinfo[kc - 1])
                if kc == 3 and pend[0] is not None:
                    pend[0]()
                    pend[0] = None
            av_last = emit_av(nkc - 1, kinfo[nkc - 1])

            def make_tail(pp=pp, qg=qg, qpar=qpar, avA=avA, avB=avB, av_last=av_last):
                p.wait("ve", av_last)
                p.wait("ve", tzeh[0])
                p.op("ve", lambda e, o=zraw[0:1, 0:512], b=PS[64:65, avA * TG:(avA + 1) * TG]:
                     e.tensor_scalar(o, b, 1e-30, None, ALU.add))
                tz2 = p.op("ve", lambda e, o=zraw[0:1, 512:1024], b=PS[64:65, avB * TG:(avB + 1) * TG]:
                           e.tensor_scalar(o, b, 1e-30, None, ALU.add), inc=True)
                p.waits("ae", [tz2, trbh[0]])
                p.op("ae", lambda e: e.activation(zln[0:1, 0:1024], zraw[0:1, 0:1024], AF.Ln))
                tze = p.op("ae", lambda e: e.activation(zrecb[0:1, 0:1024], zln[0:1, 0:1024],
                                                        AF.Exp, scale=-1.0), inc=True)
                tzeh[0] = tze

                def emit_pe():
                    rA2, rB2 = claim_slot([tze])
                    p.op("pe", lambda e, o=PS[0:64, rA2 * TG:(rA2 + 1) * TG]:
                         e.matmul(o, onesrowb[0:1, 0:64], zrecb[0:1, 0:512], start=True, stop=True))
                    trb = p.op("pe", lambda e, o=PS[0:64, rB2 * TG:(rB2 + 1) * TG]:
                               e.matmul(o, onesrowb[0:1, 0:64], zrecb[0:1, 512:1024],
                                        start=True, stop=True), inc=True)
                    trbh[0] = trb
                    p.wait("ve", trb)
                    p.op("ve", lambda e, o=rbc[0][0:64, :]: e.tensor_copy(o, PS[0:64, rA2 * TG:(rA2 + 1) * TG]))
                    trbc = p.op("ve", lambda e, o=rbc[1][0:64, :]: e.tensor_copy(o, PS[0:64, rB2 * TG:(rB2 + 1) * TG]), inc=True)
                    sguard[rA2].append(trbc)
                    sguard[rB2].append(trbc)
                    bosb.acquire(p, "ve", qpar)
                    p.op("ve", lambda e, o=osbc[2 * qpar][0:64, :], b=PS[0:64, avA * TG:(avA + 1) * TG],
                         r=rbc[0][0:64, :]: e.tensor_mul(o, b, r))
                    tosb = p.op("ve", lambda e, o=osbc[2 * qpar + 1][0:64, :], b=PS[0:64, avB * TG:(avB + 1) * TG],
                                r=rbc[1][0:64, :]: e.tensor_mul(o, b, r), inc=True)
                    bavz.release(qpar, tosb, tz2)
                    p.wait("sy", tosb)
                    p.dma("sy", o_s[pp, 0:64, qg * TG:(qg + 1) * TG], osbc[2 * qpar][0:64, :], f"s_os{qpar}")
                    tod = p.dma("sy", o_s[pp, 64:128, qg * TG:(qg + 1) * TG], osbc[2 * qpar + 1][0:64, :], f"s_os{qpar}")
                    bosb.release(qpar, tod)
                return emit_pe

            pend[0] = make_tail()
        bqk.release(s, qk_last)
        bvh.release(s, av_last)
    if pend[0] is not None:
        pend[0]()
        pend[0] = None

    if debug_stage == 2:
        p.waits("sy", p.now("ve", "ae"))
        p.dma("sy", nc.dram_tensor("dbg_nb", [128, 16], F32, kind="ExternalOutput").ap(),
              nb_sb[:], "s_dbg")
        p.dma("sy", nc.dram_tensor("dbg_rows", [1, 256], F32, kind="ExternalOutput").ap(),
              A2[0:1, 4096:4352], "s_dbg")
        _final(p)
        return _wire(nc, p)
    PH2 = p.now()

    w1buf = [A2b[:, k * 2048:(k + 1) * 2048] for k in range(3)]
    w2buf = [A2b[:, 6144 + k * 1024: 6144 + (k + 1) * 1024] for k in range(2)]
    t_w1l0, t_w2l0 = {}, {}
    p.waits("gp", [t for t in PH1 if t[0] == "pe"])
    for i in range(3):
        t_w1l0[i] = p.dma("gp", w1buf[i], w1n[0, i], "s_w1")
    for i in range(2):
        t_w2l0[i] = p.dma("gp", w2buf[i], w2n[0, i], "s_w2")

    wo_sb = A1b[:, 0:8192]
    o_sb = [A2b[:, 8192 + i * 4096: 8192 + (i + 1) * 4096] for i in range(2)]
    x3cell = [A2[:, 8192 + i * 512: 8192 + (i + 1) * 512] for i in range(3)]
    x2cell = [A2[:, 9728 + i * 512: 9728 + (i + 1) * 512] for i in range(2)]
    sq2 = [A2[:, 10752 + i * 512: 10752 + (i + 1) * 512] for i in range(2)]

    for eng in ENG:
        p.waits(eng, PH2)
    t_wo = p.dma("gp", wo_sb.rearrange("p (c n) -> p c n", c=NFC),
                 woT.rearrange("(c p) n -> p c n", p=128), "s_wo")
    bos, bx3, bx2c, bsq2 = Buf(2), Buf(3), Buf(2), Buf(2)
    b_wops, b_st2 = Buf(2), Buf(1)
    wo_r = wo_sb.rearrange("p (c n) -> p c n", c=NFC)
    x2_r = x2_s.rearrange("(c p) t -> c p t", p=128)

    p.wait("pe", t_wo)
    xi = 0
    for tg in range(NTG):
        lo = tg * TG
        ts_ = tg % 2
        bos.acquire(p, "sy", ts_)
        tol = None
        for pp in range(NH // 2):
            dst = o_sb[ts_].rearrange("p (c n) -> p c n", c=NFC)[:, pp, :]
            tol = p.dma("sy", dst, o_s[pp, :, lo:lo + TG], f"s_ol{ts_}")
        b_st2.acquire(p, "pe", 0)
        tpm = None
        for oc in range(NFC):
            pb = oc % 2
            b_wops.acquire(p, "pe", pb)
            p.wait("pe", tol)
            for fc in range(NFC):
                tpm = p.op("pe", lambda e, o=bank(pb), l=wo_r[:, fc, oc * 128:(oc + 1) * 128],
                           r=o_sb[ts_].rearrange("p (c n) -> p c n", c=NFC)[:, fc, :],
                           st=(fc == 0), sp=(fc == NFC - 1):
                           e.matmul(o, l, r, start=st, stop=sp), inc=(fc == NFC - 1))
            sl = xi % 3
            xi += 1
            bx3.acquire(p, "sy", sl)
            tx = p.dma("sy", x3cell[sl], xT_r[oc, :, lo:lo + TG], f"s_x3{sl}")
            xs = oc % 2
            bx2c.acquire(p, "ve", xs)
            p.waits("ve", [tpm, tx])
            tx2 = p.op("ve", lambda e, o=x2cell[xs], b=bank(pb), x=x3cell[sl]: e.tensor_add(o, b, x), inc=True)
            b_wops.release(pb, tx2)
            bx3.release(sl, tx2)
            p.wait("sy", tx2)
            tx2d = p.dma("sy", x2_r[oc, :, lo:lo + TG], x2cell[xs], f"s_x2{xs}")
            ss = oc % 2
            bsq2.acquire(p, "ae", ss)
            p.wait("ae", tx2)
            tq2 = p.op("ae", lambda e, o=sq2[ss], i=x2cell[xs]: e.activation(o, i, AF.Square), inc=True)
            p.wait("pe", tx2)
            p.op("pe", lambda e, i=x2cell[xs], st=(oc == 0), sp=(oc == NFC - 1):
                 e.matmul(bank(4, p1=1), onescol[:, 0:1], i, start=st, stop=sp))
            p.wait("pe", tq2)
            tsm = p.op("pe", lambda e, i=sq2[ss], st=(oc == 0), sp=(oc == NFC - 1):
                       e.matmul(bank(5, p1=1), onescol[:, 0:1], i, start=st, stop=sp), inc=True)
            bx2c.release(xs, tsm, tx2d)
            bsq2.release(ss, tsm)
        bos.release(ts_, tpm)
        p.wait("ve", tsm)
        p.op("ve", lambda e: e.tensor_scalar(rows2[0:1, 0:512], bank(4, p1=1), 1.0 / H, None, ALU.mult))
        p.op("ve", lambda e: e.tensor_scalar(rows2[0:1, 512:1024], bank(5, p1=1), 1.0 / H, None, ALU.mult))
        p.op("ve", lambda e: e.tensor_mul(rows2[0:1, 1024:1536], rows2[0:1, 0:512], rows2[0:1, 0:512]))
        tvar = p.op("ve", lambda e: e.tensor_sub(rows2[0:1, 1536:2048], rows2[0:1, 512:1024], rows2[0:1, 1024:1536]), inc=True)
        b_st2.release(0, tvar)
        p.wait("ae", tvar)
        p.op("ae", lambda e: e.activation(rows2[0:1, 1024:1536], rows2[0:1, 1536:2048], AF.Ln, bias=NEPS))
        tr2 = p.op("ae", lambda e: e.activation(rows2[0:1, 2048:2560], rows2[0:1, 1024:1536], AF.Exp, scale=-0.5), inc=True)
        tu = p.op("ve", lambda e: e.tensor_scalar(rows2[0:1, 2560:3072], rows2[0:1, 1536:2048], NEPS, None, ALU.add), inc=True)
        p.wait("ve", tu)
        p.wait("ve", tr2)
        tur = p.op("ve", lambda e: e.reciprocal(rows2[0:1, 1024:1536], rows2[0:1, 2560:3072]), inc=True)
        p.wait("ve", tur)
        tq_ = p.op("ve", lambda e: e.tensor_mul(rows2[0:1, 2560:3072], rows2[0:1, 1536:2048], rows2[0:1, 1024:1536]), inc=True)
        p.wait("ae", tq_)
        p.op("ae", lambda e: e.activation(rows2[0:1, 1024:1536], rows2[0:1, 2560:3072], AF.Ln, bias=NEPS))
        trs2 = p.op("ae", lambda e: e.activation(rows2[0:1, 2560:3072], rows2[0:1, 1024:1536], AF.Exp, scale=-0.5), inc=True)
        p.waits("ve", [tr2, trs2])
        p.op("ve", lambda e: e.tensor_mul(rows2[0:1, 1024:1536], rows2[0:1, 2048:2560], rows2[0:1, 2560:3072]))
        p.op("ve", lambda e:
             e.scalar_tensor_tensor(rows2[0:1, 1536:2048], rows2[0:1, 0:512], -1.0, rows2[0:1, 1024:1536], ALU.mult, ALU.mult))
        p.op("ve", lambda e: e.tensor_scalar(rows2[0:1, 512:1024], rows2[0:1, 2048:2560], 1.0, None, ALU.add))
        trows = p.op("ve", lambda e:
                     e.scalar_tensor_tensor(rows2[0:1, 2560:3072], rows2[0:1, 0:512], -1.0, rows2[0:1, 2048:2560], ALU.mult, ALU.mult), inc=True)
        p.wait("sy", trows)
        for ri, sl_ in ((R_A2, (1024, 1536)), (R_B2, (1536, 2048)), (R_A3, (512, 1024)), (R_B3, (2560, 3072))):
            trd = p.dma("sy", rows_d[ri:ri + 1, lo:lo + TG], rows2[0:1, sl_[0]:sl_[1]], "s_rows")

    if debug_stage == 3:
        _final(p)
        return _wire(nc, p)
    PH3 = p.now()

    w1r = [w.rearrange("p (c n) -> p c n", c=NFC) for w in w1buf]
    w2r = [w.rearrange("p (c n) -> p c n", c=NFC) for w in w2buf]
    sSc = [A2[:, 4096 + i * 1024: 4096 + (i + 1) * 1024] for i in range(2)]
    sGc = [A2[:, 6144 + i * 1024: 6144 + (i + 1) * 1024] for i in range(2)]
    q1c = [A2[:, 8192 + i * 1024: 8192 + (i + 1) * 1024] for i in range(2)]
    scr = [A2[:, 4096 + i * 512: 4096 + (i + 1) * 512] for i in range(5)]
    bcA2 = A2[:, 6656:7168]
    bcB2 = A2[:, 7168:7680]
    bcA3 = A2[:, 7680:8192]
    bcB3 = A2[:, 8192:8704]
    cbtf = [A2[:, 10240 + i * 512: 10240 + (i + 1) * 512] for i in range(4)]
    cbtf2 = [A2[:, 10240 + i * 1024: 10240 + (i + 1) * 1024] for i in range(2)]
    gbank = lambda base: PS[:, base * TG:(base + 2) * TG]
    lbank = lambda base: PS[:, (base + 2) * TG:(base + 4) * TG]
    gw_r = gw_sb.ap().rearrange("p (c n) -> p c n", c=NFC)
    out_r = outT.rearrange("(c p) t -> p c t", p=128)

    for eng in ENG:
        p.waits(eng, PH3)

    t_out_prev = None
    t_w1cons = [None, None, None]
    t_w2cons = [None, None, None]
    last_p_ve = {0: None, 1: None}
    t_w2rd = {0: None, 1: None}
    t_cbtmm_last = None
    for half in range(2):
        hlo = half * HALF
        acc = lambda oc, t0=0, n=HALF: A1[:, oc * 2048 + t0: oc * 2048 + t0 + n]
        tsb = lambda fc, t0=0, n=TG: A3b[:, fc * 2048 + t0: fc * 2048 + t0 + n]
        hsl = lambda jj, ti: A4b[:, (jj * 4 + ti) * 512: (jj * 4 + ti + 1) * 512]
        hsl2 = lambda jj, pr: A4b[:, (jj * 4 + pr * 2) * 512: (jj * 4 + pr * 2 + 2) * 512]

        p.wait("sy", t_out_prev)
        p.waits("ve", p.now("gp", "ae"))
        t_x2l = None
        for oc in range(NFC):
            t_x2l = p.dma("sy", acc(oc), x2_r[oc, :, hlo:hlo + HALF], "s_x2l")
        b_bc = Buf(1)
        tb_prev = None
        for tgi in range(4):
            lo = hlo + tgi * TG
            b_bc.acquire(p, "pe", 0)
            p.waits("pe", p.now("ve", "ae"))
            p.waits("sy", [tb_prev])
            trl = None
            for bi, ri in enumerate((R_A2, R_B2, R_A3, R_B3)):
                trl = p.dma("sy", rows2[0:1, bi * 512:(bi + 1) * 512], rows_d[ri:ri + 1, lo:lo + TG], "s_rld")
            p.wait("pe", trl)
            tb = None
            for bi in range(4):
                tb = p.op("pe", lambda e, o=bank(bi), r=rows2[0:1, bi * 512:(bi + 1) * 512]:
                          e.matmul(o, onesrow[:], r, start=True, stop=True), inc=(bi == 3))
            tb_prev = tb
            p.wait("ve", tb)
            tbe = None
            for bi, dstc in enumerate((bcA2, bcB2, bcA3, bcB3)):
                tbe = p.op("ve", lambda e, o=dstc, b=bank(bi): e.tensor_copy(o, b), inc=(bi == 3))
            b_bc.release(0, tbe)
            p.wait("ve", t_x2l)
            tacc = None
            for oc in range(NFC):
                x2c = acc(oc, tgi * TG, TG)
                p.op("ve", lambda e, o=scr[0], i=x2c: e.tensor_mul(o, i, bcA2))
                p.op("ve", lambda e, o=scr[1], i=scr[0]: e.tensor_add(o, i, bcB2))
                p.op("ve", lambda e, o=tsb(oc, tgi * TG, TG), i=scr[1], s2=nsc[:, oc:oc + 1]:
                     e.tensor_scalar(o, i, s2, None, ALU.mult))
                p.op("ve", lambda e, o=scr[2], i=x2c: e.tensor_mul(o, i, bcA3))
                tacc = p.op("ve", lambda e, o=x2c, i=scr[2]: e.tensor_add(o, i, bcB3), inc=True)
        t_prep = tacc
        tbe_last = tbe
        last_p_ve = {0: t_prep, 1: t_prep}

        b_g, b_tr = Buf(1), Buf(1)
        t_cfm = None
        tgb = None
        for ti in range(16):
            tlo = ti * 128
            b_g.acquire(p, "pe", 0)
            p.wait("pe", t_prep)
            tgm = None
            for fc in range(NFC):
                tgm = p.op("pe", lambda e, o=bank(6)[:, 0:NE], l=tsb(fc, tlo, 128), r=gw_r[:, fc, :],
                           st=(fc == 0), sp=(fc == NFC - 1):
                           e.matmul(o, l, r, start=st, stop=sp), inc=(fc == NFC - 1))
            p.wait("ve", tgm)
            tgb = p.op("ve", lambda e: e.tensor_add(g_sb[:], bank(6)[:, 0:NE], gbb[:]), inc=True)
            b_g.release(0, tgb)
            p.wait("ve", tgb)
            tt8 = p.op("ve", lambda e: e.max(top8[:], g_sb[:]), inc=True)
            p.wait("ve", tt8)
            p.op("ve", lambda e: e.memset(m8[:], 1e30))
            p.op("ve", lambda e: e.tensor_copy(gmod[:], g_sb[:]))
            for r_ in range(4):
                tmv = p.op("ve", lambda e, i=r_: e.tensor_copy(m8[:, 0:1], top8[:, i:i + 1]), inc=True)
                p.wait("ve", tmv)
                tmr = p.op("ve", lambda e: e.match_replace(gmod[:], m8[:], gmod[:], -1e30), inc=True)
            p.wait("ve", tmr)
            tmk = p.op("ve", lambda e: e.tensor_scalar(gmask[:], gmod[:], -1e29, None, ALU.is_le), inc=True)
            tng = p.op("ve", lambda e: e.tensor_scalar(negm1[:], top8[:, 0:1], -1.0, None, ALU.mult), inc=True)
            p.waits("ae", [tng, tgb])
            tex = p.op("ae", lambda e: e.activation(comb[:], g_sb[:], AF.Exp, bias=negm1[:]), inc=True)
            p.waits("ve", [tex, tmk])
            tcu = p.op("ve", lambda e: e.tensor_mul(comb[:], comb[:], gmask[:]), inc=True)
            p.wait("ve", tcu)
            tsu = p.op("ve", lambda e: e.reduce_sum(srec[:], comb[:], AX.X), inc=True)
            p.wait("ve", tsu)
            trc = p.op("ve", lambda e: e.reciprocal(srec[:], srec[:]), inc=True)
            p.wait("ve", trc)
            tcf = p.op("ve", lambda e: e.tensor_scalar(comb[:], comb[:], srec[:], None, ALU.mult), inc=True)
            b_tr.acquire(p, "pe", 0)
            p.wait("pe", tcf)
            ttr = p.op("pe", lambda e: e.transpose(PS[0:NE, 7 * TG: 7 * TG + 128], comb[:], ident[:]), inc=True)
            p.wait("ve", ttr)
            t_cfm = p.op("ve", lambda e, o=cfm[0:NE, tlo:tlo + 128]: e.tensor_copy(o, PS[0:NE, 7 * TG: 7 * TG + 128]), inc=True)
            b_tr.release(0, t_cfm)

        p.waits("gp", [t_cfm, t_cbtmm_last])
        t_crow = p.dma("gp", crow[0:1, 0:HALF], cfm[0:1, :], "s_cr")
        p.waits("pe", [tgb, t_cfm, t_crow])
        tcbt0 = []
        for ti4 in range(4):
            t_cbtmm_last = p.op("pe", lambda e, o=bank(4 + ti4), r=crow[0:1, ti4 * 512:(ti4 + 1) * 512]:
                                e.matmul(o, onesrowb[:], r, start=True, stop=True), inc=True)
            p.wait("ve", t_cbtmm_last)
            tcc = p.op("ve", lambda e, o=cbtf[ti4], b=bank(4 + ti4): e.tensor_copy(o, b), inc=True)
            tcbt0.append(tcc)

        bb2 = Buf(2)
        bb2.release(0, tbe_last)
        bb2.release(1, tbe_last)
        ta2_pb = {}
        p.wait("pe", t_cfm)
        for oc in range(NFC):
            for tgi in range(4):
                pb = (oc * 4 + tgi) % 2
                bb2.acquire(p, "pe", pb)
                tb2 = p.op("pe", lambda e, o=bank(pb), l=b2bf[:, oc * 128:(oc + 1) * 128],
                           r=cfm[0:NE, tgi * TG:(tgi + 1) * TG]:
                           e.matmul(o, l, r, start=True, stop=True), inc=True)
                p.wait("ve", tb2)
                ta2 = p.op("ve", lambda e, o=acc(oc, tgi * TG, TG), b=bank(pb):
                           e.tensor_add(o, o, b), inc=True)
                bb2.release(pb, ta2)
                ta2_pb[pb] = ta2

        if debug_stage == 4 and half == 0:
            p.waits("sy", p.now("ve"))
            for nm, tn in (("dbg_g", g_sb), ("dbg_top8", top8), ("dbg_gmod", gmod),
                           ("dbg_gmask", gmask), ("dbg_comb", comb)):
                p.dma("sy", nc.dram_tensor(nm, list(tn.shape), F32, kind="ExternalOutput").ap(),
                      tn[:], "s_dbg")
            p.dma("sy", nc.dram_tensor("dbg_acc", [128, 16384], F32, kind="ExternalOutput").ap(),
                  A1.ap(), "s_dbg")
            p.dma("sy", nc.dram_tensor("dbg_t", [128, 16384], BF16, kind="ExternalOutput").ap(),
                  A3b[:, 0:16384], "s_dbg")
            p.dma("sy", nc.dram_tensor("dbg_cfm", [NE, HALF], BF16, kind="ExternalOutput").ap(),
                  cfm[:], "s_dbg")
            _final(p)
            return _wire(nc, p)

        b_set = Buf(2)
        b_set.release(0, ta2_pb.get(0), ta2_pb.get(1), tbe_last)
        b_set.release(1, *(tcbt0 + [tgb, t_cfm]))
        if half == 0:
            t_w1l, t_w2l = dict(t_w1l0), dict(t_w2l0)
        else:
            t_w1l, t_w2l = {}, {}
            for i in range(3):
                p.wait("gp", t_w1cons[i])
                t_w1l[i] = p.dma("gp", w1buf[i], w1n[0, i], "s_w1")
            for i in range(2):
                p.wait("gp", t_w2cons[i])
                t_w2l[i] = p.dma("gp", w2buf[i], w2n[0, i], "s_w2")
        gctr = 0
        t_hw_done = {0: None, 1: None}
        for e_ in range(_NE_RUN[0]):
            for jj in range(NFC):
                idx = e_ * 8 + jj
                for pr in range(2):
                    g = gctr
                    gctr += 1
                    gpar = g % 2
                    base = gpar * 4
                    b_set.acquire(p, "pe", g)
                    if pr == 0:
                        p.wait("pe", t_w1l[idx])
                        if idx == 0:
                            p.wait("pe", t_prep)
                    tgl = None
                    for t2 in range(2):
                        for fc in range(NFC):
                            tgl = p.op("pe", lambda e, o=bank(base + t2), l=w1r[idx % 3][:, fc, 0:128],
                                       r=tsb(fc, (pr * 2 + t2) * TG, TG), st=(fc == 0), sp=(fc == NFC - 1):
                                       e.matmul(o, l, r, start=st, stop=sp), inc=(fc == NFC - 1 and t2 == 1))
                    tll = None
                    for t2 in range(2):
                        for fc in range(NFC):
                            tll = p.op("pe", lambda e, o=bank(base + 2 + t2), l=w1r[idx % 3][:, fc, 128:256],
                                       r=tsb(fc, (pr * 2 + t2) * TG, TG), st=(fc == 0), sp=(fc == NFC - 1):
                                       e.matmul(o, l, r, start=st, stop=sp), inc=(fc == NFC - 1 and t2 == 1))
                    sS, sG, q1 = sSc[gpar], sGc[gpar], q1c[gpar]
                    bsap = b1sig[:, e_ * 8 + jj: e_ * 8 + jj + 1]
                    b1ap = b1pp[:, e_ * 16 + jj: e_ * 16 + jj + 1]
                    b1lap = b1lin1[:, e_ * 8 + jj: e_ * 8 + jj + 1]
                    p.wait("ae", tgl)
                    p.wait("ae", last_p_ve[gpar])
                    tsS = p.op("ae", lambda e, o=sS, b=gbank(base), bi=bsap:
                               e.activation(o, b, AF.Sigmoid, bias=bi, scale=1.702), inc=True)
                    p.wait("ve", tsS)
                    p.op("ve", lambda e, o=sG, b=gbank(base), bi=b1ap:
                         e.tensor_scalar(o, b, bi, 7.0, ALU.add, ALU.min))
                    tp = p.op("ve", lambda e, o=sG, i=sS, g2=sG:
                              e.scalar_tensor_tensor(o, i, SIGC, g2, ALU.min, ALU.mult), inc=True)
                    last_p_ve[gpar] = tp
                    p.wait("ve", tll)
                    tq1 = p.op("ve", lambda e, o=q1, b=lbank(base), bi=b1lap:
                               e.tensor_scalar(o, b, bi, 8.0, ALU.add, ALU.min), inc=True)
                    p.op("ve", lambda e, o=q1, i=q1, g2=sG:
                         e.scalar_tensor_tensor(o, i, -6.0, g2, ALU.max, ALU.mult))
                    if jj == 0:
                        p.wait("ve", t_w2rd[pr])
                    thw = p.op("ve", lambda e, o=hsl2(jj, pr), i=q1, c=cbtf2[pr]:
                               e.tensor_mul(o, i, c), inc=True)
                    t_hw_done[pr] = thw
                    b_set.release(g, tsS, tq1)
                    if pr == 1:
                        nxt = idx + 3
                        if nxt < NE * NFC:
                            p.wait("gp", tll)
                            t_w1l[nxt] = p.dma("gp", w1buf[nxt % 3], w1n[nxt // 8, nxt % 8], "s_w1")
                        t_w1cons[idx % 3] = tll
            t_crow_next = None
            if e_ < NE - 1:
                p.wait("gp", t_cbtmm_last)
                t_crow_next = p.dma("gp", crow[0:1, 0:HALF], cfm[e_ + 1:e_ + 2, :], "s_cr")
            for oc in range(NFC):
                idx2 = e_ * 8 + oc
                for pr in range(2):
                    g = gctr
                    gctr += 1
                    base = (g % 2) * 4
                    b_set.acquire(p, "pe", g)
                    if pr == 0:
                        p.wait("pe", t_w2l[idx2])
                    if oc == 0:
                        p.wait("pe", t_hw_done[pr])
                    tww = None
                    for fc2 in range(NFC):
                        for t2 in range(2):
                            tww = p.op("pe", lambda e, o=bank(base + t2), l=w2r[idx2 % 2][:, fc2, :],
                                       r=hsl(fc2, pr * 2 + t2), st=(fc2 == 0), sp=(fc2 == NFC - 1):
                                       e.matmul(o, l, r, start=st, stop=sp), inc=(fc2 == NFC - 1 and t2 == 1))
                    rel2 = []
                    if e_ < NE - 1 and pr == 1 and oc < 4:
                        p.wait("pe", t_crow_next)
                        t_cbtmm_last = p.op("pe", lambda e, o=bank(base + 2), r=crow[0:1, oc * 512:(oc + 1) * 512]:
                                            e.matmul(o, onesrowb[:], r, start=True, stop=True), inc=True)
                        p.wait("ve", t_cbtmm_last)
                        tcc = p.op("ve", lambda e, o=cbtf[oc], b=bank(base + 2): e.tensor_copy(o, b), inc=True)
                        rel2.append(tcc)
                    p.wait("ve", tww)
                    ta = p.op("ve", lambda e, o=acc(oc, pr * 2 * TG, 2 * TG), b=gbank(base):
                              e.tensor_add(o, o, b), inc=True)
                    rel2.append(ta)
                    if oc == NFC - 1:
                        t_w2rd[pr] = tww
                    b_set.release(g, *rel2)
                    if pr == 1:
                        nxt2 = idx2 + 2
                        if nxt2 < NE * NFC:
                            p.wait("gp", tww)
                            t_w2l[nxt2] = p.dma("gp", w2buf[nxt2 % 2], w2n[nxt2 // 8, nxt2 % 8], "s_w2")
                        t_w2cons[idx2 % 2] = tww
        p.waits("sy", p.now("ve"))
        t_out_prev = p.dma("sy", out_r[:, :, hlo:hlo + HALF],
                           A1.ap().rearrange("p (c t) -> p c t", c=NFC), "s_out")
    _final(p)
    return _wire(nc, p)


def _wire(nc, p):
    with nc.Block() as block:
        @block.tensor
        def _(t):
            for f in p.st["pe"]:
                f(t)

        @block.vector
        def _(v):
            for f in p.st["ve"]:
                f(v)

        @block.scalar
        def _(s2):
            for f in p.st["ae"]:
                f(s2)

        @block.sync
        def _(s2):
            for f in p.st["sy"]:
                f(s2)

        @block.gpsimd
        def _(g):
            for f in p.st["gp"]:
                f(g)
    return nc, p


_CACHE = {}


def _w1_layout(w1, perm):
    A = w1[:, perm, :].reshape(NE, 16, 128, NFC, 128)
    out = np.empty((NE, 8, 128, NFC, 256), np.float32)
    out[:, :, :, :, 0:128] = A[:, 0:8].transpose(0, 1, 4, 3, 2)
    out[:, :, :, :, 128:256] = A[:, 8:16].transpose(0, 1, 4, 3, 2)
    return np.ascontiguousarray(out.reshape(NE, 8, 128, 2048))


def _w2_layout(w2):
    B = w2.reshape(NE, 8, 128, NFC, 128)
    return np.ascontiguousarray(B.transpose(0, 1, 4, 3, 2).reshape(NE, 8, 128, 1024))


def _host_prep(inputs):
    x = np.asarray(inputs["L0_attn_x"], np.float32)
    Wq = np.asarray(inputs["L0_attn_Wq"], np.float32)
    Wk = np.asarray(inputs["L0_attn_Wk"], np.float32)
    Wv = np.asarray(inputs["L0_attn_Wv"], np.float32)
    Wo = np.asarray(inputs["L0_attn_Wo"], np.float32)
    nscale = np.asarray(inputs["L0_ffn_norm_scale"], np.float32)
    gate_w = np.asarray(inputs["L0_ffn_gate_w"], np.float32)
    gate_b = np.asarray(inputs["L0_ffn_gate_b"], np.float32)
    w1 = np.asarray(inputs["L0_ffn_w1"], np.float32)
    b1 = np.asarray(inputs["L0_ffn_b1"], np.float32)
    w2 = np.asarray(inputs["L0_ffn_w2"], np.float32)
    b2 = np.asarray(inputs["L0_ffn_b2"], np.float32)

    perm = np.concatenate([np.arange(0, 2048, 2), np.arange(1, 2048, 2)])
    com = {
        "wqT": np.ascontiguousarray(Wq.T),
        "wkT": np.ascontiguousarray(Wk.T),
        "wvT": np.ascontiguousarray(Wv.T),
        "woT": np.ascontiguousarray(Wo.T),
        "gwT": np.ascontiguousarray(gate_w.T),
        "gbb": np.ascontiguousarray(np.broadcast_to(gate_b, (128, NE))),
        "w1n": _w1_layout(w1, perm),
        "b1pp": np.ascontiguousarray(
            b1[:, perm].reshape(NE, 16, 128).transpose(2, 0, 1).reshape(128, NE * 16)),
        "w2n": _w2_layout(w2),
        "b2": np.ascontiguousarray(b2),
        "nsc": np.ascontiguousarray(nscale.reshape(NFC, 128).T),
    }
    inv = 1.0 / (10000.0 ** (np.arange(0, HD, 2, dtype=np.float64) / HD))
    pos = np.arange(S, dtype=np.float64)
    fr = np.outer(pos, inv)
    emb = np.concatenate([fr, fr], -1)
    cos1 = np.cos(emb).astype(np.float32).T
    sin1 = np.sin(emb).astype(np.float32).T
    com["cosb"] = np.ascontiguousarray(np.vstack([cos1, cos1]))
    com["sinb"] = np.ascontiguousarray(np.vstack([sin1, sin1]))
    mk = np.zeros((128, 4 * 1024), np.float32)
    kk = np.arange(128)[:, None]
    qq = np.arange(TG)[None, :]
    for r in range(4):
        m_ = np.where(qq >= kk + 128 * r, 0.0, -1e30)
        mk[:, r * 1024: r * 1024 + 512] = m_
        mk[:, r * 1024 + 512:(r + 1) * 1024] = m_
    com["maskb"] = mk
    rt = np.zeros((128, 128), np.float32)
    for base in (0, 64):
        for d in range(32):
            rt[base + d + 32, base + d] = -1.0
            rt[base + d, base + d + 32] = 1.0
    com["rT"] = rt
    com["ident"] = np.eye(128, dtype=np.float32)
    oc3 = np.zeros((128, 3), np.float32)
    oc3[:, 0] = 1.0
    oc3[0:64, 1] = 1.0
    oc3[64:128, 2] = 1.0
    com["onescol"] = oc3
    com["onescolb"] = oc3
    com["onesrow"] = np.ones((1, 128), np.float32)
    com["onesrowb"] = np.ones((1, 128), np.float32)
    cv = np.zeros((128, 4), np.float32)
    cv[:, 1] = NEPS
    com["cv"] = cv

    in_maps = []
    for b in range(x.shape[0]):
        m = dict(com)
        m["xT"] = np.ascontiguousarray(x[b].T)
        in_maps.append(m)
    return in_maps


def kernel(**inputs):
    in_maps = _host_prep(inputs)
    if "nc" not in _CACHE:
        _CACHE["nc"], _ = build_program()
    res = run_bass_kernel_spmd(_CACHE["nc"], in_maps, core_ids=list(range(8)))
    outs = [np.ascontiguousarray(np.asarray(r["outT"]).T) for r in res.results]
    return np.stack(outs, axis=0)
